# version 04
# baseline (speedup 1.0000x reference)
import contextlib
import numpy as np
import ml_dtypes
import concourse.bass as bass
import concourse.mybir as mybir
from concourse.bass_utils import run_bass_kernel_spmd
from concourse.alu_op_type import AluOpType as OP

F32 = mybir.dt.float32
BF16 = mybir.dt.bfloat16
AF = mybir.ActivationFunctionType
AX = mybir.AxisListType.X

B, T, D = 4, 4096, 1024
H, CAP, DFF = 16, 512, 2730
EPS_H, EPS_LN, EPS_RMS = 1e-8, 1e-5, 1e-6
NT, NL, NS = 32, 8, 32
GRPS = [(3 * i, min(3 * i + 3, NS)) for i in range(11)]
ENGINES = ("sync", "gpsimd", "tensor", "vector", "scalar")


class KB:
    def __init__(self, nc, es):
        self.nc, self.es = nc, es
        self.ops = {e: [] for e in ENGINES}
        self.cnt = {}
        self.sems = {}

    def sem(self, name):
        if name not in self.sems:
            self.sems[name] = self.es.enter_context(self.nc.semaphore(name))
            self.cnt[name] = 0
        return self.sems[name]

    def op(self, eng, fn, inc=None):
        if inc is None:
            self.ops[eng].append(fn)
            return None
        s = self.sem(inc)
        self.cnt[inc] += 1
        self.ops[eng].append(lambda e, f=fn, s=s: f(e).then_inc(s, 1))
        return self.cnt[inc]

    def dma(self, eng, out_ap, in_ap, inc, **kw):
        s = self.sem(inc)
        self.cnt[inc] += 16
        self.ops[eng].append(
            lambda e, o=out_ap, i=in_ap, s=s, kw=kw: e.dma_start(out=o, in_=i, **kw).then_inc(s, 16))
        return self.cnt[inc]

    def wait(self, eng, sem, val=None):
        s = self.sem(sem)
        v = self.cnt.get(sem, 0) if val is None else val
        if v <= 0:
            return 0
        self.ops[eng].append(lambda e, s=s, v=v: e.wait_ge(s, v))
        return v

    def barrier(self):
        marks = []
        for eng in ENGINES:
            sm = "bar_" + eng
            marks.append((sm, self.op(eng, lambda e: e.nop(), inc=sm)))
        for eng in ENGINES:
            for sm, v in marks:
                self.wait(eng, sm, v)


class Arena:
    def __init__(self, ap, total):
        self.ap, self.total, self.cur = ap, total, 0

    def mark(self):
        return self.cur

    def reset(self, m):
        self.cur = m

    def t2(self, n):
        assert self.cur + n <= self.total, f"arena overflow {self.cur}+{n}>{self.total}"
        off = self.cur
        self.cur += n
        return self.ap[:, off:off + n]

    def t3(self, a, b):
        return self.t2(a * b).rearrange("p (a b) -> p a b", a=a)


def build(stage, dump=None):
    es = contextlib.ExitStack()
    nc = bass.Bass("TRN2", target_bir_lowering=False)
    kb = KB(nc, es)

    def _dump(pairs):
        for p in pairs:
            o, i = p[0], p[1]
            eng = p[2] if len(p) > 2 else "sync"
            kb.dma(eng, o, i, "stq")
        kb.wait("sync", "stq")
        kb.wait("gpsimd", "stq")
        _finalize(kb, nc)
        return nc

    def din(name, shape):
        return nc.dram_tensor(name, list(shape), F32, kind="ExternalInput").ap()

    xrow = din("xrow", (T, D))
    xtT = din("xtT", (128, 8, T))
    wqT = din("wqT", (128, 8, D))
    wkT = din("wkT", (128, 8, D))
    wvT = din("wvT", (128, 9, H * 65))
    woT = din("woT", (128, 8, D))
    wbT = din("wbT", (128, 8, D))
    wqkT = din("wqkT", (128, 8, 2 * D))
    wvfT = din("wvfT", (128, 9, H * 65))
    woutT = din("woutT", (128, 8, D))
    w1T = din("w1T", (128, 8, DFF))
    w2T = din("w2T", (128, 8, DFF))
    w3T = din("w3T", (128, 22, D))
    bqt = din("bqt", (128, 8))
    bkt = din("bkt", (128, 8))
    brows = din("brows", (1, 2 * D))
    wrb = din("wrb", (128, D))
    g1bi = din("g1bi", (128, D))
    g2bi = din("g2bi", (128, D))
    rwrep = din("rwrep", (128, 1))
    curv = din("curv", (1, 1))
    lt128 = din("lt128", (128, 128))
    iota512b = din("iota512b", (128, 512))
    iotarb = din("iotarb", (128, 4))
    maskbd = din("maskbd", (128, 4, 512))
    onesrow = din("onesrow", (1, 128))
    onescol = din("onescol", (128, 1))
    eyef = din("eyef", (128, 128))

    out = nc.dram_tensor("out", [T, D], F32, kind="ExternalOutput").ap()

    x1d = nc.dram_tensor("x1d", [T, D], F32).ap()
    vpdb = nc.dram_tensor("vpdb", [T, H * 65], BF16).ap()
    otdb = nc.dram_tensor("otdb", [8, 128, T], BF16).ap()
    stkd = nc.dram_tensor("stkd", [128, 8, 512], F32).ap()
    rowd = nc.dram_tensor("rowd", [128, 32], F32).ap()
    rowd2 = nc.dram_tensor("rowd2", [128, 32], F32).ap()

    def sb(name, shape, dt):
        return es.enter_context(nc.sbuf_tensor(name, list(shape), dt))[:]

    onesb = sb("onesb", (1, 128), F32)
    onesbf = sb("onesbf", (1, 128), BF16)
    onesc = sb("onesc", (128, 1), F32)
    eyeb = sb("eyeb", (128, 128), F32)
    lt = sb("lt", (128, 128), F32)
    io512 = sb("io512", (128, 512), F32)
    iorb = sb("iorb", (128, 4), F32)
    bqs = sb("bqs", (128, 8), F32)
    bks = sb("bks", (128, 8), F32)
    radl = sb("radl", (128, 1), F32)
    nalpha = sb("nalpha", (128, 1), F32)
    small = sb("small", (1, 16), F32)
    stat = sb("stat", (128, 16), F32)
    epsln = sb("epsln", (128, 1), F32)
    dinv = sb("dinv", (1, 512), F32)
    dbc = sb("dbc", (64, 512), F32)
    vstk = sb("vstk", (128, 512), F32)
    sc32 = sb("sc32", (128, 32), F32)
    rk32 = sb("rk32", (128, 32), F32)
    rowst = sb("rowst", (1, 512), F32)

    A16 = Arena(sb("A16", (128, 57344), BF16), 57344)
    A32 = Arena(sb("A32", (128, 20992), F32), 20992)

    PSA = es.enter_context(nc.psum_tensor("PSA", [128, 1536], F32))[:]
    PSB = es.enter_context(nc.psum_tensor("PSB", [128, 1536], F32))[:]
    PAV = es.enter_context(nc.psum_tensor("PAV", [128, 512], F32))[:]
    PMI = es.enter_context(nc.psum_tensor("PMI", [128, 512], F32))[:]
    PROJ_PS = [PSA[:, 0:512], PSB[:, 0:512], PSA[:, 512:1024], PSB[:, 512:1024]]
    NRM_PS = [PMI[0:1, 0:512], PAV[0:1, 0:512], PSA[0:1, 1024:1536], PSB[0:1, 1024:1536]]

    for o, i in [(onesb, onesrow), (onesc, onescol), (eyeb, eyef), (lt, lt128),
                 (io512, iota512b), (iorb, iotarb),
                 (bqs, bqt), (bks, bkt), (radl, rwrep), (small[0:1, 0:1], curv)]:
        kb.dma("sync", o, i, "dq")
    kb.dma("gpsimd", onesbf, onesrow, "gq")

    XT = A16.t3(8, T)
    markA16 = A16.mark()
    markA32 = A32.mark()
    kb.dma("gpsimd", XT[:, :, :], xtT, "gq")
    kb.op("gpsimd", lambda e: e.memset(epsln, EPS_LN), inc="gp")

    kb.wait("vector", "dq")
    kb.wait("vector", "gp")
    kb.op("vector", lambda e: e.tensor_scalar(out=small[0:1, 1:2], in0=small[0:1, 0:1],
                                              scalar1=-1.0, scalar2=None, op0=OP.mult), inc="vch")
    kb.wait("vector", "vch")
    n = kb.op("vector", lambda e: e.tensor_scalar(out=small[0:1, 2:3], in0=small[0:1, 0:1],
                                                  scalar1=small[0:1, 1:2], scalar2=EPS_H,
                                                  op0=OP.max, op1=OP.max), inc="ve")
    kb.wait("scalar", "ve", n)
    kb.op("scalar", lambda e: e.activation(out=small[0:1, 4:5], in_=small[0:1, 2:3],
                                           func=AF.Ln), inc="sch")
    kb.wait("scalar", "sch")
    m = kb.op("scalar", lambda e: e.activation(out=small[0:1, 5:6], in_=small[0:1, 4:5],
                                               func=AF.Exp, scale=0.5), inc="se")
    kb.wait("vector", "se", m)
    kb.op("vector", lambda e: e.tensor_scalar(out=small[0:1, 6:7], in0=small[0:1, 5:6],
                                              scalar1=EPS_H, scalar2=8.0, op0=OP.add,
                                              op1=OP.mult), inc="vch")
    kb.wait("vector", "vch")
    kb.op("vector", lambda e: e.reciprocal(out=small[0:1, 7:8], in_=small[0:1, 6:7]), inc="vch")
    kb.wait("vector", "vch")
    n2 = kb.op("vector", lambda e: e.tensor_scalar(out=small[0:1, 8:9], in0=small[0:1, 7:8],
                                                   scalar1=-1.0, scalar2=None, op0=OP.mult), inc="ve")
    kb.wait("tensor", "ve", n2)
    k1 = kb.op("tensor", lambda e: e.matmul(out=PMI[:, 0:1], lhsT=onesb, rhs=small[0:1, 8:9],
                                            start=True, stop=True), inc="pe")
    kb.wait("vector", "pe", k1)
    kb.op("vector", lambda e: e.tensor_copy(out=nalpha, in_=PMI[:, 0:1]), inc="ve")
    ms = kb.op("scalar", lambda e: e.activation(out=radl, in_=radl, func=AF.Exp, scale=-1.0), inc="se")
    kb.wait("vector", "se", ms)
    kb.op("vector", lambda e: e.tensor_scalar(out=radl, in0=radl, scalar1=1.0,
                                              scalar2=None, op0=OP.add), inc="vch")
    kb.wait("vector", "vch")
    kb.op("vector", lambda e: e.reciprocal(out=radl, in_=radl), inc="ve")
    kb.barrier()
    if dump == "smalls":
        return _dump([(out[0:1, 0:16], small), (out[1:2, 0:1], nalpha[0:1, :]),
                      (out[2:3, 0:1], radl[0:1, :])])

    def f_chain(dst, src, sa, sbuf, mul_radial=False):
        kb.wait("scalar", "ve")
        kb.op("scalar", lambda e: e.activation(out=sa, in_=src, func=AF.Ln))
        kb.op("scalar", lambda e: e.activation(out=sa, in_=sa, func=AF.Exp, scale=0.5))
        q = kb.op("scalar", lambda e: e.activation(out=sbuf, in_=sa, func=AF.Exp, scale=2.0), inc="se")
        kb.wait("vector", "se", q)
        kb.op("vector", lambda e: e.tensor_scalar(out=sbuf, in0=sbuf, scalar1=1.0,
                                                  scalar2=None, op0=OP.add))
        kb.op("vector", lambda e: e.reciprocal(out=sbuf, in_=sbuf), inc="vch")
        kb.wait("vector", "vch")
        kb.op("vector", lambda e: e.tensor_scalar(out=sbuf, in0=sbuf, scalar1=-2.0,
                                                  scalar2=1.0, op0=OP.mult, op1=OP.add))
        kb.op("vector", lambda e: e.tensor_scalar(out=sa, in0=sa, scalar1=EPS_H,
                                                  scalar2=None, op0=OP.add))
        kb.op("vector", lambda e: e.reciprocal(out=sa, in_=sa), inc="vch")
        kb.wait("vector", "vch")
        if mul_radial:
            kb.op("vector", lambda e: e.tensor_scalar(out=sa, in0=sa, scalar1=radl,
                                                      scalar2=None, op0=OP.mult))
        return kb.op("vector", lambda e: e.tensor_tensor(out=dst, in0=sa, in1=sbuf,
                                                         op=OP.mult), inc="ve")

    mk16 = A16.mark(); mk32 = A32.mark()
    WVs = A16.t3(9, 260)
    VEVB = A16.t2(260)
    VEV = A32.t2(260)
    SQ5 = A32.t2(260)
    for ch in range(4):
        c0 = 260 * ch
        kb.wait("gpsimd", "pe")
        for kt in range(9):
            kb.dma("gpsimd", WVs[0:128, kt, :], wvT[:, kt, c0:c0 + 260], "gq")
        kb.wait("tensor", "gq")
        for tb in range(NT):
            pscur = PSA if tb % 2 == 0 else PSB
            for kt in range(8):
                kb.op("tensor", lambda e, kt=kt, tb=tb, p=pscur: e.matmul(
                    out=p[0:128, 0:260], lhsT=XT[0:128, kt, 128 * tb:128 * tb + 128],
                    rhs=WVs[0:128, kt, :], start=(kt == 0), stop=False))
            kb.op("tensor", lambda e, p=pscur: e.matmul(
                out=p[0:128, 0:260], lhsT=onesbf, rhs=WVs[0:1, 8, :],
                start=False, stop=True), inc="pe")
            kb.wait("vector", "pe")
            kb.op("vector", lambda e, p=pscur: e.tensor_copy(out=VEV, in_=p[0:128, 0:260]), inc="ve")
            kb.wait("tensor", "ve")
            kb.op("vector", lambda e: e.tensor_tensor(out=SQ5, in0=VEV, in1=VEV, op=OP.mult))
            for hl in range(4):
                h = 4 * ch + hl
                kb.op("vector", lambda e, hl=hl, tb=tb, h=h: e.tensor_reduce(
                    out=vstk[0:128, 16 * tb + h:16 * tb + h + 1],
                    in_=SQ5[:, 65 * hl:65 * hl + 64], axis=AX, op=OP.add))
            kb.op("vector", lambda e: e.tensor_copy(out=VEVB, in_=VEV), inc="ve")
            kb.wait("sync", "ve")
            kb.dma("sync", vpdb[128 * tb:128 * tb + 128, c0:c0 + 260], VEVB, "sq")
            kb.wait("vector", "sq")
    A16.reset(mk16)
    FCA = A32.t2(512)
    FCB = A32.t2(512)
    kb.wait("scalar", "ve")
    f_chain(vstk, vstk, FCA, FCB)
    A32.reset(mk32)
    kb.barrier()
    if dump == "vstk":
        return _dump([(out[0:128, 0:512], vstk)])
    if dump == "vpdb":
        return _dump([(out[:, :], vpdb[:, 0:1024], "gpsimd")])

    mk32 = A32.mark(); mk16 = A16.mark()
    KQP = A32.t2(T)
    KQPc = KQP.rearrange("p (c n) -> p c n", n=512)
    SQB = A32.t2(T)
    SQBc = SQB.rearrange("p (c n) -> p c n", n=512)
    nstk = A32.t3(2, 512)
    NST = A32.t3(2, 512)
    WQs = A16.t3(8, 64)
    WKs = A16.t3(8, 64)

    def proj_head(h, dst, WQb, WKb):
        kb.wait("gpsimd", "pe")
        for kt in range(8):
            kb.dma("gpsimd", WQb[0:128, kt, :], wqT[:, kt, 64 * h:64 * h + 64], "gq")
            kb.dma("gpsimd", WKb[0:128, kt, :], wkT[:, kt, 64 * h:64 * h + 64], "gq")
        kb.wait("tensor", "gq")
        for c in range(8):
            for qk in range(2):
                W = WKb if qk == 0 else WQb
                pbank = PROJ_PS[(2 * c + qk) % 4]
                for kt in range(8):
                    kb.op("tensor", lambda e, W=W, kt=kt, c=c, p=pbank: e.matmul(
                        out=p[0:64, 0:512], lhsT=W[0:128, kt, :],
                        rhs=XT[0:128, kt, 512 * c:512 * c + 512],
                        start=(kt == 0), stop=(kt == 7)), inc="pe" if kt == 7 else None)
                kb.wait("vector", "pe")
                bias = (bks if qk == 0 else bqs)[64 * (h % 2):64 * (h % 2) + 64, h // 2:h // 2 + 1]
                r0 = 0 if qk == 0 else 64
                kb.op("vector", lambda e, p=pbank, bias=bias, r0=r0, c=c, dst=dst: e.tensor_scalar(
                    out=dst[r0:r0 + 64, 512 * c:512 * c + 512], in0=p[0:64, 0:512],
                    scalar1=bias, scalar2=None, op0=OP.add), inc="ve")
        kb.wait("tensor", "ve")

    for h in range(H):
        proj_head(h, KQP, WQs, WKs)
        kb.op("vector", lambda e: e.tensor_tensor(out=SQB, in0=KQP, in1=KQP, op=OP.mult), inc="ve")
        kb.wait("tensor", "ve")
        for c in range(8):
            pk = NRM_PS[c % 4]
            pq = NRM_PS[(c % 4) ^ 1]
            kb.op("tensor", lambda e, c=c, pk=pk: e.matmul(out=pk, lhsT=onesc[0:64, :],
                                                           rhs=SQBc[0:64, c, :],
                                                           start=True, stop=True), inc="pe")
            kb.op("tensor", lambda e, c=c, pq=pq: e.matmul(out=pq, lhsT=onesc[64:128, :],
                                                           rhs=SQBc[64:128, c, :],
                                                           start=True, stop=True), inc="pe")
            kb.wait("vector", "pe")
            p = h + 16 * c
            kb.op("vector", lambda e, pk=pk: e.tensor_copy(out=NST[0:1, 0, :], in_=pk))
            kb.op("vector", lambda e, pq=pq: e.tensor_copy(out=NST[0:1, 1, :], in_=pq), inc="ve")
            kb.wait("sync", "ve")
            kb.dma("sync", nstk[p:p + 1, 1, 0:512], NST[0:1, 0, :], "sq")
            kb.dma("sync", nstk[p:p + 1, 0, 0:512], NST[0:1, 1, :], "sq")
            kb.wait("tensor", "ve")
            kb.wait("vector", "sq")

    if dump == "nstk":
        return _dump([(out[0:128, 0:512], nstk[:, 0, :]),
                      (out[128:256, 0:512], nstk[:, 1, :])])

    FK = A32.t2(512)
    KN = A32.t2(512)
    FCA = A32.t2(512)
    FCB = A32.t2(512)
    csk = A32.t2(512)
    csq = A32.t2(512)
    rxk = A32.t3(3, 512)
    rxq = A32.t3(3, 512)
    kb.wait("scalar", "ve")
    kb.wait("scalar", "sq")
    for side in range(2):
        f_chain(FK, nstk[:, 1 - side, :], FCA, FCB, mul_radial=(side == 1))
        rx = rxk if side == 0 else rxq
        cs = csk if side == 0 else csq
        kb.op("vector", lambda e: e.tensor_tensor(out=KN, in0=FK, in1=FK, op=OP.mult))
        kb.op("vector", lambda e, side=side: e.tensor_tensor(out=KN, in0=KN,
                                                             in1=nstk[:, 1 - side, :], op=OP.mult))
        kb.op("vector", lambda e, rx=rx: e.tensor_scalar(out=rx[:, 2, :], in0=KN, scalar1=-1.0,
                                                         scalar2=1.0, op0=OP.mult, op1=OP.add))
        kb.op("vector", lambda e, rx=rx: e.tensor_scalar(out=rx[:, 2, :], in0=rx[:, 2, :],
                                                         scalar1=1e-20, scalar2=None, op0=OP.max))
        kb.op("vector", lambda e, rx=rx: e.reciprocal(out=rx[:, 1, :], in_=rx[:, 2, :]), inc="vch")
        kb.wait("vector", "vch")
        kb.op("vector", lambda e, rx=rx: e.memset(rx[:, 2, :], 1.0))
        kb.op("vector", lambda e, rx=rx, cs=cs: e.tensor_tensor(out=cs, in0=FK,
                                                                in1=rx[:, 1, :], op=OP.mult))
        if side == 0:
            kb.op("vector", lambda e, cs=cs: e.tensor_scalar(out=cs, in0=cs, scalar1=-4.0,
                                                             scalar2=None, op0=OP.mult))
        kb.op("vector", lambda e, rx=rx: e.tensor_tensor(out=rx[:, 0, :], in0=KN,
                                                         in1=rx[:, 1, :], op=OP.mult))
        kb.op("vector", lambda e, rx=rx: e.tensor_scalar(out=rx[:, 0, :], in0=rx[:, 0, :],
                                                         scalar1=2.0, scalar2=None, op0=OP.mult), inc="ve")
    kb.wait("sync", "ve")
    kb.dma("sync", stkd[:, 0, :], csk, "sq")
    kb.dma("sync", stkd[:, 1, :], csq, "sq")
    for r in range(3):
        kb.dma("sync", stkd[:, 2 + r, :], rxk[:, r, :], "sq")
        kb.dma("sync", stkd[:, 5 + r, :], rxq[:, r, :], "sq")
    kb.barrier()
    A32.reset(mk32)
    A16.reset(mk16)
    if dump == "stkd":
        return _dump([(out[128 * j:128 * j + 128, 0:512], stkd[:, j, :]) for j in range(8)])

    KQb = [A16.t2(T) for _ in range(2)]
    QQb = [A16.t2(T) for _ in range(2)]
    VPh = A16.t3(NT, 65)
    ATb = [A16.t2(1536) for _ in range(2)]
    OTS = A16.t2(512)
    WQs2 = A16.t3(8, 64)
    WKs2 = A16.t3(8, 64)
    KQP2 = A32.t2(T)
    LxB = [A32.t2(1536) for _ in range(2)]

    def stk_row(slot, h, c):
        return stkd[:, slot, :].rearrange("(j h) c -> h j c", h=16)[h:h + 1, c, 0:512]

    attn_end = {}

    def prep_head(h):
        hb = h % 2
        KQ_, QQ_ = KQb[hb], QQb[hb]
        if h - 2 in attn_end:
            kb.wait("vector", "pesc", attn_end[h - 2][0])
            kb.wait("gpsimd", "pesc", attn_end[h - 2][0])
        kb.wait("tensor", "seln")
        proj_head(h, KQP2, WQs2, WKs2)
        for side in range(2):
            src_ = KQP2[0:64, :] if side == 0 else KQP2[64:128, :]
            dst_ = KQ_ if side == 0 else QQ_
            for c in range(8):
                vk = kb.dma("sync", rowst, stk_row(side, h, c), "sq")
                kb.wait("tensor", "sq", vk)
                kb.op("tensor", lambda e: e.matmul(out=PMI[0:64, 0:512], lhsT=onesb[0:1, 0:64],
                                                   rhs=rowst, start=True, stop=True), inc="pe")
                kb.wait("vector", "pe")
                kb.op("vector", lambda e, c=c, dst_=dst_, src_=src_: e.tensor_tensor(
                    out=dst_[0:64, 512 * c:512 * c + 512],
                    in0=src_[:, 512 * c:512 * c + 512], in1=PMI[0:64, 0:512],
                    op=OP.mult), inc="ve")
                kb.wait("tensor", "ve")
                kb.wait("sync", "ve")
        kv = None
        for r in range(3):
            kb.dma("gpsimd",
                   KQ_[64 + r:65 + r, :].rearrange("p (j c) -> p j c", c=512),
                   stkd[:, 2 + r, :].rearrange("(j h) c -> h j c", h=16)[h:h + 1, :, :], "gq")
            kv = kb.dma("gpsimd",
                        QQ_[64 + r:65 + r, :].rearrange("p (j c) -> p j c", c=512),
                        stkd[:, 5 + r, :].rearrange("(j h) c -> h j c", h=16)[h:h + 1, :, :], "gq")
        return kv

    def load_v(h):
        kb.wait("gpsimd", "peav")
        gv = kb.dma("gpsimd", VPh[:, :, :],
                    vpdb[:, 65 * h:65 * h + 65].rearrange("(tb p) c -> p tb c", p=128), "gq")
        return gv

    def attn_head(h, prep_mark, v_mark):
        hb = h % 2
        KQ_, QQ_ = KQb[hb], QQb[hb]
        kb.wait("tensor", "gq", prep_mark)
        kb.wait("vector", "gq", v_mark)
        if h - 1 in attn_end:
            kb.wait("vector", "peav", attn_end[h - 1][1])
        for tb in range(NT):
            kb.op("vector", lambda e, tb=tb, h=h: e.tensor_scalar(
                out=VPh[:, tb, 0:64], in0=VPh[:, tb, 0:64],
                scalar1=vstk[0:128, 16 * tb + h:16 * tb + h + 1], scalar2=None,
                op0=OP.mult), inc="vfv")
        kb.wait("tensor", "vfv")
        for lc in range(NL):
            for gi, (s0, s1) in enumerate(GRPS):
                psg = PSA if gi % 2 == 0 else PSB
                nst = s1 - s0
                for i in range(s0, s1):
                    j = i - s0
                    kb.op("tensor", lambda e, i=i, j=j, p=psg, KQ_=KQ_, QQ_=QQ_, lc=lc: e.matmul(
                        out=p[0:128, 512 * j:512 * j + 512],
                        lhsT=KQ_[0:64, 128 * i:128 * i + 128],
                        rhs=QQ_[0:64, 512 * lc:512 * lc + 512], start=True, stop=False))
                    kb.op("tensor", lambda e, i=i, j=j, p=psg, KQ_=KQ_, QQ_=QQ_, lc=lc: e.matmul(
                        out=p[0:128, 512 * j:512 * j + 512],
                        lhsT=KQ_[64:67, 128 * i:128 * i + 128],
                        rhs=QQ_[64:67, 512 * lc:512 * lc + 512], start=False, stop=True),
                        inc="pesc" if i == s1 - 1 else None)
                kb.wait("scalar", "pesc")
                kb.op("scalar", lambda e, p=psg, nst=nst, gi=gi: e.activation(
                    out=LxB[gi % 2][:, 0:512 * nst], in_=p[0:128, 0:512 * nst],
                    func=AF.Ln), inc="seln")
                kb.wait("scalar", "peav")
                kb.op("scalar", lambda e, nst=nst, gi=gi: e.activation(
                    out=ATb[gi % 2][:, 0:512 * nst], in_=LxB[gi % 2][:, 0:512 * nst],
                    func=AF.Exp, scale=nalpha), inc="seexp")
                kb.wait("tensor", "seexp")
                kb.wait("tensor", "seln")
                for i in range(s0, s1):
                    j = i - s0
                    kb.op("tensor", lambda e, i=i, j=j, gi=gi: e.matmul(
                        out=PAV[0:65, 0:512], lhsT=VPh[:, i, 0:65],
                        rhs=ATb[gi % 2][:, 512 * j:512 * j + 512],
                        start=(i == 0), stop=(i == NS - 1)),
                        inc="peav" if i == s1 - 1 else None)
            kb.wait("vector", "peav")
            kb.op("vector", lambda e: e.reciprocal(out=dinv, in_=PAV[64:65, 0:512]), inc="ve")
            kb.wait("tensor", "ve")
            kb.op("tensor", lambda e: e.matmul(out=PMI[0:64, 0:512], lhsT=onesb[0:1, 0:64],
                                               rhs=dinv, start=True, stop=True), inc="pe")
            kb.wait("vector", "pe")
            kb.op("vector", lambda e: e.tensor_copy(out=dbc, in_=PMI[0:64, 0:512]), inc="ve")
            kb.op("vector", lambda e: e.tensor_tensor(out=OTS[0:64, :], in0=PAV[0:64, 0:512],
                                                      in1=dbc, op=OP.mult), inc="veot")
            kb.wait("tensor", "veot")
            kb.wait("sync", "veot")
            kb.dma("sync", otdb[h // 2, 64 * (h % 2):64 * (h % 2) + 64, 512 * lc:512 * lc + 512],
                   OTS[0:64, :], "sq")
            kb.wait("vector", "sq")
        attn_end[h] = (kb.cnt["pesc"], kb.cnt["peav"], kb.cnt["seexp"])

    pm = prep_head(0)
    vm = load_v(0)
    for h in range(H):
        if h + 1 < H:
            nxt = prep_head(h + 1)
        else:
            nxt = None
        attn_head(h, pm, vm)
        if h + 1 < H:
            vm = load_v(h + 1)
        pm = nxt
    kb.barrier()
    A16.reset(markA16)
    A32.reset(markA32)
    if dump == "otdb":
        return _dump([(out[128 * j:128 * j + 128, :], otdb[j, :, 0:1024], "gpsimd")
                      for j in range(8)])

    WOc = A16.t3(8, D)
    WBc = A16.t3(8, D)
    OTL = [A16.t3(8, 128) for _ in range(2)]
    XST = [A32.t2(D) for _ in range(2)]
    YB = A32.t2(D)
    X1B = A32.t2(D)
    SQy = A32.t2(D)
    browS = A32.ap[0:1, A32.cur:A32.cur + 2 * D]; A32.cur += 2 * D
    wrbsS = A32.t2(D)
    kb.dma("gpsimd", WOc[:, :, :], woT, "gq")
    kb.dma("gpsimd", WBc[:, :, :], wbT, "gq")
    kb.dma("sync", browS, brows, "sq")
    kb.dma("sync", wrbsS, wrb, "sq")
    kb.wait("tensor", "gq")
    kb.wait("tensor", "sq")

    for lb in range(NT):
        bb_ = lb % 2
        kb.wait("sync", "pe")
        kb.wait("sync", "ve")
        for j in range(8):
            kb.dma("sync", OTL[bb_][:, j, :], otdb[j, :, 128 * lb:128 * lb + 128], "sq")
        kb.dma("sync", XST[bb_], xrow[128 * lb:128 * lb + 128, :], "sq")
        kb.wait("tensor", "sq")
        for jc in range(2):
            for j in range(8):
                kb.op("tensor", lambda e, j=j, jc=jc, bb_=bb_: e.matmul(
                    out=PSA[0:128, 512 * jc:512 * jc + 512], lhsT=OTL[bb_][:, j, :],
                    rhs=WOc[:, j, 512 * jc:512 * jc + 512], start=(j == 0), stop=False))
            for kt in range(8):
                kb.op("tensor", lambda e, kt=kt, jc=jc, lb=lb: e.matmul(
                    out=PSA[0:128, 512 * jc:512 * jc + 512],
                    lhsT=XT[0:128, kt, 128 * lb:128 * lb + 128],
                    rhs=WBc[0:128, kt, 512 * jc:512 * jc + 512], start=False, stop=False))
            kb.op("tensor", lambda e, jc=jc: e.matmul(
                out=PSA[0:128, 512 * jc:512 * jc + 512], lhsT=onesb,
                rhs=browS[0:1, 512 * jc:512 * jc + 512], start=False, stop=False))
            kb.op("tensor", lambda e, jc=jc: e.matmul(
                out=PSA[0:128, 512 * jc:512 * jc + 512], lhsT=onesb,
                rhs=browS[0:1, D + 512 * jc:D + 512 * jc + 512], start=False, stop=True),
                inc="pe")
        kb.wait("vector", "pe")
        kb.op("vector", lambda e, bb_=bb_: e.scalar_tensor_tensor(
            out=YB, in0=XST[bb_], scalar=1.0, in1=PSA[0:128, 0:1024],
            op0=OP.mult, op1=OP.add), inc="ve")
        kb.wait("tensor", "ve")
        kb.op("vector", lambda e: e.tensor_reduce(out=stat[:, 0:1], in_=YB, axis=AX, op=OP.add))
        kb.op("vector", lambda e: e.tensor_tensor(out=SQy, in0=YB, in1=YB, op=OP.mult))
        kb.op("vector", lambda e: e.tensor_reduce(out=stat[:, 1:2], in_=SQy, axis=AX, op=OP.add))
        kb.op("vector", lambda e: e.tensor_scalar(out=stat[:, 2:3], in0=stat[:, 0:1],
                                                  scalar1=1.0 / D, scalar2=None, op0=OP.mult))
        kb.op("vector", lambda e: e.tensor_scalar(out=stat[:, 4:5], in0=stat[:, 1:2],
                                                  scalar1=1.0 / D, scalar2=None,
                                                  op0=OP.mult), inc="vch")
        kb.wait("vector", "vch")
        kb.op("vector", lambda e: e.tensor_tensor(out=stat[:, 3:4], in0=stat[:, 2:3],
                                                  in1=stat[:, 2:3], op=OP.mult), inc="vch")
        kb.wait("vector", "vch")
        v1 = kb.op("vector", lambda e: e.tensor_tensor(out=stat[:, 5:6], in0=stat[:, 4:5],
                                                       in1=stat[:, 3:4], op=OP.subtract), inc="ve")
        kb.wait("scalar", "ve", v1)
        kb.op("scalar", lambda e: e.activation(out=stat[:, 6:7], in_=stat[:, 5:6],
                                               func=AF.Ln, bias=epsln), inc="sch")
        kb.wait("scalar", "sch")
        s1_ = kb.op("scalar", lambda e: e.activation(out=stat[:, 7:8], in_=stat[:, 6:7],
                                                     func=AF.Exp, scale=-0.5), inc="se")
        kb.wait("vector", "se", s1_)
        kb.op("vector", lambda e: e.tensor_scalar(out=stat[:, 8:9], in0=stat[:, 2:3],
                                                  scalar1=-1.0, scalar2=None,
                                                  op0=OP.mult), inc="vch")
        kb.wait("vector", "vch")
        kb.op("vector", lambda e: e.tensor_scalar(out=X1B, in0=YB, scalar1=stat[:, 8:9],
                                                  scalar2=stat[:, 7:8], op0=OP.add, op1=OP.mult))
        kb.op("vector", lambda e: e.tensor_tensor(out=SQy, in0=X1B, in1=wrbsS, op=OP.mult))
        kb.op("vector", lambda e, lb=lb: e.tensor_reduce(out=sc32[:, lb:lb + 1], in_=SQy,
                                                         axis=AX, op=OP.add), inc="ve")
        kb.wait("sync", "ve")
        kb.dma("sync", x1d[128 * lb:128 * lb + 128, :], X1B, "sq")
        if stage == 1:
            kb.dma("sync", out[128 * lb:128 * lb + 128, :], X1B, "sq")
        kb.wait("vector", "sq")
    if stage == 1:
        _finalize(kb, nc)
        return nc
    kb.barrier()
    A16.reset(0)
    A32.reset(markA32)

    sbc_mark = A32.mark()
    SBC = A32.t2(T)
    ED = A32.t2(128)
    EM = A32.t2(128)

    def bcast_row(src32, dst, rd):
        kb.wait("sync", "ve")
        kb.dma("sync", rd[:, :], src32, "sq")
        kb.wait("sync", "sq")
        for c in range(8):
            kb.dma("sync", rowst.rearrange("r (a p) -> r a p", p=128),
                   rd.rearrange("p c -> () c p")[0:1, 4 * c:4 * c + 4, :], "bcr",
                   allow_slow_non_contiguous=True)
            kb.dma("sync", small[0:1, 15:16], curv, "bfl")
            kb.wait("tensor", "bfl")
            kb.op("tensor", lambda e: e.matmul(out=PMI[0:128, 0:512], lhsT=onesb,
                                               rhs=rowst, start=True, stop=True), inc="pe")
            kb.wait("vector", "pe")
            kb.op("vector", lambda e, c=c, dst=dst: e.tensor_copy(
                out=dst[:, 512 * c:512 * c + 512], in_=PMI[0:128, 0:512]), inc="ve")
            kb.wait("tensor", "ve")
            kb.wait("sync", "ve")

    bcast_row(sc32, SBC, rowd)
    if dump == "sbc":
        return _dump([(out[0:128, 0:32], sc32)] +
                     [(out[128 * (1 + k):128 * (2 + k), :], SBC[:, 1024 * k:1024 * k + 1024])
                      for k in range(4)])
    RTMP = A32.t2(T)
    for c in range(NT):
        kb.op("vector", lambda e, c=c: e.tensor_scalar(
            out=ED, in0=SBC[:, 128 * c:128 * c + 128], scalar1=sc32[:, c:c + 1],
            scalar2=None, op0=OP.is_equal))
        kb.op("vector", lambda e: e.tensor_tensor(out=EM, in0=ED, in1=lt, op=OP.mult))
        kb.op("vector", lambda e: e.tensor_reduce(out=stat[:, 2:3], in_=EM, axis=AX, op=OP.add))
        kb.op("vector", lambda e, c=c: e.tensor_scalar(
            out=RTMP, in0=SBC, scalar1=sc32[:, c:c + 1], scalar2=0.0,
            op0=OP.is_gt, op1=OP.add, accum_out=stat[:, 0:1]),
            inc="vch" if c == 0 else None)
        if c > 0:
            kb.op("vector", lambda e, c=c: e.tensor_scalar(
                out=RTMP[:, 0:128 * c], in0=SBC[:, 0:128 * c], scalar1=sc32[:, c:c + 1],
                scalar2=0.0, op0=OP.is_equal, op1=OP.add, accum_out=stat[:, 1:2]), inc="vch")
        kb.wait("vector", "vch")
        if c > 0:
            kb.op("vector", lambda e: e.tensor_tensor(out=stat[:, 0:1], in0=stat[:, 0:1],
                                                      in1=stat[:, 1:2], op=OP.add), inc="vch")
            kb.wait("vector", "vch")
        kb.op("vector", lambda e: e.tensor_tensor(out=stat[:, 0:1], in0=stat[:, 0:1],
                                                  in1=stat[:, 2:3], op=OP.add), inc="vch")
        kb.wait("vector", "vch")
        kb.op("vector", lambda e, c=c: e.tensor_copy(out=rk32[:, c:c + 1], in_=stat[:, 0:1]), inc="ve")
    A32.reset(sbc_mark)
    RBC = A32.t2(T)
    bcast_row(rk32, RBC, rowd2)
    if dump == "ranks":
        return _dump([(out[0:128, 0:32], rk32)] +
                     [(out[128 * (1 + k):128 * (2 + k), :], RBC[:, 1024 * k:1024 * k + 1024])
                      for k in range(4)])

    PTt = [A16.t2(CAP) for _ in range(2)]
    X1bf = [A16.t2(D) for _ in range(2)]
    X1fs = [A32.t2(D) for _ in range(2)]
    XSEL = A32.t3(4, D)
    GPS = [PSA[:, 0:512], PSA[:, 512:1024], PSA[:, 1024:1536], PSB[:, 0:512],
           PSB[:, 512:1024], PSB[:, 1024:1536], PAV[:, 0:512], PMI[:, 0:512]]
    gthm = {}
    for c in range(NT):
        cb = c % 2
        if c >= 2:
            kb.wait("vector", "pegth", gthm[c - 2])
        kb.wait("sync", "ve")
        kb.dma("sync", X1fs[cb], x1d[128 * c:128 * c + 128, :], "sq")
        kb.wait("vector", "sq")
        kb.op("vector", lambda e, cb=cb: e.tensor_copy(out=X1bf[cb], in_=X1fs[cb]))
        kb.op("vector", lambda e, c=c, cb=cb: e.tensor_scalar(
            out=PTt[cb], in0=io512, scalar1=rk32[:, c:c + 1], scalar2=None,
            op0=OP.is_equal), inc="ve")
        kb.wait("tensor", "ve")
        for rb in range(4):
            for ec in range(2):
                last = (rb == 3 and ec == 1)
                kb.op("tensor", lambda e, c=c, cb=cb, rb=rb, ec=ec: e.matmul(
                    out=GPS[2 * rb + ec][0:128, :], lhsT=PTt[cb][:, 128 * rb:128 * rb + 128],
                    rhs=X1bf[cb][:, 512 * ec:512 * ec + 512],
                    start=(c == 0), stop=(c == NT - 1)), inc="pegth" if last else None)
        gthm[c] = kb.cnt["pegth"]
    kb.wait("vector", "pegth", gthm[NT - 1])
    for rb in range(4):
        for ec in range(2):
            kb.op("vector", lambda e, rb=rb, ec=ec: e.tensor_copy(
                out=XSEL[:, rb, 512 * ec:512 * ec + 512], in_=GPS[2 * rb + ec][0:128, :]), inc="ve")
    kb.wait("tensor", "ve")

    H1F = A32.t3(4, D)
    XATT = A32.t3(4, D)
    SQyB = A32.t2(D)
    MSKS = A32.t2(512)
    H1T = A16.t3(9, CAP)
    QKT = A16.t3(16, CAP)
    VIB = A16.t3(4, H * 65)
    IAT = A16.t3(4, 512)
    OTi = [A16.t2(CAP) for _ in range(8)]
    WS1 = A16.t3(8, 512)
    WS2 = A16.t3(9, 512)
    A1B = A16.t2(512)
    ABT = A16.t3(22, CAP)
    DELB = A16.t3(4, D)
    MKB = A16.t3(4, 512)
    g1bS = A16.t2(D)
    g2bS = A16.t2(D)
    kb.dma("gpsimd", MKB[:, :, :], maskbd, "gq")
    kb.dma("gpsimd", g1bS, g1bi, "gq")
    kb.dma("gpsimd", g2bS, g2bi, "gq")

    def rmsnorm(dst3, src3, gs):
        for rt in range(4):
            kb.op("vector", lambda e, rt=rt: e.tensor_tensor(out=SQyB, in0=src3[:, rt, :],
                                                             in1=src3[:, rt, :], op=OP.mult))
            kb.op("vector", lambda e: e.tensor_reduce(out=stat[:, 0:1], in_=SQyB,
                                                      axis=AX, op=OP.add), inc="vch")
            kb.wait("vector", "vch")
            r = kb.op("vector", lambda e: e.tensor_scalar(out=stat[:, 1:2], in0=stat[:, 0:1],
                                                          scalar1=1.0 / D, scalar2=EPS_RMS,
                                                          op0=OP.mult, op1=OP.add), inc="ve")
            kb.wait("scalar", "ve", r)
            kb.op("scalar", lambda e: e.activation(out=stat[:, 2:3], in_=stat[:, 1:2],
                                                   func=AF.Ln), inc="sch")
            kb.wait("scalar", "sch")
            s2 = kb.op("scalar", lambda e: e.activation(out=stat[:, 3:4], in_=stat[:, 2:3],
                                                        func=AF.Exp, scale=-0.5), inc="se")
            kb.wait("vector", "se", s2)
            kb.op("vector", lambda e, rt=rt: e.tensor_scalar(out=SQyB, in0=src3[:, rt, :],
                                                             scalar1=stat[:, 3:4], scalar2=None,
                                                             op0=OP.mult))
            kb.op("vector", lambda e, rt=rt, dst3=dst3, gs=gs: e.tensor_tensor(
                out=dst3[:, rt, :], in0=SQyB, in1=gs, op=OP.mult), inc="ve")

    def transpose_to(dst3, src3, nkt):
        kb.wait("tensor", "ve")
        for rt in range(4):
            for eh in range(nkt):
                kb.op("tensor", lambda e, rt=rt, eh=eh: e.transpose(
                    out=PMI[0:128, 0:128], in_=src3[0:128, rt, 128 * eh:128 * eh + 128],
                    identity=eyeb), inc="pe")
                kb.wait("vector", "pe")
                kb.op("vector", lambda e, rt=rt, eh=eh: e.tensor_copy(
                    out=dst3[:, eh, 128 * rt:128 * rt + 128], in_=PMI[0:128, 0:128]), inc="ve")
                kb.wait("tensor", "ve")

    if dump == "x1d":
        return _dump([(out[:, :], x1d[:, :])])
    if dump == "xsel":
        return _dump([(out[128 * rt:128 * rt + 128, :], XSEL[:, rt, :]) for rt in range(4)])
    kb.wait("vector", "gq")
    rmsnorm(H1F, XSEL, g1bS)
    if dump == "h1":
        return _dump([(out[128 * rt:128 * rt + 128, :], H1F[:, rt, :]) for rt in range(4)])
    kb.op("gpsimd", lambda e: e.memset(H1T[0:1, 8, :], 1.0), inc="gp")
    kb.wait("tensor", "gp")
    transpose_to(H1T, H1F, 8)
    for dch in range(4):
        kb.wait("gpsimd", "pe")
        for kt in range(8):
            kb.dma("gpsimd", WS1[0:128, kt, :], wqkT[:, kt, 512 * dch:512 * dch + 512], "gq")
        kb.wait("tensor", "gq")
        for db in range(4):
            for kt in range(8):
                kb.op("tensor", lambda e, db=db, kt=kt: e.matmul(
                    out=PAV[0:128, 0:512], lhsT=WS1[0:128, kt, 128 * db:128 * db + 128],
                    rhs=H1T[:, kt, :], start=(kt == 0), stop=(kt == 7)),
                    inc="pe" if kt == 7 else None)
            kb.wait("vector", "pe")
            kb.op("vector", lambda e, dch=dch, db=db: e.tensor_copy(
                out=QKT[:, 4 * dch + db, :], in_=PAV[0:128, 0:512]), inc="ve")
            kb.wait("tensor", "ve")
    for ch in range(4):
        c0 = 260 * ch
        kb.wait("gpsimd", "pe")
        for kt in range(9):
            kb.dma("gpsimd", WS2[0:128, kt, 0:260], wvfT[:, kt, c0:c0 + 260], "gq")
        kb.wait("tensor", "gq")
        for rt in range(4):
            for kt in range(8):
                kb.op("tensor", lambda e, rt=rt, kt=kt: e.matmul(
                    out=PAV[0:128, 0:260], lhsT=H1T[:, kt, 128 * rt:128 * rt + 128],
                    rhs=WS2[0:128, kt, 0:260], start=(kt == 0), stop=False))
            kb.op("tensor", lambda e, rt=rt: e.matmul(
                out=PAV[0:128, 0:260], lhsT=H1T[0:1, 8, 128 * rt:128 * rt + 128],
                rhs=WS2[0:1, 8, 0:260], start=False, stop=True), inc="pe")
            kb.wait("vector", "pe")
            kb.op("vector", lambda e, rt=rt, c0=c0: e.tensor_copy(
                out=VIB[:, rt, c0:c0 + 260], in_=PAV[0:128, 0:260]), inc="ve")
            kb.wait("tensor", "ve")
    for h in range(H):
        qblk, qrow = h // 2, 64 * (h % 2)
        kblk, krow = 8 + h // 2, 64 * (h % 2)
        if h > 0:
            kb.wait("scalar", "peav")
        for st in range(4):
            psl = PSA[0:128, 512 * st:512 * st + 512] if st < 3 else PSB[0:128, 0:512]
            kb.op("tensor", lambda e, st=st, psl=psl, qblk=qblk, qrow=qrow, kblk=kblk, krow=krow: e.matmul(
                out=psl, lhsT=QKT[krow:krow + 64, kblk, 128 * st:128 * st + 128],
                rhs=QKT[qrow:qrow + 64, qblk, 0:512], start=True, stop=True), inc="pesc")
        kb.wait("vector", "pesc")
        for st in range(4):
            psl = PSA[0:128, 512 * st:512 * st + 512] if st < 3 else PSB[0:128, 0:512]
            if st > 0:
                kb.wait("vector", "seexp")
            vv = kb.op("vector", lambda e, st=st, psl=psl: e.scalar_tensor_tensor(
                out=MSKS, in0=psl, scalar=0.125, in1=MKB[:, st, :],
                op0=OP.mult, op1=OP.add), inc="ve")
            kb.wait("scalar", "ve", vv)
            kb.op("scalar", lambda e, st=st: e.activation(out=IAT[:, st, :], in_=MSKS,
                                                          func=AF.Exp), inc="seexp")
        kb.wait("tensor", "seexp")
        for st in range(4):
            kb.op("tensor", lambda e, st=st, h=h: e.matmul(
                out=PAV[0:65, 0:512], lhsT=VIB[:, st, 65 * h:65 * h + 65],
                rhs=IAT[:, st, :], start=(st == 0), stop=(st == 3)),
                inc="peav" if st == 3 else None)
        kb.wait("vector", "peav")
        kb.op("vector", lambda e: e.reciprocal(out=dinv, in_=PAV[64:65, 0:512]), inc="ve")
        kb.wait("tensor", "ve")
        kb.op("tensor", lambda e: e.matmul(out=PMI[0:64, 0:512], lhsT=onesb[0:1, 0:64],
                                           rhs=dinv, start=True, stop=True), inc="pe")
        kb.wait("vector", "pe")
        kb.op("vector", lambda e: e.tensor_copy(out=dbc, in_=PMI[0:64, 0:512]), inc="ve")
        kb.op("vector", lambda e, h=h: e.tensor_tensor(
            out=OTi[h // 2][64 * (h % 2):64 * (h % 2) + 64, :], in0=PAV[0:64, 0:512],
            in1=dbc, op=OP.mult), inc="veot")
        kb.wait("tensor", "veot")
    for jc in range(2):
        kb.wait("gpsimd", "pe")
        for kt in range(8):
            kb.dma("gpsimd", WS1[0:128, kt, :], woutT[:, kt, 512 * jc:512 * jc + 512], "gq")
        kb.wait("tensor", "gq")
        for rt in range(4):
            for j in range(8):
                kb.op("tensor", lambda e, rt=rt, j=j: e.matmul(
                    out=PAV[0:128, 0:512], lhsT=OTi[j][:, 128 * rt:128 * rt + 128],
                    rhs=WS1[0:128, j, :], start=(j == 0), stop=(j == 7)),
                    inc="pe" if j == 7 else None)
            kb.wait("vector", "pe")
            kb.op("vector", lambda e, rt=rt, jc=jc: e.tensor_copy(
                out=XATT[:, rt, 512 * jc:512 * jc + 512], in_=PAV[0:128, 0:512]), inc="ve")
            kb.wait("tensor", "ve")
    if dump == "xattn":
        return _dump([(out[128 * rt:128 * rt + 128, :], XATT[:, rt, :]) for rt in range(4)])
    for rt in range(4):
        kb.op("vector", lambda e, rt=rt: e.tensor_tensor(out=XATT[:, rt, :], in0=XATT[:, rt, :],
                                                         in1=XSEL[:, rt, :], op=OP.add), inc="ve")
    rmsnorm(H1F, XATT, g2bS)
    if dump == "h2":
        return _dump([(out[128 * rt:128 * rt + 128, :], H1F[:, rt, :]) for rt in range(4)])
    transpose_to(H1T, H1F, 8)
    ABQ = H1F
    FCH = [(0, 1024), (1024, 2048), (2048, 2730)]
    for f0, f1 in FCH:
        for sub in range((f1 - f0 + 511) // 512):
            s0f = f0 + 512 * sub
            nn = min(512, f1 - s0f)
            kb.wait("gpsimd", "pe")
            for kt in range(8):
                kb.dma("gpsimd", WS1[0:128, kt, 0:nn], w1T[:, kt, s0f:s0f + nn], "gq")
                kb.dma("gpsimd", WS2[0:128, kt, 0:nn], w2T[:, kt, s0f:s0f + nn], "gq")
            kb.wait("tensor", "gq")
            for rt in range(4):
                for kt in range(8):
                    kb.op("tensor", lambda e, rt=rt, kt=kt, nn=nn: e.matmul(
                        out=PAV[0:128, 0:nn], lhsT=H1T[:, kt, 128 * rt:128 * rt + 128],
                        rhs=WS1[0:128, kt, 0:nn], start=(kt == 0), stop=(kt == 7)))
                for kt in range(8):
                    kb.op("tensor", lambda e, rt=rt, kt=kt, nn=nn: e.matmul(
                        out=PMI[0:128, 0:nn], lhsT=H1T[:, kt, 128 * rt:128 * rt + 128],
                        rhs=WS2[0:128, kt, 0:nn], start=(kt == 0), stop=(kt == 7)),
                        inc="pe" if kt == 7 else None)
                kb.wait("scalar", "pe")
                kb.op("scalar", lambda e, nn=nn: e.activation(out=A1B[:, 0:nn],
                                                              in_=PAV[0:128, 0:nn],
                                                              func=AF.Silu), inc="se")
                kb.wait("vector", "se")
                kb.op("vector", lambda e, rt=rt, nn=nn, s0f=s0f, f0=f0: e.tensor_tensor(
                    out=ABQ[:, rt, s0f - f0:s0f - f0 + nn], in0=A1B[:, 0:nn],
                    in1=PMI[0:128, 0:nn], op=OP.mult), inc="ve")
                kb.wait("tensor", "ve")
        for rt in range(4):
            for eh in range((f1 - f0 + 127) // 128):
                gkt = (f0 + 128 * eh) // 128
                ew = min(128, f1 - f0 - 128 * eh)
                kb.op("tensor", lambda e, rt=rt, eh=eh, ew=ew: e.transpose(
                    out=PMI[0:ew, 0:128], in_=ABQ[0:128, rt, 128 * eh:128 * eh + ew],
                    identity=eyeb), inc="pe")
                kb.wait("vector", "pe")
                kb.op("vector", lambda e, rt=rt, gkt=gkt, ew=ew: e.tensor_copy(
                    out=ABT[0:ew, gkt, 128 * rt:128 * rt + 128], in_=PMI[0:ew, 0:128]), inc="ve")
                kb.wait("tensor", "ve")
    w3_mark = A16.mark()
    W3s = [A16.t2(D) for _ in range(2)]
    xfm = {}
    for kt in range(22):
        kw = min(128, DFF - 128 * kt)
        if kt >= 2:
            kb.wait("gpsimd", "pexf", xfm[kt - 2])
        kb.dma("gpsimd", W3s[kt % 2], w3T[:, kt, :], "gq")
        kb.wait("tensor", "gq")
        for rt in range(4):
            for jc in range(2):
                last = (rt == 3 and jc == 1)
                kb.op("tensor", lambda e, rt=rt, jc=jc, kt=kt, kw=kw: e.matmul(
                    out=GPS[2 * rt + jc][0:128, :], lhsT=ABT[0:kw, kt, 128 * rt:128 * rt + 128],
                    rhs=W3s[kt % 2][0:kw, 512 * jc:512 * jc + 512],
                    start=(kt == 0), stop=(kt == 21)), inc="pexf" if last else None)
        xfm[kt] = kb.cnt["pexf"]
    kb.wait("vector", "pexf", xfm[21])
    for rt in range(4):
        kb.op("vector", lambda e, rt=rt: e.tensor_tensor(
            out=XATT[:, rt, :], in0=XATT[:, rt, :], in1=XSEL[:, rt, :], op=OP.subtract))
        kb.op("vector", lambda e, rt=rt: e.scalar_tensor_tensor(
            out=XATT[:, rt, 0:512], in0=XATT[:, rt, 0:512], scalar=1.0,
            in1=GPS[2 * rt + 0][0:128, :], op0=OP.mult, op1=OP.add))
        kb.op("vector", lambda e, rt=rt: e.scalar_tensor_tensor(
            out=XATT[:, rt, 512:1024], in0=XATT[:, rt, 512:1024], scalar=1.0,
            in1=GPS[2 * rt + 1][0:128, :], op0=OP.mult, op1=OP.add))
        kb.op("vector", lambda e, rt=rt: e.tensor_copy(out=DELB[:, rt, :], in_=XATT[:, rt, :]), inc="ve")
    kb.wait("tensor", "ve")
    if dump == "delb":
        return _dump([(out[128 * rt:128 * rt + 128, :], XATT[:, rt, :]) for rt in range(4)])
    if dump == "qkt":
        return _dump([(out[128 * i:128 * i + 128, 0:512], QKT[:, i, :], "gpsimd")
                      for i in range(16)])

    A16.reset(w3_mark)
    PRB = A16.t3(4, 128)
    Y2B = A32.t2(D)
    for li in range(NT):
        lb = li
        cb = li % 2
        kb.wait("sync", "ve")
        kb.dma("sync", X1fs[cb], x1d[128 * lb:128 * lb + 128, :], "sq")
        kb.wait("vector", "pe")
        for rb in range(4):
            kb.op("vector", lambda e, lb=lb, rb=rb: e.tensor_scalar(
                out=PRB[:, rb, :], in0=RBC[:, 128 * lb:128 * lb + 128],
                scalar1=iorb[:, rb:rb + 1], scalar2=None, op0=OP.is_equal), inc="ve")
        kb.wait("tensor", "ve")
        for jc in range(2):
            for rb in range(4):
                kb.op("tensor", lambda e, rb=rb, jc=jc: e.matmul(
                    out=PAV[0:128, 0:512], lhsT=PRB[:, rb, :],
                    rhs=DELB[0:128, rb, 512 * jc:512 * jc + 512],
                    start=(rb == 0), stop=(rb == 3)), inc="pe" if rb == 3 else None)
            kb.wait("vector", "pe")
            kb.wait("vector", "sq")
            kb.op("vector", lambda e, jc=jc, cb=cb: e.scalar_tensor_tensor(
                out=Y2B[:, 512 * jc:512 * jc + 512], in0=X1fs[cb][:, 512 * jc:512 * jc + 512],
                scalar=2.0, in1=PAV[0:128, 0:512], op0=OP.mult, op1=OP.add), inc="ve")
            kb.wait("tensor", "ve")
        kb.op("vector", lambda e: e.tensor_reduce(out=stat[:, 0:1], in_=Y2B, axis=AX, op=OP.add))
        kb.op("vector", lambda e: e.tensor_tensor(out=SQyB, in0=Y2B, in1=Y2B, op=OP.mult))
        kb.op("vector", lambda e: e.tensor_reduce(out=stat[:, 1:2], in_=SQyB, axis=AX, op=OP.add))
        kb.op("vector", lambda e: e.tensor_scalar(out=stat[:, 2:3], in0=stat[:, 0:1],
                                                  scalar1=1.0 / D, scalar2=None, op0=OP.mult))
        kb.op("vector", lambda e: e.tensor_scalar(out=stat[:, 4:5], in0=stat[:, 1:2],
                                                  scalar1=1.0 / D, scalar2=None,
                                                  op0=OP.mult), inc="vch")
        kb.wait("vector", "vch")
        kb.op("vector", lambda e: e.tensor_tensor(out=stat[:, 3:4], in0=stat[:, 2:3],
                                                  in1=stat[:, 2:3], op=OP.mult), inc="vch")
        kb.wait("vector", "vch")
        v1 = kb.op("vector", lambda e: e.tensor_tensor(out=stat[:, 5:6], in0=stat[:, 4:5],
                                                       in1=stat[:, 3:4], op=OP.subtract), inc="ve")
        kb.wait("scalar", "ve", v1)
        kb.op("scalar", lambda e: e.activation(out=stat[:, 6:7], in_=stat[:, 5:6],
                                               func=AF.Ln, bias=epsln), inc="sch")
        kb.wait("scalar", "sch")
        s1_ = kb.op("scalar", lambda e: e.activation(out=stat[:, 7:8], in_=stat[:, 6:7],
                                                     func=AF.Exp, scale=-0.5), inc="se")
        kb.wait("vector", "se", s1_)
        kb.op("vector", lambda e: e.tensor_scalar(out=stat[:, 8:9], in0=stat[:, 2:3],
                                                  scalar1=-1.0, scalar2=None,
                                                  op0=OP.mult), inc="vch")
        kb.wait("vector", "vch")
        kb.op("vector", lambda e: e.tensor_scalar(out=Y2B, in0=Y2B, scalar1=stat[:, 8:9],
                                                  scalar2=stat[:, 7:8], op0=OP.add, op1=OP.mult), inc="ve")
        kb.wait("sync", "ve")
        kb.dma("sync", out[128 * li:128 * li + 128, :], Y2B, "stq")
        kb.wait("vector", "stq")

    _finalize(kb, nc)
    return nc


def _finalize(kb, nc):
    kb.wait("sync", "stq") if "stq" in kb.cnt else None
    kb.wait("sync", "sq")
    with nc.Block() as block:
        @block.sync
        def _(e):
            for f in kb.ops["sync"]:
                f(e)

        @block.gpsimd
        def _(e):
            for f in kb.ops["gpsimd"]:
                f(e)

        @block.tensor
        def _(e):
            for f in kb.ops["tensor"]:
                f(e)

        @block.vector
        def _(e):
            for f in kb.ops["vector"]:
                f(e)

        @block.scalar
        def _(e):
            for f in kb.ops["scalar"]:
                f(e)


def _wtiles(aT, ktiles):
    K, N = aT.shape
    assert K <= 128 * ktiles
    buf = np.zeros((128 * ktiles, N), dtype=np.float32)
    buf[:K] = aT
    return np.ascontiguousarray(buf.reshape(ktiles, 128, N).transpose(1, 0, 2))


def _prep_inputs(inp, b):
    x = np.ascontiguousarray(inp["L0_attn_query"][b]).astype(np.float32, copy=False)
    d = {}
    d["xrow"] = x
    d["xtT"] = _wtiles(np.ascontiguousarray(x.T), 8)
    d["wqT"] = _wtiles(np.ascontiguousarray(inp["L0_attn_Wq"].T), 8)
    d["wkT"] = _wtiles(np.ascontiguousarray(inp["L0_attn_Wk"].T), 8)
    wv = np.zeros((1152, H * 65), dtype=np.float32)
    wvf = np.zeros((1152, H * 65), dtype=np.float32)
    WvT = inp["L0_attn_Wv"].T
    Wqkv = inp["L0_ffn_W_qkv"]
    for h in range(H):
        wv[0:1024, 65 * h:65 * h + 64] = WvT[:, 64 * h:64 * h + 64]
        wv[1024, 65 * h:65 * h + 64] = inp["L0_attn_bv"][64 * h:64 * h + 64]
        wv[1024, 65 * h + 64] = 1.0
        wvf[0:1024, 65 * h:65 * h + 64] = Wqkv[2 * D + 64 * h:2 * D + 64 * h + 64, :].T
        wvf[1024, 65 * h + 64] = 1.0
    d["wvT"] = np.ascontiguousarray(wv.reshape(9, 128, H * 65).transpose(1, 0, 2))
    d["wvfT"] = np.ascontiguousarray(wvf.reshape(9, 128, H * 65).transpose(1, 0, 2))
    d["woT"] = _wtiles(np.ascontiguousarray(inp["L0_attn_Wo"].T), 8)
    d["wbT"] = _wtiles(np.ascontiguousarray(inp["L0_attn_Wb"].T), 8)
    d["wqkT"] = _wtiles(np.ascontiguousarray(Wqkv[0:2 * D, :].T), 8)
    d["woutT"] = _wtiles(np.ascontiguousarray(inp["L0_ffn_W_out"].T), 8)
    d["w1T"] = _wtiles(np.ascontiguousarray(inp["L0_ffn_W1"].T), 8)
    d["w2T"] = _wtiles(np.ascontiguousarray(inp["L0_ffn_W2"].T), 8)
    d["w3T"] = _wtiles(np.ascontiguousarray(inp["L0_ffn_W3"].T), 22)
    d["bqt"] = np.ascontiguousarray(inp["L0_attn_bq"].reshape(8, 128).T)
    d["bkt"] = np.ascontiguousarray(inp["L0_attn_bk"].reshape(8, 128).T)
    d["brows"] = np.concatenate([inp["L0_attn_bo"], inp["L0_attn_bb"]]).reshape(1, 2 * D)
    d["wrb"] = np.ascontiguousarray(np.broadcast_to(inp["L0_ffn_w_router"], (128, D)))
    d["g1bi"] = np.ascontiguousarray(np.broadcast_to(inp["L0_ffn_g1"], (128, D)))
    d["g2bi"] = np.ascontiguousarray(np.broadcast_to(inp["L0_ffn_g2"], (128, D)))
    d["rwrep"] = np.ascontiguousarray(
        inp["L0_attn_radial_weights"].reshape(H)[np.arange(128) % H]).reshape(128, 1)
    d["curv"] = np.asarray(inp["L0_attn_curvature"], dtype=np.float32).reshape(1, 1)
    p = np.arange(128)
    s = np.arange(128)
    d["lt128"] = (s[None, :] < p[:, None]).astype(np.float32)
    d["iota512b"] = np.ascontiguousarray(
        np.broadcast_to(np.arange(512, dtype=np.float32), (128, 512)))
    d["iotarb"] = (p[:, None] + 128 * np.arange(4)[None, :]).astype(np.float32)
    mb = np.zeros((128, 4, 512), dtype=np.float32)
    for st in range(4):
        sglob = 128 * st + p
        mb[:, st, :] = np.where(sglob[:, None] <= np.arange(512)[None, :], 0.0, -1e30)
    d["maskbd"] = mb
    d["onesrow"] = np.ones((1, 128), dtype=np.float32)
    d["onescol"] = np.ones((128, 1), dtype=np.float32)
    d["eyef"] = np.eye(128, dtype=np.float32)
    for k in d:
        d[k] = np.ascontiguousarray(d[k], dtype=np.float32)
    return d


def kernel(**inputs):
    nc = build(2)
    in_maps = [_prep_inputs(inputs, c % 4) for c in range(8)]
    res = run_bass_kernel_spmd(nc, in_maps, core_ids=list(range(8)))
    outp = np.empty((B, T, D), dtype=np.float32)
    for b_ in range(4):
        outp[b_] = res.results[b_]["out"]
    return outp


# version 05
# speedup vs baseline: 1.1755x; 1.1755x over previous
import contextlib
import numpy as np
import ml_dtypes
import concourse.bass as bass
import concourse.mybir as mybir
from concourse.bass_utils import run_bass_kernel_spmd
from concourse.alu_op_type import AluOpType as OP

F32 = mybir.dt.float32
BF16 = mybir.dt.bfloat16
AF = mybir.ActivationFunctionType
AX = mybir.AxisListType.X

B, T, D = 4, 4096, 1024
H, CAP, DFF = 16, 512, 2730
EPS_H, EPS_LN, EPS_RMS = 1e-8, 1e-5, 1e-6
NT, NL, NS = 32, 8, 32
GRPS = [(3 * i, min(3 * i + 3, NS)) for i in range(11)]
ENGINES = ("sync", "gpsimd", "tensor", "vector", "scalar")


class KB:
    def __init__(self, nc, es):
        self.nc, self.es = nc, es
        self.ops = {e: [] for e in ENGINES}
        self.cnt = {}
        self.sems = {}

    def sem(self, name):
        if name not in self.sems:
            self.sems[name] = self.es.enter_context(self.nc.semaphore(name))
            self.cnt[name] = 0
        return self.sems[name]

    def op(self, eng, fn, inc=None):
        if inc is None:
            self.ops[eng].append(fn)
            return None
        s = self.sem(inc)
        self.cnt[inc] += 1
        self.ops[eng].append(lambda e, f=fn, s=s: f(e).then_inc(s, 1))
        return self.cnt[inc]

    def dma(self, eng, out_ap, in_ap, inc, **kw):
        s = self.sem(inc)
        self.cnt[inc] += 16
        self.ops[eng].append(
            lambda e, o=out_ap, i=in_ap, s=s, kw=kw: e.dma_start(out=o, in_=i, **kw).then_inc(s, 16))
        return self.cnt[inc]

    def wait(self, eng, sem, val=None):
        s = self.sem(sem)
        v = self.cnt.get(sem, 0) if val is None else val
        if v <= 0:
            return 0
        self.ops[eng].append(lambda e, s=s, v=v: e.wait_ge(s, v))
        return v

    def barrier(self):
        marks = []
        for eng in ENGINES:
            sm = "bar_" + eng
            marks.append((sm, self.op(eng, lambda e: e.nop(), inc=sm)))
        for eng in ENGINES:
            for sm, v in marks:
                self.wait(eng, sm, v)


class Arena:
    def __init__(self, ap, total):
        self.ap, self.total, self.cur = ap, total, 0

    def mark(self):
        return self.cur

    def reset(self, m):
        self.cur = m

    def t2(self, n):
        assert self.cur + n <= self.total, f"arena overflow {self.cur}+{n}>{self.total}"
        off = self.cur
        self.cur += n
        return self.ap[:, off:off + n]

    def t3(self, a, b):
        return self.t2(a * b).rearrange("p (a b) -> p a b", a=a)


def build(stage, dump=None):
    es = contextlib.ExitStack()
    nc = bass.Bass("TRN2", target_bir_lowering=False)
    kb = KB(nc, es)

    def _dump(pairs):
        for p in pairs:
            o, i = p[0], p[1]
            eng = p[2] if len(p) > 2 else "sync"
            kb.dma(eng, o, i, "stq")
        kb.wait("sync", "stq")
        kb.wait("gpsimd", "stq")
        _finalize(kb, nc)
        return nc

    def din(name, shape):
        return nc.dram_tensor(name, list(shape), F32, kind="ExternalInput").ap()

    xrow = din("xrow", (T, D))
    xtT = din("xtT", (128, 8, T))
    wqT = din("wqT", (128, 8, D))
    wkT = din("wkT", (128, 8, D))
    wvT = din("wvT", (128, 9, H * 65))
    woT = din("woT", (128, 8, D))
    wbT = din("wbT", (128, 8, D))
    wqkT = din("wqkT", (128, 8, 2 * D))
    wvfT = din("wvfT", (128, 9, H * 65))
    woutT = din("woutT", (128, 8, D))
    w1T = din("w1T", (128, 8, DFF))
    w2T = din("w2T", (128, 8, DFF))
    w3T = din("w3T", (128, 22, D))
    bqt = din("bqt", (128, 8))
    bkt = din("bkt", (128, 8))
    brows = din("brows", (1, 2 * D))
    wrb = din("wrb", (128, D))
    g1bi = din("g1bi", (128, D))
    g2bi = din("g2bi", (128, D))
    rwrep = din("rwrep", (128, 1))
    curv = din("curv", (1, 1))
    lt128 = din("lt128", (128, 128))
    iota512b = din("iota512b", (128, 512))
    iotarb = din("iotarb", (128, 4))
    maskbd = din("maskbd", (128, 4, 512))
    onesrow = din("onesrow", (1, 128))
    onescol = din("onescol", (128, 1))
    eyef = din("eyef", (128, 128))

    out = nc.dram_tensor("out", [T, D], F32, kind="ExternalOutput").ap()

    x1d = nc.dram_tensor("x1d", [T, D], F32).ap()
    vpdb = nc.dram_tensor("vpdb", [T, H * 65], BF16).ap()
    otdb = nc.dram_tensor("otdb", [8, 128, T], BF16).ap()
    stkd = nc.dram_tensor("stkd", [128, 8, 512], F32).ap()
    rowd = nc.dram_tensor("rowd", [128, 32], F32).ap()
    rowd2 = nc.dram_tensor("rowd2", [128, 32], F32).ap()

    def sb(name, shape, dt):
        return es.enter_context(nc.sbuf_tensor(name, list(shape), dt))[:]

    onesb = sb("onesb", (1, 128), F32)
    onesbf = sb("onesbf", (1, 128), BF16)
    onesc = sb("onesc", (128, 1), F32)
    eyeb = sb("eyeb", (128, 128), F32)
    lt = sb("lt", (128, 128), F32)
    io512 = sb("io512", (128, 512), F32)
    iorb = sb("iorb", (128, 4), F32)
    bqs = sb("bqs", (128, 8), F32)
    bks = sb("bks", (128, 8), F32)
    radl = sb("radl", (128, 1), F32)
    nalpha = sb("nalpha", (128, 1), F32)
    small = sb("small", (1, 16), F32)
    stat = sb("stat", (128, 16), F32)
    epsln = sb("epsln", (128, 1), F32)
    dinv = sb("dinv", (1, 512), F32)
    dbc = sb("dbc", (64, 512), F32)
    vstk = sb("vstk", (128, 512), F32)
    sc32 = sb("sc32", (128, 32), F32)
    rk32 = sb("rk32", (128, 32), F32)
    rowst = sb("rowst", (1, 512), F32)

    A16 = Arena(sb("A16", (128, 57344), BF16), 57344)
    A32 = Arena(sb("A32", (128, 20992), F32), 20992)

    PSA = es.enter_context(nc.psum_tensor("PSA", [128, 1536], F32))[:]
    PSB = es.enter_context(nc.psum_tensor("PSB", [128, 1536], F32))[:]
    PAV = es.enter_context(nc.psum_tensor("PAV", [128, 512], F32))[:]
    PMI = es.enter_context(nc.psum_tensor("PMI", [128, 512], F32))[:]
    PROJ_PS = [PSA[:, 0:512], PSB[:, 0:512], PSA[:, 512:1024], PSB[:, 512:1024]]
    NRM_PS = [PMI[0:1, 0:512], PAV[0:1, 0:512], PSA[0:1, 1024:1536], PSB[0:1, 1024:1536]]

    for o, i in [(onesb, onesrow), (onesc, onescol), (eyeb, eyef), (lt, lt128),
                 (io512, iota512b), (iorb, iotarb),
                 (bqs, bqt), (bks, bkt), (radl, rwrep), (small[0:1, 0:1], curv)]:
        kb.dma("sync", o, i, "dq")
    kb.dma("gpsimd", onesbf, onesrow, "gq")

    XT = A16.t3(8, T)
    markA16 = A16.mark()
    markA32 = A32.mark()
    kb.dma("gpsimd", XT[:, :, :], xtT, "gq")
    kb.op("gpsimd", lambda e: e.memset(epsln, EPS_LN), inc="gp")

    kb.wait("vector", "dq")
    kb.wait("vector", "gp")
    kb.op("vector", lambda e: e.tensor_scalar(out=small[0:1, 1:2], in0=small[0:1, 0:1],
                                              scalar1=-1.0, scalar2=None, op0=OP.mult), inc="vch")
    kb.wait("vector", "vch")
    n = kb.op("vector", lambda e: e.tensor_scalar(out=small[0:1, 2:3], in0=small[0:1, 0:1],
                                                  scalar1=small[0:1, 1:2], scalar2=EPS_H,
                                                  op0=OP.max, op1=OP.max), inc="ve")
    kb.wait("scalar", "ve", n)
    kb.op("scalar", lambda e: e.activation(out=small[0:1, 4:5], in_=small[0:1, 2:3],
                                           func=AF.Ln), inc="sch")
    kb.wait("scalar", "sch")
    m = kb.op("scalar", lambda e: e.activation(out=small[0:1, 5:6], in_=small[0:1, 4:5],
                                               func=AF.Exp, scale=0.5), inc="se")
    kb.wait("vector", "se", m)
    kb.op("vector", lambda e: e.tensor_scalar(out=small[0:1, 6:7], in0=small[0:1, 5:6],
                                              scalar1=EPS_H, scalar2=8.0, op0=OP.add,
                                              op1=OP.mult), inc="vch")
    kb.wait("vector", "vch")
    kb.op("vector", lambda e: e.reciprocal(out=small[0:1, 7:8], in_=small[0:1, 6:7]), inc="vch")
    kb.wait("vector", "vch")
    n2 = kb.op("vector", lambda e: e.tensor_scalar(out=small[0:1, 8:9], in0=small[0:1, 7:8],
                                                   scalar1=-1.0, scalar2=None, op0=OP.mult), inc="ve")
    kb.wait("tensor", "ve", n2)
    k1 = kb.op("tensor", lambda e: e.matmul(out=PMI[:, 0:1], lhsT=onesb, rhs=small[0:1, 8:9],
                                            start=True, stop=True), inc="pe")
    kb.wait("vector", "pe", k1)
    kb.op("vector", lambda e: e.tensor_copy(out=nalpha, in_=PMI[:, 0:1]), inc="ve")
    ms = kb.op("scalar", lambda e: e.activation(out=radl, in_=radl, func=AF.Exp, scale=-1.0), inc="se")
    kb.wait("vector", "se", ms)
    kb.op("vector", lambda e: e.tensor_scalar(out=radl, in0=radl, scalar1=1.0,
                                              scalar2=None, op0=OP.add), inc="vch")
    kb.wait("vector", "vch")
    kb.op("vector", lambda e: e.reciprocal(out=radl, in_=radl), inc="ve")
    kb.barrier()
    if dump == "smalls":
        return _dump([(out[0:1, 0:16], small), (out[1:2, 0:1], nalpha[0:1, :]),
                      (out[2:3, 0:1], radl[0:1, :])])

    def f_chain(dst, src, sa, sbuf, mul_radial=False):
        kb.wait("scalar", "ve")
        kb.op("scalar", lambda e: e.activation(out=sa, in_=src, func=AF.Ln))
        kb.op("scalar", lambda e: e.activation(out=sa, in_=sa, func=AF.Exp, scale=0.5))
        q = kb.op("scalar", lambda e: e.activation(out=sbuf, in_=sa, func=AF.Exp, scale=2.0), inc="se")
        kb.wait("vector", "se", q)
        kb.op("vector", lambda e: e.tensor_scalar(out=sbuf, in0=sbuf, scalar1=1.0,
                                                  scalar2=None, op0=OP.add))
        kb.op("vector", lambda e: e.reciprocal(out=sbuf, in_=sbuf), inc="vch")
        kb.wait("vector", "vch")
        kb.op("vector", lambda e: e.tensor_scalar(out=sbuf, in0=sbuf, scalar1=-2.0,
                                                  scalar2=1.0, op0=OP.mult, op1=OP.add))
        kb.op("vector", lambda e: e.tensor_scalar(out=sa, in0=sa, scalar1=EPS_H,
                                                  scalar2=None, op0=OP.add))
        kb.op("vector", lambda e: e.reciprocal(out=sa, in_=sa), inc="vch")
        kb.wait("vector", "vch")
        if mul_radial:
            kb.op("vector", lambda e: e.tensor_scalar(out=sa, in0=sa, scalar1=radl,
                                                      scalar2=None, op0=OP.mult))
        return kb.op("vector", lambda e: e.tensor_tensor(out=dst, in0=sa, in1=sbuf,
                                                         op=OP.mult), inc="ve")

    mk16 = A16.mark(); mk32 = A32.mark()
    WVs = A16.t3(9, 260)
    VEVB = A16.t2(260)
    VEV = A32.t2(260)
    SQ5 = A32.t2(260)
    for ch in range(4):
        c0 = 260 * ch
        kb.wait("gpsimd", "pe")
        for kt in range(9):
            kb.dma("gpsimd", WVs[0:128, kt, :], wvT[:, kt, c0:c0 + 260], "gq")
        kb.wait("tensor", "gq")
        for tb in range(NT):
            pscur = PSA if tb % 2 == 0 else PSB
            for kt in range(8):
                kb.op("tensor", lambda e, kt=kt, tb=tb, p=pscur: e.matmul(
                    out=p[0:128, 0:260], lhsT=XT[0:128, kt, 128 * tb:128 * tb + 128],
                    rhs=WVs[0:128, kt, :], start=(kt == 0), stop=False))
            kb.op("tensor", lambda e, p=pscur: e.matmul(
                out=p[0:128, 0:260], lhsT=onesbf, rhs=WVs[0:1, 8, :],
                start=False, stop=True), inc="pe")
            kb.wait("vector", "pe")
            kb.op("vector", lambda e, p=pscur: e.tensor_copy(out=VEV, in_=p[0:128, 0:260]), inc="ve")
            kb.wait("tensor", "ve")
            kb.op("vector", lambda e: e.tensor_tensor(out=SQ5, in0=VEV, in1=VEV, op=OP.mult))
            for hl in range(4):
                h = 4 * ch + hl
                kb.op("vector", lambda e, hl=hl, tb=tb, h=h: e.tensor_reduce(
                    out=vstk[0:128, 16 * tb + h:16 * tb + h + 1],
                    in_=SQ5[:, 65 * hl:65 * hl + 64], axis=AX, op=OP.add))
            kb.op("vector", lambda e: e.tensor_copy(out=VEVB, in_=VEV), inc="ve")
            kb.wait("sync", "ve")
            kb.dma("sync", vpdb[128 * tb:128 * tb + 128, c0:c0 + 260], VEVB, "sq")
            kb.wait("vector", "sq")
    A16.reset(mk16)
    FCA = A32.t2(512)
    FCB = A32.t2(512)
    kb.wait("scalar", "ve")
    f_chain(vstk, vstk, FCA, FCB)
    A32.reset(mk32)
    kb.barrier()
    if dump == "vstk":
        return _dump([(out[0:128, 0:512], vstk)])
    if dump == "vpdb":
        return _dump([(out[:, :], vpdb[:, 0:1024], "gpsimd")])

    mk32 = A32.mark(); mk16 = A16.mark()
    KQP = A32.t2(T)
    KQPc = KQP.rearrange("p (c n) -> p c n", n=512)
    SQB = A32.t2(T)
    SQBc = SQB.rearrange("p (c n) -> p c n", n=512)
    nstk = A32.t3(2, 512)
    NST = A32.t3(2, 512)
    WQs = A16.t3(8, 64)
    WKs = A16.t3(8, 64)

    def proj_head(h, dst, WQb, WKb):
        kb.wait("gpsimd", "pe")
        for kt in range(8):
            kb.dma("gpsimd", WQb[0:128, kt, :], wqT[:, kt, 64 * h:64 * h + 64], "gq")
            kb.dma("gpsimd", WKb[0:128, kt, :], wkT[:, kt, 64 * h:64 * h + 64], "gq")
        kb.wait("tensor", "gq")
        for c in range(8):
            for qk in range(2):
                W = WKb if qk == 0 else WQb
                pbank = PROJ_PS[(2 * c + qk) % 4]
                for kt in range(8):
                    kb.op("tensor", lambda e, W=W, kt=kt, c=c, p=pbank: e.matmul(
                        out=p[0:64, 0:512], lhsT=W[0:128, kt, :],
                        rhs=XT[0:128, kt, 512 * c:512 * c + 512],
                        start=(kt == 0), stop=(kt == 7)), inc="pe" if kt == 7 else None)
                kb.wait("vector", "pe")
                bias = (bks if qk == 0 else bqs)[64 * (h % 2):64 * (h % 2) + 64, h // 2:h // 2 + 1]
                r0 = 0 if qk == 0 else 64
                kb.op("vector", lambda e, p=pbank, bias=bias, r0=r0, c=c, dst=dst: e.tensor_scalar(
                    out=dst[r0:r0 + 64, 512 * c:512 * c + 512], in0=p[0:64, 0:512],
                    scalar1=bias, scalar2=None, op0=OP.add), inc="ve")
        kb.wait("tensor", "ve")

    for h in range(H):
        proj_head(h, KQP, WQs, WKs)
        kb.op("vector", lambda e: e.tensor_tensor(out=SQB, in0=KQP, in1=KQP, op=OP.mult), inc="ve")
        kb.wait("tensor", "ve")
        for c in range(8):
            pk = NRM_PS[c % 4]
            pq = NRM_PS[(c % 4) ^ 1]
            kb.op("tensor", lambda e, c=c, pk=pk: e.matmul(out=pk, lhsT=onesc[0:64, :],
                                                           rhs=SQBc[0:64, c, :],
                                                           start=True, stop=True), inc="pe")
            kb.op("tensor", lambda e, c=c, pq=pq: e.matmul(out=pq, lhsT=onesc[64:128, :],
                                                           rhs=SQBc[64:128, c, :],
                                                           start=True, stop=True), inc="pe")
            kb.wait("vector", "pe")
            p = h + 16 * c
            kb.op("vector", lambda e, pk=pk: e.tensor_copy(out=NST[0:1, 0, :], in_=pk))
            kb.op("vector", lambda e, pq=pq: e.tensor_copy(out=NST[0:1, 1, :], in_=pq), inc="ve")
            kb.wait("sync", "ve")
            kb.dma("sync", nstk[p:p + 1, 1, 0:512], NST[0:1, 0, :], "sq")
            kb.dma("sync", nstk[p:p + 1, 0, 0:512], NST[0:1, 1, :], "sq")
            kb.wait("tensor", "ve")
            kb.wait("vector", "sq")

    if dump == "nstk":
        return _dump([(out[0:128, 0:512], nstk[:, 0, :]),
                      (out[128:256, 0:512], nstk[:, 1, :])])

    FK = A32.t2(512)
    KN = A32.t2(512)
    FCA = A32.t2(512)
    FCB = A32.t2(512)
    csk = A32.t2(512)
    csq = A32.t2(512)
    rxk = A32.t3(3, 512)
    rxq = A32.t3(3, 512)
    kb.wait("scalar", "ve")
    kb.wait("scalar", "sq")
    for side in range(2):
        f_chain(FK, nstk[:, 1 - side, :], FCA, FCB, mul_radial=(side == 1))
        rx = rxk if side == 0 else rxq
        cs = csk if side == 0 else csq
        kb.op("vector", lambda e: e.tensor_tensor(out=KN, in0=FK, in1=FK, op=OP.mult))
        kb.op("vector", lambda e, side=side: e.tensor_tensor(out=KN, in0=KN,
                                                             in1=nstk[:, 1 - side, :], op=OP.mult))
        kb.op("vector", lambda e, rx=rx: e.tensor_scalar(out=rx[:, 2, :], in0=KN, scalar1=-1.0,
                                                         scalar2=1.0, op0=OP.mult, op1=OP.add))
        kb.op("vector", lambda e, rx=rx: e.tensor_scalar(out=rx[:, 2, :], in0=rx[:, 2, :],
                                                         scalar1=1e-20, scalar2=None, op0=OP.max))
        kb.op("vector", lambda e, rx=rx: e.reciprocal(out=rx[:, 1, :], in_=rx[:, 2, :]), inc="vch")
        kb.wait("vector", "vch")
        kb.op("vector", lambda e, rx=rx: e.memset(rx[:, 2, :], 1.0))
        kb.op("vector", lambda e, rx=rx, cs=cs: e.tensor_tensor(out=cs, in0=FK,
                                                                in1=rx[:, 1, :], op=OP.mult))
        if side == 0:
            kb.op("vector", lambda e, cs=cs: e.tensor_scalar(out=cs, in0=cs, scalar1=-4.0,
                                                             scalar2=None, op0=OP.mult))
        kb.op("vector", lambda e, rx=rx: e.tensor_tensor(out=rx[:, 0, :], in0=KN,
                                                         in1=rx[:, 1, :], op=OP.mult))
        kb.op("vector", lambda e, rx=rx: e.tensor_scalar(out=rx[:, 0, :], in0=rx[:, 0, :],
                                                         scalar1=2.0, scalar2=None, op0=OP.mult), inc="ve")
    kb.wait("sync", "ve")
    kb.dma("sync", stkd[:, 0, :], csk, "sq")
    kb.dma("sync", stkd[:, 1, :], csq, "sq")
    for r in range(3):
        kb.dma("sync", stkd[:, 2 + r, :], rxk[:, r, :], "sq")
        kb.dma("sync", stkd[:, 5 + r, :], rxq[:, r, :], "sq")
    kb.barrier()
    A32.reset(mk32)
    A16.reset(mk16)
    if dump == "stkd":
        return _dump([(out[128 * j:128 * j + 128, 0:512], stkd[:, j, :]) for j in range(8)])

    KQb = [A16.t2(T) for _ in range(2)]
    QQb = [A16.t2(T) for _ in range(2)]
    VPh = A16.t3(NT, 65)
    ATb = [A16.t2(1536) for _ in range(2)]
    OTS = A16.t2(512)
    WQs2 = A16.t3(8, 64)
    WKs2 = A16.t3(8, 64)
    KQP2 = A32.t2(T)
    LxB = [A32.t2(1536) for _ in range(2)]

    def stk_row(slot, h, c):
        return stkd[:, slot, :].rearrange("(j h) c -> h j c", h=16)[h:h + 1, c, 0:512]

    attn_end = {}
    av_last = {0: 0, 1: 0}

    def prep_head(h):
        hb = h % 2
        KQ_, QQ_ = KQb[hb], QQb[hb]
        if h - 2 in attn_end:
            kb.wait("vector", "pesc", attn_end[h - 2][0])
            kb.wait("gpsimd", "pesc", attn_end[h - 2][0])
        kb.wait("tensor", "seln")
        proj_head(h, KQP2, WQs2, WKs2)
        for side in range(2):
            src_ = KQP2[0:64, :] if side == 0 else KQP2[64:128, :]
            dst_ = KQ_ if side == 0 else QQ_
            for c in range(8):
                vk = kb.dma("sync", rowst, stk_row(side, h, c), "sq")
                kb.wait("tensor", "sq", vk)
                kb.op("tensor", lambda e: e.matmul(out=PMI[0:64, 0:512], lhsT=onesb[0:1, 0:64],
                                                   rhs=rowst, start=True, stop=True), inc="pe")
                kb.wait("vector", "pe")
                kb.op("vector", lambda e, c=c, dst_=dst_, src_=src_: e.tensor_tensor(
                    out=dst_[0:64, 512 * c:512 * c + 512],
                    in0=src_[:, 512 * c:512 * c + 512], in1=PMI[0:64, 0:512],
                    op=OP.mult), inc="ve")
                kb.wait("tensor", "ve")
                kb.wait("sync", "ve")
        kv = None
        for r in range(3):
            kb.dma("gpsimd",
                   KQ_[64 + r:65 + r, :].rearrange("p (j c) -> p j c", c=512),
                   stkd[:, 2 + r, :].rearrange("(j h) c -> h j c", h=16)[h:h + 1, :, :], "gq")
            kv = kb.dma("gpsimd",
                        QQ_[64 + r:65 + r, :].rearrange("p (j c) -> p j c", c=512),
                        stkd[:, 5 + r, :].rearrange("(j h) c -> h j c", h=16)[h:h + 1, :, :], "gq")
        return kv

    def load_v(h):
        kb.wait("gpsimd", "peav")
        gv = kb.dma("gpsimd", VPh[:, :, :],
                    vpdb[:, 65 * h:65 * h + 65].rearrange("(tb p) c -> p tb c", p=128), "gq")
        return gv

    def attn_head(h, prep_mark, v_mark):
        hb = h % 2
        KQ_, QQ_ = KQb[hb], QQb[hb]
        kb.wait("tensor", "gq", prep_mark)
        kb.wait("vector", "gq", v_mark)
        if h - 1 in attn_end:
            kb.wait("vector", "peav", attn_end[h - 1][1])
        for tb in range(NT):
            kb.op("vector", lambda e, tb=tb, h=h: e.tensor_scalar(
                out=VPh[:, tb, 0:64], in0=VPh[:, tb, 0:64],
                scalar1=vstk[0:128, 16 * tb + h:16 * tb + h + 1], scalar2=None,
                op0=OP.mult), inc="vfv")
        kb.wait("tensor", "vfv")
        for lc in range(NL):
            for gi, (s0, s1) in enumerate(GRPS):
                psg = PSA if gi % 2 == 0 else PSB
                nst = s1 - s0
                for i in range(s0, s1):
                    j = i - s0
                    kb.op("tensor", lambda e, i=i, j=j, p=psg, KQ_=KQ_, QQ_=QQ_, lc=lc: e.matmul(
                        out=p[0:128, 512 * j:512 * j + 512],
                        lhsT=KQ_[0:67, 128 * i:128 * i + 128],
                        rhs=QQ_[0:67, 512 * lc:512 * lc + 512], start=True, stop=True),
                        inc="pesc" if i == s1 - 1 else None)
                kb.wait("scalar", "pesc")
                kb.op("scalar", lambda e, p=psg, nst=nst, gi=gi: e.activation(
                    out=LxB[gi % 2][:, 0:512 * nst], in_=p[0:128, 0:512 * nst],
                    func=AF.Ln), inc="seln")
                kb.wait("scalar", "peav", av_last[gi % 2])
                kb.op("scalar", lambda e, nst=nst, gi=gi: e.activation(
                    out=ATb[gi % 2][:, 0:512 * nst], in_=LxB[gi % 2][:, 0:512 * nst],
                    func=AF.Exp, scale=nalpha), inc="seexp")
                kb.wait("tensor", "seexp")
                for i in range(s0, s1):
                    j = i - s0
                    kb.op("tensor", lambda e, i=i, j=j, gi=gi: e.matmul(
                        out=PAV[0:65, 0:512], lhsT=VPh[:, i, 0:65],
                        rhs=ATb[gi % 2][:, 512 * j:512 * j + 512],
                        start=(i == 0), stop=(i == NS - 1)),
                        inc="peav" if i == s1 - 1 else None)
                av_last[gi % 2] = kb.cnt["peav"]
            kb.wait("vector", "peav")
            kb.op("vector", lambda e: e.reciprocal(out=dinv, in_=PAV[64:65, 0:512]), inc="ve")
            kb.wait("tensor", "ve")
            kb.op("tensor", lambda e: e.matmul(out=PMI[0:64, 0:512], lhsT=onesb[0:1, 0:64],
                                               rhs=dinv, start=True, stop=True), inc="pe")
            kb.wait("vector", "pe")
            kb.op("vector", lambda e: e.tensor_copy(out=dbc, in_=PMI[0:64, 0:512]), inc="ve")
            kb.op("vector", lambda e: e.tensor_tensor(out=OTS[0:64, :], in0=PAV[0:64, 0:512],
                                                      in1=dbc, op=OP.mult), inc="veot")
            kb.wait("tensor", "veot")
            kb.wait("sync", "veot")
            kb.dma("sync", otdb[h // 2, 64 * (h % 2):64 * (h % 2) + 64, 512 * lc:512 * lc + 512],
                   OTS[0:64, :], "sq")
            kb.wait("vector", "sq")
        attn_end[h] = (kb.cnt["pesc"], kb.cnt["peav"], kb.cnt["seexp"])

    pm = prep_head(0)
    vm = load_v(0)
    for h in range(H):
        if h + 1 < H:
            nxt = prep_head(h + 1)
        else:
            nxt = None
        attn_head(h, pm, vm)
        if h + 1 < H:
            vm = load_v(h + 1)
        pm = nxt
    kb.barrier()
    A16.reset(markA16)
    A32.reset(markA32)
    if dump == "otdb":
        return _dump([(out[128 * j:128 * j + 128, :], otdb[j, :, 0:1024], "gpsimd")
                      for j in range(8)])

    WOc = A16.t3(8, D)
    WBc = A16.t3(8, D)
    OTL = [A16.t3(8, 128) for _ in range(2)]
    XST = [A32.t2(D) for _ in range(2)]
    YB = A32.t2(D)
    X1B = A32.t2(D)
    SQy = A32.t2(D)
    browS = A32.ap[0:1, A32.cur:A32.cur + 2 * D]; A32.cur += 2 * D
    wrbsS = A32.t2(D)
    kb.dma("gpsimd", WOc[:, :, :], woT, "gq")
    kb.dma("gpsimd", WBc[:, :, :], wbT, "gq")
    kb.dma("sync", browS, brows, "sq")
    kb.dma("sync", wrbsS, wrb, "sq")
    kb.wait("tensor", "gq")
    kb.wait("tensor", "sq")

    for lb in range(NT):
        bb_ = lb % 2
        kb.wait("sync", "pe")
        kb.wait("sync", "ve")
        for j in range(8):
            kb.dma("sync", OTL[bb_][:, j, :], otdb[j, :, 128 * lb:128 * lb + 128], "sq")
        kb.dma("sync", XST[bb_], xrow[128 * lb:128 * lb + 128, :], "sq")
        kb.wait("tensor", "sq")
        for jc in range(2):
            for j in range(8):
                kb.op("tensor", lambda e, j=j, jc=jc, bb_=bb_: e.matmul(
                    out=PSA[0:128, 512 * jc:512 * jc + 512], lhsT=OTL[bb_][:, j, :],
                    rhs=WOc[:, j, 512 * jc:512 * jc + 512], start=(j == 0), stop=False))
            for kt in range(8):
                kb.op("tensor", lambda e, kt=kt, jc=jc, lb=lb: e.matmul(
                    out=PSA[0:128, 512 * jc:512 * jc + 512],
                    lhsT=XT[0:128, kt, 128 * lb:128 * lb + 128],
                    rhs=WBc[0:128, kt, 512 * jc:512 * jc + 512], start=False, stop=False))
            kb.op("tensor", lambda e, jc=jc: e.matmul(
                out=PSA[0:128, 512 * jc:512 * jc + 512], lhsT=onesb,
                rhs=browS[0:1, 512 * jc:512 * jc + 512], start=False, stop=False))
            kb.op("tensor", lambda e, jc=jc: e.matmul(
                out=PSA[0:128, 512 * jc:512 * jc + 512], lhsT=onesb,
                rhs=browS[0:1, D + 512 * jc:D + 512 * jc + 512], start=False, stop=True),
                inc="pe")
        kb.wait("vector", "pe")
        kb.op("vector", lambda e, bb_=bb_: e.scalar_tensor_tensor(
            out=YB, in0=XST[bb_], scalar=1.0, in1=PSA[0:128, 0:1024],
            op0=OP.mult, op1=OP.add), inc="ve")
        kb.wait("tensor", "ve")
        kb.op("vector", lambda e: e.tensor_reduce(out=stat[:, 0:1], in_=YB, axis=AX, op=OP.add))
        kb.op("vector", lambda e: e.tensor_tensor(out=SQy, in0=YB, in1=YB, op=OP.mult))
        kb.op("vector", lambda e: e.tensor_reduce(out=stat[:, 1:2], in_=SQy, axis=AX, op=OP.add))
        kb.op("vector", lambda e: e.tensor_scalar(out=stat[:, 2:3], in0=stat[:, 0:1],
                                                  scalar1=1.0 / D, scalar2=None, op0=OP.mult))
        kb.op("vector", lambda e: e.tensor_scalar(out=stat[:, 4:5], in0=stat[:, 1:2],
                                                  scalar1=1.0 / D, scalar2=None,
                                                  op0=OP.mult), inc="vch")
        kb.wait("vector", "vch")
        kb.op("vector", lambda e: e.tensor_tensor(out=stat[:, 3:4], in0=stat[:, 2:3],
                                                  in1=stat[:, 2:3], op=OP.mult), inc="vch")
        kb.wait("vector", "vch")
        v1 = kb.op("vector", lambda e: e.tensor_tensor(out=stat[:, 5:6], in0=stat[:, 4:5],
                                                       in1=stat[:, 3:4], op=OP.subtract), inc="ve")
        kb.wait("scalar", "ve", v1)
        kb.op("scalar", lambda e: e.activation(out=stat[:, 6:7], in_=stat[:, 5:6],
                                               func=AF.Ln, bias=epsln), inc="sch")
        kb.wait("scalar", "sch")
        s1_ = kb.op("scalar", lambda e: e.activation(out=stat[:, 7:8], in_=stat[:, 6:7],
                                                     func=AF.Exp, scale=-0.5), inc="se")
        kb.wait("vector", "se", s1_)
        kb.op("vector", lambda e: e.tensor_scalar(out=stat[:, 8:9], in0=stat[:, 2:3],
                                                  scalar1=-1.0, scalar2=None,
                                                  op0=OP.mult), inc="vch")
        kb.wait("vector", "vch")
        kb.op("vector", lambda e: e.tensor_scalar(out=X1B, in0=YB, scalar1=stat[:, 8:9],
                                                  scalar2=stat[:, 7:8], op0=OP.add, op1=OP.mult))
        kb.op("vector", lambda e: e.tensor_tensor(out=SQy, in0=X1B, in1=wrbsS, op=OP.mult))
        kb.op("vector", lambda e, lb=lb: e.tensor_reduce(out=sc32[:, lb:lb + 1], in_=SQy,
                                                         axis=AX, op=OP.add), inc="ve")
        kb.wait("sync", "ve")
        kb.dma("sync", x1d[128 * lb:128 * lb + 128, :], X1B, "sq")
        if stage == 1:
            kb.dma("sync", out[128 * lb:128 * lb + 128, :], X1B, "sq")
        kb.wait("vector", "sq")
    if stage == 1:
        _finalize(kb, nc)
        return nc
    kb.barrier()
    A16.reset(0)
    A32.reset(markA32)

    sbc_mark = A32.mark()
    SBC = A32.t2(T)
    ED = A32.t2(128)
    EM = A32.t2(128)

    def bcast_row(src32, dst, rd):
        kb.wait("sync", "ve")
        kb.dma("sync", rd[:, :], src32, "sq")
        kb.wait("sync", "sq")
        for c in range(8):
            kb.dma("sync", rowst.rearrange("r (a p) -> r a p", p=128),
                   rd.rearrange("p c -> () c p")[0:1, 4 * c:4 * c + 4, :], "bcr",
                   allow_slow_non_contiguous=True)
            kb.dma("sync", small[0:1, 15:16], curv, "bfl")
            kb.wait("tensor", "bfl")
            kb.op("tensor", lambda e: e.matmul(out=PMI[0:128, 0:512], lhsT=onesb,
                                               rhs=rowst, start=True, stop=True), inc="pe")
            kb.wait("vector", "pe")
            kb.op("vector", lambda e, c=c, dst=dst: e.tensor_copy(
                out=dst[:, 512 * c:512 * c + 512], in_=PMI[0:128, 0:512]), inc="ve")
            kb.wait("tensor", "ve")
            kb.wait("sync", "ve")

    bcast_row(sc32, SBC, rowd)
    if dump == "sbc":
        return _dump([(out[0:128, 0:32], sc32)] +
                     [(out[128 * (1 + k):128 * (2 + k), :], SBC[:, 1024 * k:1024 * k + 1024])
                      for k in range(4)])
    RTMP = A32.t2(T)
    for c in range(NT):
        kb.op("vector", lambda e, c=c: e.tensor_scalar(
            out=ED, in0=SBC[:, 128 * c:128 * c + 128], scalar1=sc32[:, c:c + 1],
            scalar2=None, op0=OP.is_equal))
        kb.op("vector", lambda e: e.tensor_tensor(out=EM, in0=ED, in1=lt, op=OP.mult))
        kb.op("vector", lambda e: e.tensor_reduce(out=stat[:, 2:3], in_=EM, axis=AX, op=OP.add))
        kb.op("vector", lambda e, c=c: e.tensor_scalar(
            out=RTMP, in0=SBC, scalar1=sc32[:, c:c + 1], scalar2=0.0,
            op0=OP.is_gt, op1=OP.add, accum_out=stat[:, 0:1]),
            inc="vch" if c == 0 else None)
        if c > 0:
            kb.op("vector", lambda e, c=c: e.tensor_scalar(
                out=RTMP[:, 0:128 * c], in0=SBC[:, 0:128 * c], scalar1=sc32[:, c:c + 1],
                scalar2=0.0, op0=OP.is_equal, op1=OP.add, accum_out=stat[:, 1:2]), inc="vch")
        kb.wait("vector", "vch")
        if c > 0:
            kb.op("vector", lambda e: e.tensor_tensor(out=stat[:, 0:1], in0=stat[:, 0:1],
                                                      in1=stat[:, 1:2], op=OP.add), inc="vch")
            kb.wait("vector", "vch")
        kb.op("vector", lambda e: e.tensor_tensor(out=stat[:, 0:1], in0=stat[:, 0:1],
                                                  in1=stat[:, 2:3], op=OP.add), inc="vch")
        kb.wait("vector", "vch")
        kb.op("vector", lambda e, c=c: e.tensor_copy(out=rk32[:, c:c + 1], in_=stat[:, 0:1]), inc="ve")
    A32.reset(sbc_mark)
    RBC = A32.t2(T)
    bcast_row(rk32, RBC, rowd2)
    if dump == "ranks":
        return _dump([(out[0:128, 0:32], rk32)] +
                     [(out[128 * (1 + k):128 * (2 + k), :], RBC[:, 1024 * k:1024 * k + 1024])
                      for k in range(4)])

    PTt = [A16.t2(CAP) for _ in range(2)]
    X1bf = [A16.t2(D) for _ in range(2)]
    X1fs = [A32.t2(D) for _ in range(2)]
    XSEL = A32.t3(4, D)
    GPS = [PSA[:, 0:512], PSA[:, 512:1024], PSA[:, 1024:1536], PSB[:, 0:512],
           PSB[:, 512:1024], PSB[:, 1024:1536], PAV[:, 0:512], PMI[:, 0:512]]
    gthm = {}
    for c in range(NT):
        cb = c % 2
        if c >= 2:
            kb.wait("vector", "pegth", gthm[c - 2])
        kb.wait("sync", "ve")
        kb.dma("sync", X1fs[cb], x1d[128 * c:128 * c + 128, :], "sq")
        kb.wait("vector", "sq")
        kb.op("vector", lambda e, cb=cb: e.tensor_copy(out=X1bf[cb], in_=X1fs[cb]))
        kb.op("vector", lambda e, c=c, cb=cb: e.tensor_scalar(
            out=PTt[cb], in0=io512, scalar1=rk32[:, c:c + 1], scalar2=None,
            op0=OP.is_equal), inc="ve")
        kb.wait("tensor", "ve")
        for rb in range(4):
            for ec in range(2):
                last = (rb == 3 and ec == 1)
                kb.op("tensor", lambda e, c=c, cb=cb, rb=rb, ec=ec: e.matmul(
                    out=GPS[2 * rb + ec][0:128, :], lhsT=PTt[cb][:, 128 * rb:128 * rb + 128],
                    rhs=X1bf[cb][:, 512 * ec:512 * ec + 512],
                    start=(c == 0), stop=(c == NT - 1)), inc="pegth" if last else None)
        gthm[c] = kb.cnt["pegth"]
    kb.wait("vector", "pegth", gthm[NT - 1])
    for rb in range(4):
        for ec in range(2):
            kb.op("vector", lambda e, rb=rb, ec=ec: e.tensor_copy(
                out=XSEL[:, rb, 512 * ec:512 * ec + 512], in_=GPS[2 * rb + ec][0:128, :]), inc="ve")
    kb.wait("tensor", "ve")

    H1F = A32.t3(4, D)
    XATT = A32.t3(4, D)
    SQyB = A32.t2(D)
    MSKS = A32.t2(512)
    H1T = A16.t3(9, CAP)
    QKT = A16.t3(16, CAP)
    VIB = A16.t3(4, H * 65)
    IAT = A16.t3(4, 512)
    OTi = [A16.t2(CAP) for _ in range(8)]
    WS1 = A16.t3(8, 512)
    WS2 = A16.t3(9, 512)
    A1B = A16.t2(512)
    ABT = A16.t3(22, CAP)
    DELB = A16.t3(4, D)
    MKB = A16.t3(4, 512)
    g1bS = A16.t2(D)
    g2bS = A16.t2(D)
    kb.dma("gpsimd", MKB[:, :, :], maskbd, "gq")
    kb.dma("gpsimd", g1bS, g1bi, "gq")
    kb.dma("gpsimd", g2bS, g2bi, "gq")

    def rmsnorm(dst3, src3, gs):
        for rt in range(4):
            kb.op("vector", lambda e, rt=rt: e.tensor_tensor(out=SQyB, in0=src3[:, rt, :],
                                                             in1=src3[:, rt, :], op=OP.mult))
            kb.op("vector", lambda e: e.tensor_reduce(out=stat[:, 0:1], in_=SQyB,
                                                      axis=AX, op=OP.add), inc="vch")
            kb.wait("vector", "vch")
            r = kb.op("vector", lambda e: e.tensor_scalar(out=stat[:, 1:2], in0=stat[:, 0:1],
                                                          scalar1=1.0 / D, scalar2=EPS_RMS,
                                                          op0=OP.mult, op1=OP.add), inc="ve")
            kb.wait("scalar", "ve", r)
            kb.op("scalar", lambda e: e.activation(out=stat[:, 2:3], in_=stat[:, 1:2],
                                                   func=AF.Ln), inc="sch")
            kb.wait("scalar", "sch")
            s2 = kb.op("scalar", lambda e: e.activation(out=stat[:, 3:4], in_=stat[:, 2:3],
                                                        func=AF.Exp, scale=-0.5), inc="se")
            kb.wait("vector", "se", s2)
            kb.op("vector", lambda e, rt=rt: e.tensor_scalar(out=SQyB, in0=src3[:, rt, :],
                                                             scalar1=stat[:, 3:4], scalar2=None,
                                                             op0=OP.mult))
            kb.op("vector", lambda e, rt=rt, dst3=dst3, gs=gs: e.tensor_tensor(
                out=dst3[:, rt, :], in0=SQyB, in1=gs, op=OP.mult), inc="ve")

    def transpose_to(dst3, src3, nkt):
        kb.wait("tensor", "ve")
        for rt in range(4):
            for eh in range(nkt):
                kb.op("tensor", lambda e, rt=rt, eh=eh: e.transpose(
                    out=PMI[0:128, 0:128], in_=src3[0:128, rt, 128 * eh:128 * eh + 128],
                    identity=eyeb), inc="pe")
                kb.wait("vector", "pe")
                kb.op("vector", lambda e, rt=rt, eh=eh: e.tensor_copy(
                    out=dst3[:, eh, 128 * rt:128 * rt + 128], in_=PMI[0:128, 0:128]), inc="ve")
                kb.wait("tensor", "ve")

    if dump == "x1d":
        return _dump([(out[:, :], x1d[:, :])])
    if dump == "xsel":
        return _dump([(out[128 * rt:128 * rt + 128, :], XSEL[:, rt, :]) for rt in range(4)])
    kb.wait("vector", "gq")
    rmsnorm(H1F, XSEL, g1bS)
    if dump == "h1":
        return _dump([(out[128 * rt:128 * rt + 128, :], H1F[:, rt, :]) for rt in range(4)])
    kb.op("gpsimd", lambda e: e.memset(H1T[0:1, 8, :], 1.0), inc="gp")
    kb.wait("tensor", "gp")
    transpose_to(H1T, H1F, 8)
    for dch in range(4):
        kb.wait("gpsimd", "pe")
        for kt in range(8):
            kb.dma("gpsimd", WS1[0:128, kt, :], wqkT[:, kt, 512 * dch:512 * dch + 512], "gq")
        kb.wait("tensor", "gq")
        for db in range(4):
            for kt in range(8):
                kb.op("tensor", lambda e, db=db, kt=kt: e.matmul(
                    out=PAV[0:128, 0:512], lhsT=WS1[0:128, kt, 128 * db:128 * db + 128],
                    rhs=H1T[:, kt, :], start=(kt == 0), stop=(kt == 7)),
                    inc="pe" if kt == 7 else None)
            kb.wait("vector", "pe")
            kb.op("vector", lambda e, dch=dch, db=db: e.tensor_copy(
                out=QKT[:, 4 * dch + db, :], in_=PAV[0:128, 0:512]), inc="ve")
            kb.wait("tensor", "ve")
    for ch in range(4):
        c0 = 260 * ch
        kb.wait("gpsimd", "pe")
        for kt in range(9):
            kb.dma("gpsimd", WS2[0:128, kt, 0:260], wvfT[:, kt, c0:c0 + 260], "gq")
        kb.wait("tensor", "gq")
        for rt in range(4):
            for kt in range(8):
                kb.op("tensor", lambda e, rt=rt, kt=kt: e.matmul(
                    out=PAV[0:128, 0:260], lhsT=H1T[:, kt, 128 * rt:128 * rt + 128],
                    rhs=WS2[0:128, kt, 0:260], start=(kt == 0), stop=False))
            kb.op("tensor", lambda e, rt=rt: e.matmul(
                out=PAV[0:128, 0:260], lhsT=H1T[0:1, 8, 128 * rt:128 * rt + 128],
                rhs=WS2[0:1, 8, 0:260], start=False, stop=True), inc="pe")
            kb.wait("vector", "pe")
            kb.op("vector", lambda e, rt=rt, c0=c0: e.tensor_copy(
                out=VIB[:, rt, c0:c0 + 260], in_=PAV[0:128, 0:260]), inc="ve")
            kb.wait("tensor", "ve")
    for h in range(H):
        qblk, qrow = h // 2, 64 * (h % 2)
        kblk, krow = 8 + h // 2, 64 * (h % 2)
        if h > 0:
            kb.wait("scalar", "peav")
        for st in range(4):
            psl = PSA[0:128, 512 * st:512 * st + 512] if st < 3 else PSB[0:128, 0:512]
            kb.op("tensor", lambda e, st=st, psl=psl, qblk=qblk, qrow=qrow, kblk=kblk, krow=krow: e.matmul(
                out=psl, lhsT=QKT[krow:krow + 64, kblk, 128 * st:128 * st + 128],
                rhs=QKT[qrow:qrow + 64, qblk, 0:512], start=True, stop=True), inc="pesc")
        kb.wait("vector", "pesc")
        for st in range(4):
            psl = PSA[0:128, 512 * st:512 * st + 512] if st < 3 else PSB[0:128, 0:512]
            if st > 0:
                kb.wait("vector", "seexp")
            vv = kb.op("vector", lambda e, st=st, psl=psl: e.scalar_tensor_tensor(
                out=MSKS, in0=psl, scalar=0.125, in1=MKB[:, st, :],
                op0=OP.mult, op1=OP.add), inc="ve")
            kb.wait("scalar", "ve", vv)
            kb.op("scalar", lambda e, st=st: e.activation(out=IAT[:, st, :], in_=MSKS,
                                                          func=AF.Exp), inc="seexp")
        kb.wait("tensor", "seexp")
        for st in range(4):
            kb.op("tensor", lambda e, st=st, h=h: e.matmul(
                out=PAV[0:65, 0:512], lhsT=VIB[:, st, 65 * h:65 * h + 65],
                rhs=IAT[:, st, :], start=(st == 0), stop=(st == 3)),
                inc="peav" if st == 3 else None)
        kb.wait("vector", "peav")
        kb.op("vector", lambda e: e.reciprocal(out=dinv, in_=PAV[64:65, 0:512]), inc="ve")
        kb.wait("tensor", "ve")
        kb.op("tensor", lambda e: e.matmul(out=PMI[0:64, 0:512], lhsT=onesb[0:1, 0:64],
                                           rhs=dinv, start=True, stop=True), inc="pe")
        kb.wait("vector", "pe")
        kb.op("vector", lambda e: e.tensor_copy(out=dbc, in_=PMI[0:64, 0:512]), inc="ve")
        kb.op("vector", lambda e, h=h: e.tensor_tensor(
            out=OTi[h // 2][64 * (h % 2):64 * (h % 2) + 64, :], in0=PAV[0:64, 0:512],
            in1=dbc, op=OP.mult), inc="veot")
        kb.wait("tensor", "veot")
    for jc in range(2):
        kb.wait("gpsimd", "pe")
        for kt in range(8):
            kb.dma("gpsimd", WS1[0:128, kt, :], woutT[:, kt, 512 * jc:512 * jc + 512], "gq")
        kb.wait("tensor", "gq")
        for rt in range(4):
            for j in range(8):
                kb.op("tensor", lambda e, rt=rt, j=j: e.matmul(
                    out=PAV[0:128, 0:512], lhsT=OTi[j][:, 128 * rt:128 * rt + 128],
                    rhs=WS1[0:128, j, :], start=(j == 0), stop=(j == 7)),
                    inc="pe" if j == 7 else None)
            kb.wait("vector", "pe")
            kb.op("vector", lambda e, rt=rt, jc=jc: e.tensor_copy(
                out=XATT[:, rt, 512 * jc:512 * jc + 512], in_=PAV[0:128, 0:512]), inc="ve")
            kb.wait("tensor", "ve")
    if dump == "xattn":
        return _dump([(out[128 * rt:128 * rt + 128, :], XATT[:, rt, :]) for rt in range(4)])
    for rt in range(4):
        kb.op("vector", lambda e, rt=rt: e.tensor_tensor(out=XATT[:, rt, :], in0=XATT[:, rt, :],
                                                         in1=XSEL[:, rt, :], op=OP.add), inc="ve")
    rmsnorm(H1F, XATT, g2bS)
    if dump == "h2":
        return _dump([(out[128 * rt:128 * rt + 128, :], H1F[:, rt, :]) for rt in range(4)])
    transpose_to(H1T, H1F, 8)
    ABQ = H1F
    FCH = [(0, 1024), (1024, 2048), (2048, 2730)]
    for f0, f1 in FCH:
        for sub in range((f1 - f0 + 511) // 512):
            s0f = f0 + 512 * sub
            nn = min(512, f1 - s0f)
            kb.wait("gpsimd", "pe")
            for kt in range(8):
                kb.dma("gpsimd", WS1[0:128, kt, 0:nn], w1T[:, kt, s0f:s0f + nn], "gq")
                kb.dma("gpsimd", WS2[0:128, kt, 0:nn], w2T[:, kt, s0f:s0f + nn], "gq")
            kb.wait("tensor", "gq")
            for rt in range(4):
                for kt in range(8):
                    kb.op("tensor", lambda e, rt=rt, kt=kt, nn=nn: e.matmul(
                        out=PAV[0:128, 0:nn], lhsT=H1T[:, kt, 128 * rt:128 * rt + 128],
                        rhs=WS1[0:128, kt, 0:nn], start=(kt == 0), stop=(kt == 7)))
                for kt in range(8):
                    kb.op("tensor", lambda e, rt=rt, kt=kt, nn=nn: e.matmul(
                        out=PMI[0:128, 0:nn], lhsT=H1T[:, kt, 128 * rt:128 * rt + 128],
                        rhs=WS2[0:128, kt, 0:nn], start=(kt == 0), stop=(kt == 7)),
                        inc="pe" if kt == 7 else None)
                kb.wait("scalar", "pe")
                kb.op("scalar", lambda e, nn=nn: e.activation(out=A1B[:, 0:nn],
                                                              in_=PAV[0:128, 0:nn],
                                                              func=AF.Silu), inc="se")
                kb.wait("vector", "se")
                kb.op("vector", lambda e, rt=rt, nn=nn, s0f=s0f, f0=f0: e.tensor_tensor(
                    out=ABQ[:, rt, s0f - f0:s0f - f0 + nn], in0=A1B[:, 0:nn],
                    in1=PMI[0:128, 0:nn], op=OP.mult), inc="ve")
                kb.wait("tensor", "ve")
        for rt in range(4):
            for eh in range((f1 - f0 + 127) // 128):
                gkt = (f0 + 128 * eh) // 128
                ew = min(128, f1 - f0 - 128 * eh)
                kb.op("tensor", lambda e, rt=rt, eh=eh, ew=ew: e.transpose(
                    out=PMI[0:ew, 0:128], in_=ABQ[0:128, rt, 128 * eh:128 * eh + ew],
                    identity=eyeb), inc="pe")
                kb.wait("vector", "pe")
                kb.op("vector", lambda e, rt=rt, gkt=gkt, ew=ew: e.tensor_copy(
                    out=ABT[0:ew, gkt, 128 * rt:128 * rt + 128], in_=PMI[0:ew, 0:128]), inc="ve")
                kb.wait("tensor", "ve")
    w3_mark = A16.mark()
    W3s = [A16.t2(D) for _ in range(2)]
    xfm = {}
    for kt in range(22):
        kw = min(128, DFF - 128 * kt)
        if kt >= 2:
            kb.wait("gpsimd", "pexf", xfm[kt - 2])
        kb.dma("gpsimd", W3s[kt % 2], w3T[:, kt, :], "gq")
        kb.wait("tensor", "gq")
        for rt in range(4):
            for jc in range(2):
                last = (rt == 3 and jc == 1)
                kb.op("tensor", lambda e, rt=rt, jc=jc, kt=kt, kw=kw: e.matmul(
                    out=GPS[2 * rt + jc][0:128, :], lhsT=ABT[0:kw, kt, 128 * rt:128 * rt + 128],
                    rhs=W3s[kt % 2][0:kw, 512 * jc:512 * jc + 512],
                    start=(kt == 0), stop=(kt == 21)), inc="pexf" if last else None)
        xfm[kt] = kb.cnt["pexf"]
    kb.wait("vector", "pexf", xfm[21])
    for rt in range(4):
        kb.op("vector", lambda e, rt=rt: e.tensor_tensor(
            out=XATT[:, rt, :], in0=XATT[:, rt, :], in1=XSEL[:, rt, :], op=OP.subtract))
        kb.op("vector", lambda e, rt=rt: e.scalar_tensor_tensor(
            out=XATT[:, rt, 0:512], in0=XATT[:, rt, 0:512], scalar=1.0,
            in1=GPS[2 * rt + 0][0:128, :], op0=OP.mult, op1=OP.add))
        kb.op("vector", lambda e, rt=rt: e.scalar_tensor_tensor(
            out=XATT[:, rt, 512:1024], in0=XATT[:, rt, 512:1024], scalar=1.0,
            in1=GPS[2 * rt + 1][0:128, :], op0=OP.mult, op1=OP.add))
        kb.op("vector", lambda e, rt=rt: e.tensor_copy(out=DELB[:, rt, :], in_=XATT[:, rt, :]), inc="ve")
    kb.wait("tensor", "ve")
    if dump == "delb":
        return _dump([(out[128 * rt:128 * rt + 128, :], XATT[:, rt, :]) for rt in range(4)])
    if dump == "qkt":
        return _dump([(out[128 * i:128 * i + 128, 0:512], QKT[:, i, :], "gpsimd")
                      for i in range(16)])

    A16.reset(w3_mark)
    PRB = A16.t3(4, 128)
    Y2B = A32.t2(D)
    for li in range(NT):
        lb = li
        cb = li % 2
        kb.wait("sync", "ve")
        kb.dma("sync", X1fs[cb], x1d[128 * lb:128 * lb + 128, :], "sq")
        kb.wait("vector", "pe")
        for rb in range(4):
            kb.op("vector", lambda e, lb=lb, rb=rb: e.tensor_scalar(
                out=PRB[:, rb, :], in0=RBC[:, 128 * lb:128 * lb + 128],
                scalar1=iorb[:, rb:rb + 1], scalar2=None, op0=OP.is_equal), inc="ve")
        kb.wait("tensor", "ve")
        for jc in range(2):
            for rb in range(4):
                kb.op("tensor", lambda e, rb=rb, jc=jc: e.matmul(
                    out=PAV[0:128, 0:512], lhsT=PRB[:, rb, :],
                    rhs=DELB[0:128, rb, 512 * jc:512 * jc + 512],
                    start=(rb == 0), stop=(rb == 3)), inc="pe" if rb == 3 else None)
            kb.wait("vector", "pe")
            kb.wait("vector", "sq")
            kb.op("vector", lambda e, jc=jc, cb=cb: e.scalar_tensor_tensor(
                out=Y2B[:, 512 * jc:512 * jc + 512], in0=X1fs[cb][:, 512 * jc:512 * jc + 512],
                scalar=2.0, in1=PAV[0:128, 0:512], op0=OP.mult, op1=OP.add), inc="ve")
            kb.wait("tensor", "ve")
        kb.op("vector", lambda e: e.tensor_reduce(out=stat[:, 0:1], in_=Y2B, axis=AX, op=OP.add))
        kb.op("vector", lambda e: e.tensor_tensor(out=SQyB, in0=Y2B, in1=Y2B, op=OP.mult))
        kb.op("vector", lambda e: e.tensor_reduce(out=stat[:, 1:2], in_=SQyB, axis=AX, op=OP.add))
        kb.op("vector", lambda e: e.tensor_scalar(out=stat[:, 2:3], in0=stat[:, 0:1],
                                                  scalar1=1.0 / D, scalar2=None, op0=OP.mult))
        kb.op("vector", lambda e: e.tensor_scalar(out=stat[:, 4:5], in0=stat[:, 1:2],
                                                  scalar1=1.0 / D, scalar2=None,
                                                  op0=OP.mult), inc="vch")
        kb.wait("vector", "vch")
        kb.op("vector", lambda e: e.tensor_tensor(out=stat[:, 3:4], in0=stat[:, 2:3],
                                                  in1=stat[:, 2:3], op=OP.mult), inc="vch")
        kb.wait("vector", "vch")
        v1 = kb.op("vector", lambda e: e.tensor_tensor(out=stat[:, 5:6], in0=stat[:, 4:5],
                                                       in1=stat[:, 3:4], op=OP.subtract), inc="ve")
        kb.wait("scalar", "ve", v1)
        kb.op("scalar", lambda e: e.activation(out=stat[:, 6:7], in_=stat[:, 5:6],
                                               func=AF.Ln, bias=epsln), inc="sch")
        kb.wait("scalar", "sch")
        s1_ = kb.op("scalar", lambda e: e.activation(out=stat[:, 7:8], in_=stat[:, 6:7],
                                                     func=AF.Exp, scale=-0.5), inc="se")
        kb.wait("vector", "se", s1_)
        kb.op("vector", lambda e: e.tensor_scalar(out=stat[:, 8:9], in0=stat[:, 2:3],
                                                  scalar1=-1.0, scalar2=None,
                                                  op0=OP.mult), inc="vch")
        kb.wait("vector", "vch")
        kb.op("vector", lambda e: e.tensor_scalar(out=Y2B, in0=Y2B, scalar1=stat[:, 8:9],
                                                  scalar2=stat[:, 7:8], op0=OP.add, op1=OP.mult), inc="ve")
        kb.wait("sync", "ve")
        kb.dma("sync", out[128 * li:128 * li + 128, :], Y2B, "stq")
        kb.wait("vector", "stq")

    _finalize(kb, nc)
    return nc


def _finalize(kb, nc):
    kb.wait("sync", "stq") if "stq" in kb.cnt else None
    kb.wait("sync", "sq")
    with nc.Block() as block:
        @block.sync
        def _(e):
            for f in kb.ops["sync"]:
                f(e)

        @block.gpsimd
        def _(e):
            for f in kb.ops["gpsimd"]:
                f(e)

        @block.tensor
        def _(e):
            for f in kb.ops["tensor"]:
                f(e)

        @block.vector
        def _(e):
            for f in kb.ops["vector"]:
                f(e)

        @block.scalar
        def _(e):
            for f in kb.ops["scalar"]:
                f(e)


def _wtiles(aT, ktiles):
    K, N = aT.shape
    assert K <= 128 * ktiles
    buf = np.zeros((128 * ktiles, N), dtype=np.float32)
    buf[:K] = aT
    return np.ascontiguousarray(buf.reshape(ktiles, 128, N).transpose(1, 0, 2))


def _prep_inputs(inp, b):
    x = np.ascontiguousarray(inp["L0_attn_query"][b]).astype(np.float32, copy=False)
    d = {}
    d["xrow"] = x
    d["xtT"] = _wtiles(np.ascontiguousarray(x.T), 8)
    d["wqT"] = _wtiles(np.ascontiguousarray(inp["L0_attn_Wq"].T), 8)
    d["wkT"] = _wtiles(np.ascontiguousarray(inp["L0_attn_Wk"].T), 8)
    wv = np.zeros((1152, H * 65), dtype=np.float32)
    wvf = np.zeros((1152, H * 65), dtype=np.float32)
    WvT = inp["L0_attn_Wv"].T
    Wqkv = inp["L0_ffn_W_qkv"]
    for h in range(H):
        wv[0:1024, 65 * h:65 * h + 64] = WvT[:, 64 * h:64 * h + 64]
        wv[1024, 65 * h:65 * h + 64] = inp["L0_attn_bv"][64 * h:64 * h + 64]
        wv[1024, 65 * h + 64] = 1.0
        wvf[0:1024, 65 * h:65 * h + 64] = Wqkv[2 * D + 64 * h:2 * D + 64 * h + 64, :].T
        wvf[1024, 65 * h + 64] = 1.0
    d["wvT"] = np.ascontiguousarray(wv.reshape(9, 128, H * 65).transpose(1, 0, 2))
    d["wvfT"] = np.ascontiguousarray(wvf.reshape(9, 128, H * 65).transpose(1, 0, 2))
    d["woT"] = _wtiles(np.ascontiguousarray(inp["L0_attn_Wo"].T), 8)
    d["wbT"] = _wtiles(np.ascontiguousarray(inp["L0_attn_Wb"].T), 8)
    d["wqkT"] = _wtiles(np.ascontiguousarray(Wqkv[0:2 * D, :].T), 8)
    d["woutT"] = _wtiles(np.ascontiguousarray(inp["L0_ffn_W_out"].T), 8)
    d["w1T"] = _wtiles(np.ascontiguousarray(inp["L0_ffn_W1"].T), 8)
    d["w2T"] = _wtiles(np.ascontiguousarray(inp["L0_ffn_W2"].T), 8)
    d["w3T"] = _wtiles(np.ascontiguousarray(inp["L0_ffn_W3"].T), 22)
    d["bqt"] = np.ascontiguousarray(inp["L0_attn_bq"].reshape(8, 128).T)
    d["bkt"] = np.ascontiguousarray(inp["L0_attn_bk"].reshape(8, 128).T)
    d["brows"] = np.concatenate([inp["L0_attn_bo"], inp["L0_attn_bb"]]).reshape(1, 2 * D)
    d["wrb"] = np.ascontiguousarray(np.broadcast_to(inp["L0_ffn_w_router"], (128, D)))
    d["g1bi"] = np.ascontiguousarray(np.broadcast_to(inp["L0_ffn_g1"], (128, D)))
    d["g2bi"] = np.ascontiguousarray(np.broadcast_to(inp["L0_ffn_g2"], (128, D)))
    d["rwrep"] = np.ascontiguousarray(
        inp["L0_attn_radial_weights"].reshape(H)[np.arange(128) % H]).reshape(128, 1)
    d["curv"] = np.asarray(inp["L0_attn_curvature"], dtype=np.float32).reshape(1, 1)
    p = np.arange(128)
    s = np.arange(128)
    d["lt128"] = (s[None, :] < p[:, None]).astype(np.float32)
    d["iota512b"] = np.ascontiguousarray(
        np.broadcast_to(np.arange(512, dtype=np.float32), (128, 512)))
    d["iotarb"] = (p[:, None] + 128 * np.arange(4)[None, :]).astype(np.float32)
    mb = np.zeros((128, 4, 512), dtype=np.float32)
    for st in range(4):
        sglob = 128 * st + p
        mb[:, st, :] = np.where(sglob[:, None] <= np.arange(512)[None, :], 0.0, -1e30)
    d["maskbd"] = mb
    d["onesrow"] = np.ones((1, 128), dtype=np.float32)
    d["onescol"] = np.ones((128, 1), dtype=np.float32)
    d["eyef"] = np.eye(128, dtype=np.float32)
    for k in d:
        d[k] = np.ascontiguousarray(d[k], dtype=np.float32)
    return d


def kernel(**inputs):
    nc = build(2)
    in_maps = [_prep_inputs(inputs, c % 4) for c in range(8)]
    res = run_bass_kernel_spmd(nc, in_maps, core_ids=list(range(8)))
    outp = np.empty((B, T, D), dtype=np.float32)
    for b_ in range(4):
        outp[b_] = res.results[b_]["out"]
    return outp


# version 06
# speedup vs baseline: 1.5619x; 1.3288x over previous
import contextlib
import numpy as np
import ml_dtypes
import concourse.bass as bass
import concourse.mybir as mybir
from concourse.bass_utils import run_bass_kernel_spmd
from concourse.alu_op_type import AluOpType as OP

F32 = mybir.dt.float32
BF16 = mybir.dt.bfloat16
AF = mybir.ActivationFunctionType
AX = mybir.AxisListType.X

B, T, D = 4, 4096, 1024
H, CAP, DFF = 16, 512, 2730
EPS_H, EPS_LN, EPS_RMS = 1e-8, 1e-5, 1e-6
NT, NL, NS = 32, 8, 32
GRPS = [(3 * i, min(3 * i + 3, NS)) for i in range(11)]
ENGINES = ("sync", "gpsimd", "tensor", "vector", "scalar")


class KB:
    def __init__(self, nc, es):
        self.nc, self.es = nc, es
        self.ops = {e: [] for e in ENGINES}
        self.cnt = {}
        self.sems = {}

    def sem(self, name):
        if name not in self.sems:
            self.sems[name] = self.es.enter_context(self.nc.semaphore(name))
            self.cnt[name] = 0
        return self.sems[name]

    def op(self, eng, fn, inc=None):
        if inc is None:
            self.ops[eng].append(fn)
            return None
        s = self.sem(inc)
        self.cnt[inc] += 1
        self.ops[eng].append(lambda e, f=fn, s=s: f(e).then_inc(s, 1))
        return self.cnt[inc]

    def dma(self, eng, out_ap, in_ap, inc, **kw):
        s = self.sem(inc)
        self.cnt[inc] += 16
        self.ops[eng].append(
            lambda e, o=out_ap, i=in_ap, s=s, kw=kw: e.dma_start(out=o, in_=i, **kw).then_inc(s, 16))
        return self.cnt[inc]

    def wait(self, eng, sem, val=None):
        s = self.sem(sem)
        v = self.cnt.get(sem, 0) if val is None else val
        if v <= 0:
            return 0
        self.ops[eng].append(lambda e, s=s, v=v: e.wait_ge(s, v))
        return v

    def barrier(self):
        marks = []
        for eng in ENGINES:
            sm = "bar_" + eng
            marks.append((sm, self.op(eng, lambda e: e.nop(), inc=sm)))
        for eng in ENGINES:
            for sm, v in marks:
                self.wait(eng, sm, v)


class Arena:
    def __init__(self, ap, total):
        self.ap, self.total, self.cur = ap, total, 0

    def mark(self):
        return self.cur

    def reset(self, m):
        self.cur = m

    def t2(self, n):
        assert self.cur + n <= self.total, f"arena overflow {self.cur}+{n}>{self.total}"
        off = self.cur
        self.cur += n
        return self.ap[:, off:off + n]

    def t3(self, a, b):
        return self.t2(a * b).rearrange("p (a b) -> p a b", a=a)


def build(stage, dump=None):
    es = contextlib.ExitStack()
    nc = bass.Bass("TRN2", target_bir_lowering=False)
    kb = KB(nc, es)

    def _dump(pairs):
        for p in pairs:
            o, i = p[0], p[1]
            eng = p[2] if len(p) > 2 else "sync"
            kb.dma(eng, o, i, "stq")
        kb.wait("sync", "stq")
        kb.wait("gpsimd", "stq")
        _finalize(kb, nc)
        return nc

    def din(name, shape):
        return nc.dram_tensor(name, list(shape), F32, kind="ExternalInput").ap()

    xrow = din("xrow", (T, D))
    xtT = din("xtT", (128, 8, T))
    wqT = din("wqT", (128, 8, D))
    wkT = din("wkT", (128, 8, D))
    wvT = din("wvT", (128, 9, H * 65))
    woT = din("woT", (128, 8, D))
    wbT = din("wbT", (128, 8, D))
    wqkT = din("wqkT", (128, 8, 2 * D))
    wvfT = din("wvfT", (128, 9, H * 65))
    woutT = din("woutT", (128, 8, D))
    w1T = din("w1T", (128, 8, DFF))
    w2T = din("w2T", (128, 8, DFF))
    w3T = din("w3T", (128, 22, D))
    bqt = din("bqt", (128, 8))
    bkt = din("bkt", (128, 8))
    brows = din("brows", (1, 2 * D))
    wrb = din("wrb", (128, D))
    g1bi = din("g1bi", (128, D))
    g2bi = din("g2bi", (128, D))
    rwrep = din("rwrep", (128, 1))
    curv = din("curv", (1, 1))
    lt128 = din("lt128", (128, 128))
    iota512b = din("iota512b", (128, 512))
    iotarb = din("iotarb", (128, 4))
    maskbd = din("maskbd", (128, 4, 512))
    onesrow = din("onesrow", (1, 128))
    onescol = din("onescol", (128, 1))
    eyef = din("eyef", (128, 128))

    out = nc.dram_tensor("out", [T, D], F32, kind="ExternalOutput").ap()

    x1d = nc.dram_tensor("x1d", [T, D], F32).ap()
    vpdb = nc.dram_tensor("vpdb", [T, H * 65], BF16).ap()
    otdb = nc.dram_tensor("otdb", [8, 128, T], BF16).ap()
    stkd = nc.dram_tensor("stkd", [128, 8, 512], F32).ap()
    rowd = nc.dram_tensor("rowd", [128, 32], F32).ap()
    rowd2 = nc.dram_tensor("rowd2", [128, 32], F32).ap()

    def sb(name, shape, dt):
        return es.enter_context(nc.sbuf_tensor(name, list(shape), dt))[:]

    onesb = sb("onesb", (1, 128), F32)
    onesbf = sb("onesbf", (1, 128), BF16)
    onesc = sb("onesc", (128, 1), F32)
    eyeb = sb("eyeb", (128, 128), F32)
    lt = sb("lt", (128, 128), F32)
    io512 = sb("io512", (128, 512), F32)
    iorb = sb("iorb", (128, 4), F32)
    bqs = sb("bqs", (128, 8), F32)
    bks = sb("bks", (128, 8), F32)
    radl = sb("radl", (128, 1), F32)
    nalpha = sb("nalpha", (128, 1), F32)
    small = sb("small", (1, 16), F32)
    stat = sb("stat", (128, 16), F32)
    epsln = sb("epsln", (128, 1), F32)
    dinv = sb("dinv", (1, 512), F32)
    dbc = sb("dbc", (64, 512), F32)
    vstk = sb("vstk", (128, 512), F32)
    sc32 = sb("sc32", (128, 32), F32)
    rk32 = sb("rk32", (128, 32), F32)
    rowst = sb("rowst", (1, 512), F32)

    A16 = Arena(sb("A16", (128, 57344), BF16), 57344)
    A32 = Arena(sb("A32", (128, 20992), F32), 20992)

    PSA = es.enter_context(nc.psum_tensor("PSA", [128, 1536], F32))[:]
    PSB = es.enter_context(nc.psum_tensor("PSB", [128, 1536], F32))[:]
    PAV = es.enter_context(nc.psum_tensor("PAV", [128, 512], F32))[:]
    PMI = es.enter_context(nc.psum_tensor("PMI", [128, 512], F32))[:]
    PROJ_PS = [PSA[:, 0:512], PSB[:, 0:512], PSA[:, 512:1024], PSB[:, 512:1024]]
    NRM_PS = [PMI[0:1, 0:512], PAV[0:1, 0:512], PSA[0:1, 1024:1536], PSB[0:1, 1024:1536]]

    for o, i in [(onesb, onesrow), (onesc, onescol), (eyeb, eyef), (lt, lt128),
                 (io512, iota512b), (iorb, iotarb),
                 (bqs, bqt), (bks, bkt), (radl, rwrep), (small[0:1, 0:1], curv)]:
        kb.dma("sync", o, i, "dq")
    kb.dma("gpsimd", onesbf, onesrow, "gq")

    XT = A16.t3(8, T)
    markA16 = A16.mark()
    markA32 = A32.mark()
    kb.dma("gpsimd", XT[:, :, :], xtT, "gq")
    kb.op("gpsimd", lambda e: e.memset(epsln, EPS_LN), inc="gp")

    kb.wait("vector", "dq")
    kb.wait("vector", "gp")
    kb.op("vector", lambda e: e.tensor_scalar(out=small[0:1, 1:2], in0=small[0:1, 0:1],
                                              scalar1=-1.0, scalar2=None, op0=OP.mult), inc="vch")
    kb.wait("vector", "vch")
    n = kb.op("vector", lambda e: e.tensor_scalar(out=small[0:1, 2:3], in0=small[0:1, 0:1],
                                                  scalar1=small[0:1, 1:2], scalar2=EPS_H,
                                                  op0=OP.max, op1=OP.max), inc="ve")
    kb.wait("scalar", "ve", n)
    kb.op("scalar", lambda e: e.activation(out=small[0:1, 4:5], in_=small[0:1, 2:3],
                                           func=AF.Ln), inc="sch")
    kb.wait("scalar", "sch")
    m = kb.op("scalar", lambda e: e.activation(out=small[0:1, 5:6], in_=small[0:1, 4:5],
                                               func=AF.Exp, scale=0.5), inc="se")
    kb.wait("vector", "se", m)
    kb.op("vector", lambda e: e.tensor_scalar(out=small[0:1, 6:7], in0=small[0:1, 5:6],
                                              scalar1=EPS_H, scalar2=8.0, op0=OP.add,
                                              op1=OP.mult), inc="vch")
    kb.wait("vector", "vch")
    kb.op("vector", lambda e: e.reciprocal(out=small[0:1, 7:8], in_=small[0:1, 6:7]), inc="vch")
    kb.wait("vector", "vch")
    n2 = kb.op("vector", lambda e: e.tensor_scalar(out=small[0:1, 8:9], in0=small[0:1, 7:8],
                                                   scalar1=-1.0, scalar2=None, op0=OP.mult), inc="ve")
    kb.wait("tensor", "ve", n2)
    k1 = kb.op("tensor", lambda e: e.matmul(out=PMI[:, 0:1], lhsT=onesb, rhs=small[0:1, 8:9],
                                            start=True, stop=True), inc="pe")
    kb.wait("vector", "pe", k1)
    kb.op("vector", lambda e: e.tensor_copy(out=nalpha, in_=PMI[:, 0:1]), inc="ve")
    ms = kb.op("scalar", lambda e: e.activation(out=radl, in_=radl, func=AF.Exp, scale=-1.0), inc="se")
    kb.wait("vector", "se", ms)
    kb.op("vector", lambda e: e.tensor_scalar(out=radl, in0=radl, scalar1=1.0,
                                              scalar2=None, op0=OP.add), inc="vch")
    kb.wait("vector", "vch")
    kb.op("vector", lambda e: e.reciprocal(out=radl, in_=radl), inc="ve")
    kb.barrier()
    if dump == "smalls":
        return _dump([(out[0:1, 0:16], small), (out[1:2, 0:1], nalpha[0:1, :]),
                      (out[2:3, 0:1], radl[0:1, :])])

    def f_chain(dst, src, sa, sbuf, mul_radial=False):
        kb.wait("scalar", "ve")
        kb.op("scalar", lambda e: e.activation(out=sa, in_=src, func=AF.Ln))
        kb.op("scalar", lambda e: e.activation(out=sa, in_=sa, func=AF.Exp, scale=0.5))
        q = kb.op("scalar", lambda e: e.activation(out=sbuf, in_=sa, func=AF.Exp, scale=2.0), inc="se")
        kb.wait("vector", "se", q)
        kb.op("vector", lambda e: e.tensor_scalar(out=sbuf, in0=sbuf, scalar1=1.0,
                                                  scalar2=None, op0=OP.add))
        kb.op("vector", lambda e: e.reciprocal(out=sbuf, in_=sbuf), inc="vch")
        kb.wait("vector", "vch")
        kb.op("vector", lambda e: e.tensor_scalar(out=sbuf, in0=sbuf, scalar1=-2.0,
                                                  scalar2=1.0, op0=OP.mult, op1=OP.add))
        kb.op("vector", lambda e: e.tensor_scalar(out=sa, in0=sa, scalar1=EPS_H,
                                                  scalar2=None, op0=OP.add))
        kb.op("vector", lambda e: e.reciprocal(out=sa, in_=sa), inc="vch")
        kb.wait("vector", "vch")
        if mul_radial:
            kb.op("vector", lambda e: e.tensor_scalar(out=sa, in0=sa, scalar1=radl,
                                                      scalar2=None, op0=OP.mult))
        return kb.op("vector", lambda e: e.tensor_tensor(out=dst, in0=sa, in1=sbuf,
                                                         op=OP.mult), inc="ve")

    mk16 = A16.mark(); mk32 = A32.mark()
    WVs = A16.t3(9, 260)
    VEVB = A16.t2(260)
    VEV = A32.t2(260)
    SQ5 = A32.t2(260)
    for ch in range(4):
        c0 = 260 * ch
        kb.wait("gpsimd", "pe")
        for kt in range(9):
            kb.dma("gpsimd", WVs[0:128, kt, :], wvT[:, kt, c0:c0 + 260], "gq")
        kb.wait("tensor", "gq")
        for tb in range(NT):
            pscur = PSA if tb % 2 == 0 else PSB
            for kt in range(8):
                kb.op("tensor", lambda e, kt=kt, tb=tb, p=pscur: e.matmul(
                    out=p[0:128, 0:260], lhsT=XT[0:128, kt, 128 * tb:128 * tb + 128],
                    rhs=WVs[0:128, kt, :], start=(kt == 0), stop=False))
            kb.op("tensor", lambda e, p=pscur: e.matmul(
                out=p[0:128, 0:260], lhsT=onesbf, rhs=WVs[0:1, 8, :],
                start=False, stop=True), inc="pe")
            kb.wait("vector", "pe")
            kb.op("vector", lambda e, p=pscur: e.tensor_copy(out=VEV, in_=p[0:128, 0:260]), inc="ve")
            kb.wait("tensor", "ve")
            kb.op("vector", lambda e: e.tensor_tensor(out=SQ5, in0=VEV, in1=VEV, op=OP.mult))
            for hl in range(4):
                h = 4 * ch + hl
                kb.op("vector", lambda e, hl=hl, tb=tb, h=h: e.tensor_reduce(
                    out=vstk[0:128, 16 * tb + h:16 * tb + h + 1],
                    in_=SQ5[:, 65 * hl:65 * hl + 64], axis=AX, op=OP.add))
            kb.op("vector", lambda e: e.tensor_copy(out=VEVB, in_=VEV), inc="ve")
            kb.wait("sync", "ve")
            kb.dma("sync", vpdb[128 * tb:128 * tb + 128, c0:c0 + 260], VEVB, "sq")
            kb.wait("vector", "sq")
    A16.reset(mk16)
    FCA = A32.t2(512)
    FCB = A32.t2(512)
    kb.wait("scalar", "ve")
    f_chain(vstk, vstk, FCA, FCB)
    A32.reset(mk32)
    kb.barrier()
    if dump == "vstk":
        return _dump([(out[0:128, 0:512], vstk)])
    if dump == "vpdb":
        return _dump([(out[:, :], vpdb[:, 0:1024], "gpsimd")])

    mk32 = A32.mark(); mk16 = A16.mark()
    KQP = A32.t2(T)
    KQPc = KQP.rearrange("p (c n) -> p c n", n=512)
    SQB = A32.t2(T)
    SQBc = SQB.rearrange("p (c n) -> p c n", n=512)
    nstk = A32.t3(2, 512)
    NST = A32.t3(2, 512)
    WQs = A16.t3(8, 64)
    WKs = A16.t3(8, 64)

    def proj_head(h, dst, WQb, WKb):
        kb.wait("gpsimd", "pe")
        for kt in range(8):
            kb.dma("gpsimd", WQb[0:128, kt, :], wqT[:, kt, 64 * h:64 * h + 64], "gq")
            kb.dma("gpsimd", WKb[0:128, kt, :], wkT[:, kt, 64 * h:64 * h + 64], "gq")
        kb.wait("tensor", "gq")
        for c in range(8):
            for qk in range(2):
                W = WKb if qk == 0 else WQb
                pbank = PROJ_PS[(2 * c + qk) % 4]
                for kt in range(8):
                    kb.op("tensor", lambda e, W=W, kt=kt, c=c, p=pbank: e.matmul(
                        out=p[0:64, 0:512], lhsT=W[0:128, kt, :],
                        rhs=XT[0:128, kt, 512 * c:512 * c + 512],
                        start=(kt == 0), stop=(kt == 7)), inc="pe" if kt == 7 else None)
                kb.wait("vector", "pe")
                bias = (bks if qk == 0 else bqs)[64 * (h % 2):64 * (h % 2) + 64, h // 2:h // 2 + 1]
                r0 = 0 if qk == 0 else 64
                kb.op("vector", lambda e, p=pbank, bias=bias, r0=r0, c=c, dst=dst: e.tensor_scalar(
                    out=dst[r0:r0 + 64, 512 * c:512 * c + 512], in0=p[0:64, 0:512],
                    scalar1=bias, scalar2=None, op0=OP.add), inc="ve")
        kb.wait("tensor", "ve")

    for h in range(H):
        proj_head(h, KQP, WQs, WKs)
        kb.op("vector", lambda e: e.tensor_tensor(out=SQB, in0=KQP, in1=KQP, op=OP.mult), inc="ve")
        kb.wait("tensor", "ve")
        for c in range(8):
            pk = NRM_PS[c % 4]
            pq = NRM_PS[(c % 4) ^ 1]
            kb.op("tensor", lambda e, c=c, pk=pk: e.matmul(out=pk, lhsT=onesc[0:64, :],
                                                           rhs=SQBc[0:64, c, :],
                                                           start=True, stop=True), inc="pe")
            kb.op("tensor", lambda e, c=c, pq=pq: e.matmul(out=pq, lhsT=onesc[64:128, :],
                                                           rhs=SQBc[64:128, c, :],
                                                           start=True, stop=True), inc="pe")
            kb.wait("vector", "pe")
            p = h + 16 * c
            kb.op("vector", lambda e, pk=pk: e.tensor_copy(out=NST[0:1, 0, :], in_=pk))
            kb.op("vector", lambda e, pq=pq: e.tensor_copy(out=NST[0:1, 1, :], in_=pq), inc="ve")
            kb.wait("sync", "ve")
            kb.dma("sync", nstk[p:p + 1, 1, 0:512], NST[0:1, 0, :], "sq")
            kb.dma("sync", nstk[p:p + 1, 0, 0:512], NST[0:1, 1, :], "sq")
            kb.wait("tensor", "ve")
            kb.wait("vector", "sq")

    if dump == "nstk":
        return _dump([(out[0:128, 0:512], nstk[:, 0, :]),
                      (out[128:256, 0:512], nstk[:, 1, :])])

    FK = A32.t2(512)
    KN = A32.t2(512)
    FCA = A32.t2(512)
    FCB = A32.t2(512)
    csk = A32.t2(512)
    csq = A32.t2(512)
    rxk = A32.t3(3, 512)
    rxq = A32.t3(3, 512)
    kb.wait("scalar", "ve")
    kb.wait("scalar", "sq")
    for side in range(2):
        f_chain(FK, nstk[:, 1 - side, :], FCA, FCB, mul_radial=(side == 1))
        rx = rxk if side == 0 else rxq
        cs = csk if side == 0 else csq
        kb.op("vector", lambda e: e.tensor_tensor(out=KN, in0=FK, in1=FK, op=OP.mult))
        kb.op("vector", lambda e, side=side: e.tensor_tensor(out=KN, in0=KN,
                                                             in1=nstk[:, 1 - side, :], op=OP.mult))
        kb.op("vector", lambda e, rx=rx: e.tensor_scalar(out=rx[:, 2, :], in0=KN, scalar1=-1.0,
                                                         scalar2=1.0, op0=OP.mult, op1=OP.add))
        kb.op("vector", lambda e, rx=rx: e.tensor_scalar(out=rx[:, 2, :], in0=rx[:, 2, :],
                                                         scalar1=1e-20, scalar2=None, op0=OP.max))
        kb.op("vector", lambda e, rx=rx: e.reciprocal(out=rx[:, 1, :], in_=rx[:, 2, :]), inc="vch")
        kb.wait("vector", "vch")
        kb.op("vector", lambda e, rx=rx: e.memset(rx[:, 2, :], 1.0))
        kb.op("vector", lambda e, rx=rx, cs=cs: e.tensor_tensor(out=cs, in0=FK,
                                                                in1=rx[:, 1, :], op=OP.mult))
        if side == 0:
            kb.op("vector", lambda e, cs=cs: e.tensor_scalar(out=cs, in0=cs, scalar1=-4.0,
                                                             scalar2=None, op0=OP.mult))
        kb.op("vector", lambda e, rx=rx: e.tensor_tensor(out=rx[:, 0, :], in0=KN,
                                                         in1=rx[:, 1, :], op=OP.mult))
        kb.op("vector", lambda e, rx=rx: e.tensor_scalar(out=rx[:, 0, :], in0=rx[:, 0, :],
                                                         scalar1=2.0, scalar2=None, op0=OP.mult), inc="ve")
    kb.wait("sync", "ve")
    kb.dma("sync", stkd[:, 0, :], csk, "sq")
    kb.dma("sync", stkd[:, 1, :], csq, "sq")
    for r in range(3):
        kb.dma("sync", stkd[:, 2 + r, :], rxk[:, r, :], "sq")
        kb.dma("sync", stkd[:, 5 + r, :], rxq[:, r, :], "sq")
    kb.barrier()
    A32.reset(mk32)
    A16.reset(mk16)
    if dump == "stkd":
        return _dump([(out[128 * j:128 * j + 128, 0:512], stkd[:, j, :]) for j in range(8)])

    KQb = [A16.t2(T) for _ in range(2)]
    QQb = [A16.t2(T) for _ in range(2)]
    VPh = A16.t3(NT, 65)
    ATb = [A16.t2(1536) for _ in range(2)]
    OTS = A16.t2(512)
    WQs2 = A16.t3(8, 64)
    WKs2 = A16.t3(8, 64)
    KQP2 = A32.t2(T)
    LxB = [A32.t2(1536) for _ in range(2)]

    def stk_row(slot, h, c):
        return stkd[:, slot, :].rearrange("(j h) c -> h j c", h=16)[h:h + 1, c, 0:512]

    attn_end = {}
    av_last = {0: 0, 1: 0}
    ln_last = {0: 0, 1: 0}

    def prep_head(h):
        hb = h % 2
        KQ_, QQ_ = KQb[hb], QQb[hb]
        if h - 2 in attn_end:
            kb.wait("vector", "pesc", attn_end[h - 2][0])
            kb.wait("gpsimd", "pesc", attn_end[h - 2][0])
        kb.wait("tensor", "seln")
        proj_head(h, KQP2, WQs2, WKs2)
        for side in range(2):
            src_ = KQP2[0:64, :] if side == 0 else KQP2[64:128, :]
            dst_ = KQ_ if side == 0 else QQ_
            for c in range(8):
                vk = kb.dma("sync", rowst, stk_row(side, h, c), "sq")
                kb.wait("tensor", "sq", vk)
                kb.op("tensor", lambda e: e.matmul(out=PMI[0:64, 0:512], lhsT=onesb[0:1, 0:64],
                                                   rhs=rowst, start=True, stop=True), inc="pe")
                kb.wait("vector", "pe")
                kb.op("vector", lambda e, c=c, dst_=dst_, src_=src_: e.tensor_tensor(
                    out=dst_[0:64, 512 * c:512 * c + 512],
                    in0=src_[:, 512 * c:512 * c + 512], in1=PMI[0:64, 0:512],
                    op=OP.mult), inc="ve")
                kb.wait("tensor", "ve")
                kb.wait("sync", "ve")
        kv = None
        for r in range(3):
            kb.dma("gpsimd",
                   KQ_[64 + r:65 + r, :].rearrange("p (j c) -> p j c", c=512),
                   stkd[:, 2 + r, :].rearrange("(j h) c -> h j c", h=16)[h:h + 1, :, :], "gq")
            kv = kb.dma("gpsimd",
                        QQ_[64 + r:65 + r, :].rearrange("p (j c) -> p j c", c=512),
                        stkd[:, 5 + r, :].rearrange("(j h) c -> h j c", h=16)[h:h + 1, :, :], "gq")
        return kv

    def load_v(h):
        kb.wait("gpsimd", "peav")
        gv = kb.dma("gpsimd", VPh[:, :, :],
                    vpdb[:, 65 * h:65 * h + 65].rearrange("(tb p) c -> p tb c", p=128), "gq")
        return gv

    def attn_head(h, prep_mark, v_mark):
        hb = h % 2
        KQ_, QQ_ = KQb[hb], QQb[hb]
        kb.wait("tensor", "gq", prep_mark)
        kb.wait("vector", "gq", v_mark)
        if h - 1 in attn_end:
            kb.wait("vector", "peav", attn_end[h - 1][1])
        for tb in range(NT):
            kb.op("vector", lambda e, tb=tb, h=h: e.tensor_scalar(
                out=VPh[:, tb, 0:64], in0=VPh[:, tb, 0:64],
                scalar1=vstk[0:128, 16 * tb + h:16 * tb + h + 1], scalar2=None,
                op0=OP.mult), inc="vfv")
        kb.wait("tensor", "vfv")
        def issue_scores(lc, gi, s0, s1):
            psg = PSA if gi % 2 == 0 else PSB
            kb.wait("tensor", "seln", ln_last[gi % 2])
            for i in range(s0, s1):
                j = i - s0
                kb.op("tensor", lambda e, i=i, j=j, p=psg, KQ_=KQ_, QQ_=QQ_, lc=lc: e.matmul(
                    out=p[0:128, 512 * j:512 * j + 512],
                    lhsT=KQ_[0:67, 128 * i:128 * i + 128],
                    rhs=QQ_[0:67, 512 * lc:512 * lc + 512], start=True, stop=True),
                    inc="pesc" if i == s1 - 1 else None)
            return kb.cnt["pesc"]

        for lc in range(NL):
            sc_cnt = {0: issue_scores(lc, 0, *GRPS[0])}
            for gi, (s0, s1) in enumerate(GRPS):
                psg = PSA if gi % 2 == 0 else PSB
                nst = s1 - s0
                if gi + 1 < len(GRPS):
                    sc_cnt[gi + 1] = issue_scores(lc, gi + 1, *GRPS[gi + 1])
                kb.wait("scalar", "pesc", sc_cnt[gi])
                kb.op("scalar", lambda e, p=psg, nst=nst, gi=gi: e.activation(
                    out=LxB[gi % 2][:, 0:512 * nst], in_=p[0:128, 0:512 * nst],
                    func=AF.Ln), inc="seln")
                ln_last[gi % 2] = kb.cnt["seln"]
                kb.wait("scalar", "peav", av_last[gi % 2])
                ex = kb.op("scalar", lambda e, nst=nst, gi=gi: e.activation(
                    out=ATb[gi % 2][:, 0:512 * nst], in_=LxB[gi % 2][:, 0:512 * nst],
                    func=AF.Exp, scale=nalpha), inc="seexp")
                if gi == 0:
                    kb.wait("tensor", "veot")
                kb.wait("tensor", "seexp", ex)
                for i in range(s0, s1):
                    j = i - s0
                    kb.op("tensor", lambda e, i=i, j=j, gi=gi: e.matmul(
                        out=PAV[0:65, 0:512], lhsT=VPh[:, i, 0:65],
                        rhs=ATb[gi % 2][:, 512 * j:512 * j + 512],
                        start=(i == 0), stop=(i == NS - 1)),
                        inc="peav" if i == s1 - 1 else None)
                av_last[gi % 2] = kb.cnt["peav"]
            kb.wait("vector", "peav")
            kb.op("vector", lambda e: e.reciprocal(out=dinv, in_=PAV[64:65, 0:512]), inc="ve")
            kb.wait("tensor", "ve")
            kb.op("tensor", lambda e: e.matmul(out=PMI[0:64, 0:512], lhsT=onesb[0:1, 0:64],
                                               rhs=dinv, start=True, stop=True), inc="pe")
            kb.wait("vector", "pe")
            kb.op("vector", lambda e: e.tensor_copy(out=dbc, in_=PMI[0:64, 0:512]), inc="ve")
            kb.op("vector", lambda e: e.tensor_tensor(out=OTS[0:64, :], in0=PAV[0:64, 0:512],
                                                      in1=dbc, op=OP.mult), inc="veot")
            kb.wait("sync", "veot")
            kb.dma("sync", otdb[h // 2, 64 * (h % 2):64 * (h % 2) + 64, 512 * lc:512 * lc + 512],
                   OTS[0:64, :], "sq")
            kb.wait("vector", "sq")
        attn_end[h] = (kb.cnt["pesc"], kb.cnt["peav"], kb.cnt["seexp"])

    pm = prep_head(0)
    vm = load_v(0)
    for h in range(H):
        if h + 1 < H:
            nxt = prep_head(h + 1)
        else:
            nxt = None
        attn_head(h, pm, vm)
        if h + 1 < H:
            vm = load_v(h + 1)
        pm = nxt
    kb.barrier()
    A16.reset(markA16)
    A32.reset(markA32)
    if dump == "otdb":
        return _dump([(out[128 * j:128 * j + 128, :], otdb[j, :, 0:1024], "gpsimd")
                      for j in range(8)])

    WOc = A16.t3(8, D)
    WBc = A16.t3(8, D)
    OTL = [A16.t3(8, 128) for _ in range(2)]
    XST = [A32.t2(D) for _ in range(2)]
    YB = A32.t2(D)
    X1B = A32.t2(D)
    SQy = A32.t2(D)
    browS = A32.ap[0:1, A32.cur:A32.cur + 2 * D]; A32.cur += 2 * D
    wrbsS = A32.t2(D)
    kb.dma("gpsimd", WOc[:, :, :], woT, "gq")
    kb.dma("gpsimd", WBc[:, :, :], wbT, "gq")
    kb.dma("sync", browS, brows, "sq")
    kb.dma("sync", wrbsS, wrb, "sq")
    kb.wait("tensor", "gq")
    kb.wait("tensor", "sq")

    for lb in range(NT):
        bb_ = lb % 2
        kb.wait("sync", "pe")
        kb.wait("sync", "ve")
        for j in range(8):
            kb.dma("sync", OTL[bb_][:, j, :], otdb[j, :, 128 * lb:128 * lb + 128], "sq")
        kb.dma("sync", XST[bb_], xrow[128 * lb:128 * lb + 128, :], "sq")
        kb.wait("tensor", "sq")
        for jc in range(2):
            for j in range(8):
                kb.op("tensor", lambda e, j=j, jc=jc, bb_=bb_: e.matmul(
                    out=PSA[0:128, 512 * jc:512 * jc + 512], lhsT=OTL[bb_][:, j, :],
                    rhs=WOc[:, j, 512 * jc:512 * jc + 512], start=(j == 0), stop=False))
            for kt in range(8):
                kb.op("tensor", lambda e, kt=kt, jc=jc, lb=lb: e.matmul(
                    out=PSA[0:128, 512 * jc:512 * jc + 512],
                    lhsT=XT[0:128, kt, 128 * lb:128 * lb + 128],
                    rhs=WBc[0:128, kt, 512 * jc:512 * jc + 512], start=False, stop=False))
            kb.op("tensor", lambda e, jc=jc: e.matmul(
                out=PSA[0:128, 512 * jc:512 * jc + 512], lhsT=onesb,
                rhs=browS[0:1, 512 * jc:512 * jc + 512], start=False, stop=False))
            kb.op("tensor", lambda e, jc=jc: e.matmul(
                out=PSA[0:128, 512 * jc:512 * jc + 512], lhsT=onesb,
                rhs=browS[0:1, D + 512 * jc:D + 512 * jc + 512], start=False, stop=True),
                inc="pe")
        kb.wait("vector", "pe")
        kb.op("vector", lambda e, bb_=bb_: e.scalar_tensor_tensor(
            out=YB, in0=XST[bb_], scalar=1.0, in1=PSA[0:128, 0:1024],
            op0=OP.mult, op1=OP.add), inc="ve")
        kb.wait("tensor", "ve")
        kb.op("vector", lambda e: e.tensor_reduce(out=stat[:, 0:1], in_=YB, axis=AX, op=OP.add))
        kb.op("vector", lambda e: e.tensor_tensor(out=SQy, in0=YB, in1=YB, op=OP.mult))
        kb.op("vector", lambda e: e.tensor_reduce(out=stat[:, 1:2], in_=SQy, axis=AX, op=OP.add))
        kb.op("vector", lambda e: e.tensor_scalar(out=stat[:, 2:3], in0=stat[:, 0:1],
                                                  scalar1=1.0 / D, scalar2=None, op0=OP.mult))
        kb.op("vector", lambda e: e.tensor_scalar(out=stat[:, 4:5], in0=stat[:, 1:2],
                                                  scalar1=1.0 / D, scalar2=None,
                                                  op0=OP.mult), inc="vch")
        kb.wait("vector", "vch")
        kb.op("vector", lambda e: e.tensor_tensor(out=stat[:, 3:4], in0=stat[:, 2:3],
                                                  in1=stat[:, 2:3], op=OP.mult), inc="vch")
        kb.wait("vector", "vch")
        v1 = kb.op("vector", lambda e: e.tensor_tensor(out=stat[:, 5:6], in0=stat[:, 4:5],
                                                       in1=stat[:, 3:4], op=OP.subtract), inc="ve")
        kb.wait("scalar", "ve", v1)
        kb.op("scalar", lambda e: e.activation(out=stat[:, 6:7], in_=stat[:, 5:6],
                                               func=AF.Ln, bias=epsln), inc="sch")
        kb.wait("scalar", "sch")
        s1_ = kb.op("scalar", lambda e: e.activation(out=stat[:, 7:8], in_=stat[:, 6:7],
                                                     func=AF.Exp, scale=-0.5), inc="se")
        kb.wait("vector", "se", s1_)
        kb.op("vector", lambda e: e.tensor_scalar(out=stat[:, 8:9], in0=stat[:, 2:3],
                                                  scalar1=-1.0, scalar2=None,
                                                  op0=OP.mult), inc="vch")
        kb.wait("vector", "vch")
        kb.op("vector", lambda e: e.tensor_scalar(out=X1B, in0=YB, scalar1=stat[:, 8:9],
                                                  scalar2=stat[:, 7:8], op0=OP.add, op1=OP.mult))
        kb.op("vector", lambda e: e.tensor_tensor(out=SQy, in0=X1B, in1=wrbsS, op=OP.mult))
        kb.op("vector", lambda e, lb=lb: e.tensor_reduce(out=sc32[:, lb:lb + 1], in_=SQy,
                                                         axis=AX, op=OP.add), inc="ve")
        kb.wait("sync", "ve")
        kb.dma("sync", x1d[128 * lb:128 * lb + 128, :], X1B, "sq")
        if stage == 1:
            kb.dma("sync", out[128 * lb:128 * lb + 128, :], X1B, "sq")
        kb.wait("vector", "sq")
    if stage == 1:
        _finalize(kb, nc)
        return nc
    kb.barrier()
    A16.reset(0)
    A32.reset(markA32)

    sbc_mark = A32.mark()
    SBC = A32.t2(T)
    ED = A32.t2(128)
    EM = A32.t2(128)

    def bcast_row(src32, dst, rd):
        kb.wait("sync", "ve")
        kb.dma("sync", rd[:, :], src32, "sq")
        kb.wait("sync", "sq")
        for c in range(8):
            kb.dma("sync", rowst.rearrange("r (a p) -> r a p", p=128),
                   rd.rearrange("p c -> () c p")[0:1, 4 * c:4 * c + 4, :], "bcr",
                   allow_slow_non_contiguous=True)
            kb.dma("sync", small[0:1, 15:16], curv, "bfl")
            kb.wait("tensor", "bfl")
            kb.op("tensor", lambda e: e.matmul(out=PMI[0:128, 0:512], lhsT=onesb,
                                               rhs=rowst, start=True, stop=True), inc="pe")
            kb.wait("vector", "pe")
            kb.op("vector", lambda e, c=c, dst=dst: e.tensor_copy(
                out=dst[:, 512 * c:512 * c + 512], in_=PMI[0:128, 0:512]), inc="ve")
            kb.wait("tensor", "ve")
            kb.wait("sync", "ve")

    bcast_row(sc32, SBC, rowd)
    if dump == "sbc":
        return _dump([(out[0:128, 0:32], sc32)] +
                     [(out[128 * (1 + k):128 * (2 + k), :], SBC[:, 1024 * k:1024 * k + 1024])
                      for k in range(4)])
    RTMP = A32.t2(T)
    for c in range(NT):
        kb.op("vector", lambda e, c=c: e.tensor_scalar(
            out=ED, in0=SBC[:, 128 * c:128 * c + 128], scalar1=sc32[:, c:c + 1],
            scalar2=None, op0=OP.is_equal))
        kb.op("vector", lambda e: e.tensor_tensor(out=EM, in0=ED, in1=lt, op=OP.mult))
        kb.op("vector", lambda e: e.tensor_reduce(out=stat[:, 2:3], in_=EM, axis=AX, op=OP.add))
        kb.op("vector", lambda e, c=c: e.tensor_scalar(
            out=RTMP, in0=SBC, scalar1=sc32[:, c:c + 1], scalar2=0.0,
            op0=OP.is_gt, op1=OP.add, accum_out=stat[:, 0:1]),
            inc="vch" if c == 0 else None)
        if c > 0:
            kb.op("vector", lambda e, c=c: e.tensor_scalar(
                out=RTMP[:, 0:128 * c], in0=SBC[:, 0:128 * c], scalar1=sc32[:, c:c + 1],
                scalar2=0.0, op0=OP.is_equal, op1=OP.add, accum_out=stat[:, 1:2]), inc="vch")
        kb.wait("vector", "vch")
        if c > 0:
            kb.op("vector", lambda e: e.tensor_tensor(out=stat[:, 0:1], in0=stat[:, 0:1],
                                                      in1=stat[:, 1:2], op=OP.add), inc="vch")
            kb.wait("vector", "vch")
        kb.op("vector", lambda e: e.tensor_tensor(out=stat[:, 0:1], in0=stat[:, 0:1],
                                                  in1=stat[:, 2:3], op=OP.add), inc="vch")
        kb.wait("vector", "vch")
        kb.op("vector", lambda e, c=c: e.tensor_copy(out=rk32[:, c:c + 1], in_=stat[:, 0:1]), inc="ve")
    A32.reset(sbc_mark)
    RBC = A32.t2(T)
    bcast_row(rk32, RBC, rowd2)
    if dump == "ranks":
        return _dump([(out[0:128, 0:32], rk32)] +
                     [(out[128 * (1 + k):128 * (2 + k), :], RBC[:, 1024 * k:1024 * k + 1024])
                      for k in range(4)])

    PTt = [A16.t2(CAP) for _ in range(2)]
    X1bf = [A16.t2(D) for _ in range(2)]
    X1fs = [A32.t2(D) for _ in range(2)]
    XSEL = A32.t3(4, D)
    GPS = [PSA[:, 0:512], PSA[:, 512:1024], PSA[:, 1024:1536], PSB[:, 0:512],
           PSB[:, 512:1024], PSB[:, 1024:1536], PAV[:, 0:512], PMI[:, 0:512]]
    gthm = {}
    for c in range(NT):
        cb = c % 2
        if c >= 2:
            kb.wait("vector", "pegth", gthm[c - 2])
        kb.wait("sync", "ve")
        kb.dma("sync", X1fs[cb], x1d[128 * c:128 * c + 128, :], "sq")
        kb.wait("vector", "sq")
        kb.op("vector", lambda e, cb=cb: e.tensor_copy(out=X1bf[cb], in_=X1fs[cb]))
        kb.op("vector", lambda e, c=c, cb=cb: e.tensor_scalar(
            out=PTt[cb], in0=io512, scalar1=rk32[:, c:c + 1], scalar2=None,
            op0=OP.is_equal), inc="ve")
        kb.wait("tensor", "ve")
        for rb in range(4):
            for ec in range(2):
                last = (rb == 3 and ec == 1)
                kb.op("tensor", lambda e, c=c, cb=cb, rb=rb, ec=ec: e.matmul(
                    out=GPS[2 * rb + ec][0:128, :], lhsT=PTt[cb][:, 128 * rb:128 * rb + 128],
                    rhs=X1bf[cb][:, 512 * ec:512 * ec + 512],
                    start=(c == 0), stop=(c == NT - 1)), inc="pegth" if last else None)
        gthm[c] = kb.cnt["pegth"]
    kb.wait("vector", "pegth", gthm[NT - 1])
    for rb in range(4):
        for ec in range(2):
            kb.op("vector", lambda e, rb=rb, ec=ec: e.tensor_copy(
                out=XSEL[:, rb, 512 * ec:512 * ec + 512], in_=GPS[2 * rb + ec][0:128, :]), inc="ve")
    kb.wait("tensor", "ve")

    H1F = A32.t3(4, D)
    XATT = A32.t3(4, D)
    SQyB = A32.t2(D)
    MSKS = A32.t2(512)
    H1T = A16.t3(9, CAP)
    QKT = A16.t3(16, CAP)
    VIB = A16.t3(4, H * 65)
    IAT = A16.t3(4, 512)
    OTi = [A16.t2(CAP) for _ in range(8)]
    WS1 = A16.t3(8, 512)
    WS2 = A16.t3(9, 512)
    A1B = A16.t2(512)
    ABT = A16.t3(22, CAP)
    DELB = A16.t3(4, D)
    MKB = A16.t3(4, 512)
    g1bS = A16.t2(D)
    g2bS = A16.t2(D)
    kb.dma("gpsimd", MKB[:, :, :], maskbd, "gq")
    kb.dma("gpsimd", g1bS, g1bi, "gq")
    kb.dma("gpsimd", g2bS, g2bi, "gq")

    def rmsnorm(dst3, src3, gs):
        for rt in range(4):
            kb.op("vector", lambda e, rt=rt: e.tensor_tensor(out=SQyB, in0=src3[:, rt, :],
                                                             in1=src3[:, rt, :], op=OP.mult))
            kb.op("vector", lambda e: e.tensor_reduce(out=stat[:, 0:1], in_=SQyB,
                                                      axis=AX, op=OP.add), inc="vch")
            kb.wait("vector", "vch")
            r = kb.op("vector", lambda e: e.tensor_scalar(out=stat[:, 1:2], in0=stat[:, 0:1],
                                                          scalar1=1.0 / D, scalar2=EPS_RMS,
                                                          op0=OP.mult, op1=OP.add), inc="ve")
            kb.wait("scalar", "ve", r)
            kb.op("scalar", lambda e: e.activation(out=stat[:, 2:3], in_=stat[:, 1:2],
                                                   func=AF.Ln), inc="sch")
            kb.wait("scalar", "sch")
            s2 = kb.op("scalar", lambda e: e.activation(out=stat[:, 3:4], in_=stat[:, 2:3],
                                                        func=AF.Exp, scale=-0.5), inc="se")
            kb.wait("vector", "se", s2)
            kb.op("vector", lambda e, rt=rt: e.tensor_scalar(out=SQyB, in0=src3[:, rt, :],
                                                             scalar1=stat[:, 3:4], scalar2=None,
                                                             op0=OP.mult))
            kb.op("vector", lambda e, rt=rt, dst3=dst3, gs=gs: e.tensor_tensor(
                out=dst3[:, rt, :], in0=SQyB, in1=gs, op=OP.mult), inc="ve")

    def transpose_to(dst3, src3, nkt):
        kb.wait("tensor", "ve")
        for rt in range(4):
            for eh in range(nkt):
                kb.op("tensor", lambda e, rt=rt, eh=eh: e.transpose(
                    out=PMI[0:128, 0:128], in_=src3[0:128, rt, 128 * eh:128 * eh + 128],
                    identity=eyeb), inc="pe")
                kb.wait("vector", "pe")
                kb.op("vector", lambda e, rt=rt, eh=eh: e.tensor_copy(
                    out=dst3[:, eh, 128 * rt:128 * rt + 128], in_=PMI[0:128, 0:128]), inc="ve")
                kb.wait("tensor", "ve")

    if dump == "x1d":
        return _dump([(out[:, :], x1d[:, :])])
    if dump == "xsel":
        return _dump([(out[128 * rt:128 * rt + 128, :], XSEL[:, rt, :]) for rt in range(4)])
    kb.wait("vector", "gq")
    rmsnorm(H1F, XSEL, g1bS)
    if dump == "h1":
        return _dump([(out[128 * rt:128 * rt + 128, :], H1F[:, rt, :]) for rt in range(4)])
    kb.op("gpsimd", lambda e: e.memset(H1T[0:1, 8, :], 1.0), inc="gp")
    kb.wait("tensor", "gp")
    transpose_to(H1T, H1F, 8)
    for dch in range(4):
        kb.wait("gpsimd", "pe")
        for kt in range(8):
            kb.dma("gpsimd", WS1[0:128, kt, :], wqkT[:, kt, 512 * dch:512 * dch + 512], "gq")
        kb.wait("tensor", "gq")
        for db in range(4):
            for kt in range(8):
                kb.op("tensor", lambda e, db=db, kt=kt: e.matmul(
                    out=PAV[0:128, 0:512], lhsT=WS1[0:128, kt, 128 * db:128 * db + 128],
                    rhs=H1T[:, kt, :], start=(kt == 0), stop=(kt == 7)),
                    inc="pe" if kt == 7 else None)
            kb.wait("vector", "pe")
            kb.op("vector", lambda e, dch=dch, db=db: e.tensor_copy(
                out=QKT[:, 4 * dch + db, :], in_=PAV[0:128, 0:512]), inc="ve")
            kb.wait("tensor", "ve")
    for ch in range(4):
        c0 = 260 * ch
        kb.wait("gpsimd", "pe")
        for kt in range(9):
            kb.dma("gpsimd", WS2[0:128, kt, 0:260], wvfT[:, kt, c0:c0 + 260], "gq")
        kb.wait("tensor", "gq")
        for rt in range(4):
            for kt in range(8):
                kb.op("tensor", lambda e, rt=rt, kt=kt: e.matmul(
                    out=PAV[0:128, 0:260], lhsT=H1T[:, kt, 128 * rt:128 * rt + 128],
                    rhs=WS2[0:128, kt, 0:260], start=(kt == 0), stop=False))
            kb.op("tensor", lambda e, rt=rt: e.matmul(
                out=PAV[0:128, 0:260], lhsT=H1T[0:1, 8, 128 * rt:128 * rt + 128],
                rhs=WS2[0:1, 8, 0:260], start=False, stop=True), inc="pe")
            kb.wait("vector", "pe")
            kb.op("vector", lambda e, rt=rt, c0=c0: e.tensor_copy(
                out=VIB[:, rt, c0:c0 + 260], in_=PAV[0:128, 0:260]), inc="ve")
            kb.wait("tensor", "ve")
    for h in range(H):
        qblk, qrow = h // 2, 64 * (h % 2)
        kblk, krow = 8 + h // 2, 64 * (h % 2)
        if h > 0:
            kb.wait("scalar", "peav")
        for st in range(4):
            psl = PSA[0:128, 512 * st:512 * st + 512] if st < 3 else PSB[0:128, 0:512]
            kb.op("tensor", lambda e, st=st, psl=psl, qblk=qblk, qrow=qrow, kblk=kblk, krow=krow: e.matmul(
                out=psl, lhsT=QKT[krow:krow + 64, kblk, 128 * st:128 * st + 128],
                rhs=QKT[qrow:qrow + 64, qblk, 0:512], start=True, stop=True), inc="pesc")
        kb.wait("vector", "pesc")
        for st in range(4):
            psl = PSA[0:128, 512 * st:512 * st + 512] if st < 3 else PSB[0:128, 0:512]
            if st > 0:
                kb.wait("vector", "seexp")
            vv = kb.op("vector", lambda e, st=st, psl=psl: e.scalar_tensor_tensor(
                out=MSKS, in0=psl, scalar=0.125, in1=MKB[:, st, :],
                op0=OP.mult, op1=OP.add), inc="ve")
            kb.wait("scalar", "ve", vv)
            kb.op("scalar", lambda e, st=st: e.activation(out=IAT[:, st, :], in_=MSKS,
                                                          func=AF.Exp), inc="seexp")
        kb.wait("tensor", "seexp")
        for st in range(4):
            kb.op("tensor", lambda e, st=st, h=h: e.matmul(
                out=PAV[0:65, 0:512], lhsT=VIB[:, st, 65 * h:65 * h + 65],
                rhs=IAT[:, st, :], start=(st == 0), stop=(st == 3)),
                inc="peav" if st == 3 else None)
        kb.wait("vector", "peav")
        kb.op("vector", lambda e: e.reciprocal(out=dinv, in_=PAV[64:65, 0:512]), inc="ve")
        kb.wait("tensor", "ve")
        kb.op("tensor", lambda e: e.matmul(out=PMI[0:64, 0:512], lhsT=onesb[0:1, 0:64],
                                           rhs=dinv, start=True, stop=True), inc="pe")
        kb.wait("vector", "pe")
        kb.op("vector", lambda e: e.tensor_copy(out=dbc, in_=PMI[0:64, 0:512]), inc="ve")
        kb.op("vector", lambda e, h=h: e.tensor_tensor(
            out=OTi[h // 2][64 * (h % 2):64 * (h % 2) + 64, :], in0=PAV[0:64, 0:512],
            in1=dbc, op=OP.mult), inc="veot")
        kb.wait("tensor", "veot")
    for jc in range(2):
        kb.wait("gpsimd", "pe")
        for kt in range(8):
            kb.dma("gpsimd", WS1[0:128, kt, :], woutT[:, kt, 512 * jc:512 * jc + 512], "gq")
        kb.wait("tensor", "gq")
        for rt in range(4):
            for j in range(8):
                kb.op("tensor", lambda e, rt=rt, j=j: e.matmul(
                    out=PAV[0:128, 0:512], lhsT=OTi[j][:, 128 * rt:128 * rt + 128],
                    rhs=WS1[0:128, j, :], start=(j == 0), stop=(j == 7)),
                    inc="pe" if j == 7 else None)
            kb.wait("vector", "pe")
            kb.op("vector", lambda e, rt=rt, jc=jc: e.tensor_copy(
                out=XATT[:, rt, 512 * jc:512 * jc + 512], in_=PAV[0:128, 0:512]), inc="ve")
            kb.wait("tensor", "ve")
    if dump == "xattn":
        return _dump([(out[128 * rt:128 * rt + 128, :], XATT[:, rt, :]) for rt in range(4)])
    for rt in range(4):
        kb.op("vector", lambda e, rt=rt: e.tensor_tensor(out=XATT[:, rt, :], in0=XATT[:, rt, :],
                                                         in1=XSEL[:, rt, :], op=OP.add), inc="ve")
    rmsnorm(H1F, XATT, g2bS)
    if dump == "h2":
        return _dump([(out[128 * rt:128 * rt + 128, :], H1F[:, rt, :]) for rt in range(4)])
    transpose_to(H1T, H1F, 8)
    ABQ = H1F
    FCH = [(0, 1024), (1024, 2048), (2048, 2730)]
    for f0, f1 in FCH:
        for sub in range((f1 - f0 + 511) // 512):
            s0f = f0 + 512 * sub
            nn = min(512, f1 - s0f)
            kb.wait("gpsimd", "pe")
            for kt in range(8):
                kb.dma("gpsimd", WS1[0:128, kt, 0:nn], w1T[:, kt, s0f:s0f + nn], "gq")
                kb.dma("gpsimd", WS2[0:128, kt, 0:nn], w2T[:, kt, s0f:s0f + nn], "gq")
            kb.wait("tensor", "gq")
            for rt in range(4):
                for kt in range(8):
                    kb.op("tensor", lambda e, rt=rt, kt=kt, nn=nn: e.matmul(
                        out=PAV[0:128, 0:nn], lhsT=H1T[:, kt, 128 * rt:128 * rt + 128],
                        rhs=WS1[0:128, kt, 0:nn], start=(kt == 0), stop=(kt == 7)))
                for kt in range(8):
                    kb.op("tensor", lambda e, rt=rt, kt=kt, nn=nn: e.matmul(
                        out=PMI[0:128, 0:nn], lhsT=H1T[:, kt, 128 * rt:128 * rt + 128],
                        rhs=WS2[0:128, kt, 0:nn], start=(kt == 0), stop=(kt == 7)),
                        inc="pe" if kt == 7 else None)
                kb.wait("scalar", "pe")
                kb.op("scalar", lambda e, nn=nn: e.activation(out=A1B[:, 0:nn],
                                                              in_=PAV[0:128, 0:nn],
                                                              func=AF.Silu), inc="se")
                kb.wait("vector", "se")
                kb.op("vector", lambda e, rt=rt, nn=nn, s0f=s0f, f0=f0: e.tensor_tensor(
                    out=ABQ[:, rt, s0f - f0:s0f - f0 + nn], in0=A1B[:, 0:nn],
                    in1=PMI[0:128, 0:nn], op=OP.mult), inc="ve")
                kb.wait("tensor", "ve")
        for rt in range(4):
            for eh in range((f1 - f0 + 127) // 128):
                gkt = (f0 + 128 * eh) // 128
                ew = min(128, f1 - f0 - 128 * eh)
                kb.op("tensor", lambda e, rt=rt, eh=eh, ew=ew: e.transpose(
                    out=PMI[0:ew, 0:128], in_=ABQ[0:128, rt, 128 * eh:128 * eh + ew],
                    identity=eyeb), inc="pe")
                kb.wait("vector", "pe")
                kb.op("vector", lambda e, rt=rt, gkt=gkt, ew=ew: e.tensor_copy(
                    out=ABT[0:ew, gkt, 128 * rt:128 * rt + 128], in_=PMI[0:ew, 0:128]), inc="ve")
                kb.wait("tensor", "ve")
    w3_mark = A16.mark()
    W3s = [A16.t2(D) for _ in range(2)]
    xfm = {}
    for kt in range(22):
        kw = min(128, DFF - 128 * kt)
        if kt >= 2:
            kb.wait("gpsimd", "pexf", xfm[kt - 2])
        kb.dma("gpsimd", W3s[kt % 2], w3T[:, kt, :], "gq")
        kb.wait("tensor", "gq")
        for rt in range(4):
            for jc in range(2):
                last = (rt == 3 and jc == 1)
                kb.op("tensor", lambda e, rt=rt, jc=jc, kt=kt, kw=kw: e.matmul(
                    out=GPS[2 * rt + jc][0:128, :], lhsT=ABT[0:kw, kt, 128 * rt:128 * rt + 128],
                    rhs=W3s[kt % 2][0:kw, 512 * jc:512 * jc + 512],
                    start=(kt == 0), stop=(kt == 21)), inc="pexf" if last else None)
        xfm[kt] = kb.cnt["pexf"]
    kb.wait("vector", "pexf", xfm[21])
    for rt in range(4):
        kb.op("vector", lambda e, rt=rt: e.tensor_tensor(
            out=XATT[:, rt, :], in0=XATT[:, rt, :], in1=XSEL[:, rt, :], op=OP.subtract))
        kb.op("vector", lambda e, rt=rt: e.scalar_tensor_tensor(
            out=XATT[:, rt, 0:512], in0=XATT[:, rt, 0:512], scalar=1.0,
            in1=GPS[2 * rt + 0][0:128, :], op0=OP.mult, op1=OP.add))
        kb.op("vector", lambda e, rt=rt: e.scalar_tensor_tensor(
            out=XATT[:, rt, 512:1024], in0=XATT[:, rt, 512:1024], scalar=1.0,
            in1=GPS[2 * rt + 1][0:128, :], op0=OP.mult, op1=OP.add))
        kb.op("vector", lambda e, rt=rt: e.tensor_copy(out=DELB[:, rt, :], in_=XATT[:, rt, :]), inc="ve")
    kb.wait("tensor", "ve")
    if dump == "delb":
        return _dump([(out[128 * rt:128 * rt + 128, :], XATT[:, rt, :]) for rt in range(4)])
    if dump == "qkt":
        return _dump([(out[128 * i:128 * i + 128, 0:512], QKT[:, i, :], "gpsimd")
                      for i in range(16)])

    A16.reset(w3_mark)
    PRB = A16.t3(4, 128)
    Y2B = A32.t2(D)
    for li in range(NT):
        lb = li
        cb = li % 2
        kb.wait("sync", "ve")
        kb.dma("sync", X1fs[cb], x1d[128 * lb:128 * lb + 128, :], "sq")
        kb.wait("vector", "pe")
        for rb in range(4):
            kb.op("vector", lambda e, lb=lb, rb=rb: e.tensor_scalar(
                out=PRB[:, rb, :], in0=RBC[:, 128 * lb:128 * lb + 128],
                scalar1=iorb[:, rb:rb + 1], scalar2=None, op0=OP.is_equal), inc="ve")
        kb.wait("tensor", "ve")
        for jc in range(2):
            for rb in range(4):
                kb.op("tensor", lambda e, rb=rb, jc=jc: e.matmul(
                    out=PAV[0:128, 0:512], lhsT=PRB[:, rb, :],
                    rhs=DELB[0:128, rb, 512 * jc:512 * jc + 512],
                    start=(rb == 0), stop=(rb == 3)), inc="pe" if rb == 3 else None)
            kb.wait("vector", "pe")
            kb.wait("vector", "sq")
            kb.op("vector", lambda e, jc=jc, cb=cb: e.scalar_tensor_tensor(
                out=Y2B[:, 512 * jc:512 * jc + 512], in0=X1fs[cb][:, 512 * jc:512 * jc + 512],
                scalar=2.0, in1=PAV[0:128, 0:512], op0=OP.mult, op1=OP.add), inc="ve")
            kb.wait("tensor", "ve")
        kb.op("vector", lambda e: e.tensor_reduce(out=stat[:, 0:1], in_=Y2B, axis=AX, op=OP.add))
        kb.op("vector", lambda e: e.tensor_tensor(out=SQyB, in0=Y2B, in1=Y2B, op=OP.mult))
        kb.op("vector", lambda e: e.tensor_reduce(out=stat[:, 1:2], in_=SQyB, axis=AX, op=OP.add))
        kb.op("vector", lambda e: e.tensor_scalar(out=stat[:, 2:3], in0=stat[:, 0:1],
                                                  scalar1=1.0 / D, scalar2=None, op0=OP.mult))
        kb.op("vector", lambda e: e.tensor_scalar(out=stat[:, 4:5], in0=stat[:, 1:2],
                                                  scalar1=1.0 / D, scalar2=None,
                                                  op0=OP.mult), inc="vch")
        kb.wait("vector", "vch")
        kb.op("vector", lambda e: e.tensor_tensor(out=stat[:, 3:4], in0=stat[:, 2:3],
                                                  in1=stat[:, 2:3], op=OP.mult), inc="vch")
        kb.wait("vector", "vch")
        v1 = kb.op("vector", lambda e: e.tensor_tensor(out=stat[:, 5:6], in0=stat[:, 4:5],
                                                       in1=stat[:, 3:4], op=OP.subtract), inc="ve")
        kb.wait("scalar", "ve", v1)
        kb.op("scalar", lambda e: e.activation(out=stat[:, 6:7], in_=stat[:, 5:6],
                                               func=AF.Ln, bias=epsln), inc="sch")
        kb.wait("scalar", "sch")
        s1_ = kb.op("scalar", lambda e: e.activation(out=stat[:, 7:8], in_=stat[:, 6:7],
                                                     func=AF.Exp, scale=-0.5), inc="se")
        kb.wait("vector", "se", s1_)
        kb.op("vector", lambda e: e.tensor_scalar(out=stat[:, 8:9], in0=stat[:, 2:3],
                                                  scalar1=-1.0, scalar2=None,
                                                  op0=OP.mult), inc="vch")
        kb.wait("vector", "vch")
        kb.op("vector", lambda e: e.tensor_scalar(out=Y2B, in0=Y2B, scalar1=stat[:, 8:9],
                                                  scalar2=stat[:, 7:8], op0=OP.add, op1=OP.mult), inc="ve")
        kb.wait("sync", "ve")
        kb.dma("sync", out[128 * li:128 * li + 128, :], Y2B, "stq")
        kb.wait("vector", "stq")

    _finalize(kb, nc)
    return nc


def _finalize(kb, nc):
    kb.wait("sync", "stq") if "stq" in kb.cnt else None
    kb.wait("sync", "sq")
    with nc.Block() as block:
        @block.sync
        def _(e):
            for f in kb.ops["sync"]:
                f(e)

        @block.gpsimd
        def _(e):
            for f in kb.ops["gpsimd"]:
                f(e)

        @block.tensor
        def _(e):
            for f in kb.ops["tensor"]:
                f(e)

        @block.vector
        def _(e):
            for f in kb.ops["vector"]:
                f(e)

        @block.scalar
        def _(e):
            for f in kb.ops["scalar"]:
                f(e)


def _wtiles(aT, ktiles):
    K, N = aT.shape
    assert K <= 128 * ktiles
    buf = np.zeros((128 * ktiles, N), dtype=np.float32)
    buf[:K] = aT
    return np.ascontiguousarray(buf.reshape(ktiles, 128, N).transpose(1, 0, 2))


def _prep_inputs(inp, b):
    x = np.ascontiguousarray(inp["L0_attn_query"][b]).astype(np.float32, copy=False)
    d = {}
    d["xrow"] = x
    d["xtT"] = _wtiles(np.ascontiguousarray(x.T), 8)
    d["wqT"] = _wtiles(np.ascontiguousarray(inp["L0_attn_Wq"].T), 8)
    d["wkT"] = _wtiles(np.ascontiguousarray(inp["L0_attn_Wk"].T), 8)
    wv = np.zeros((1152, H * 65), dtype=np.float32)
    wvf = np.zeros((1152, H * 65), dtype=np.float32)
    WvT = inp["L0_attn_Wv"].T
    Wqkv = inp["L0_ffn_W_qkv"]
    for h in range(H):
        wv[0:1024, 65 * h:65 * h + 64] = WvT[:, 64 * h:64 * h + 64]
        wv[1024, 65 * h:65 * h + 64] = inp["L0_attn_bv"][64 * h:64 * h + 64]
        wv[1024, 65 * h + 64] = 1.0
        wvf[0:1024, 65 * h:65 * h + 64] = Wqkv[2 * D + 64 * h:2 * D + 64 * h + 64, :].T
        wvf[1024, 65 * h + 64] = 1.0
    d["wvT"] = np.ascontiguousarray(wv.reshape(9, 128, H * 65).transpose(1, 0, 2))
    d["wvfT"] = np.ascontiguousarray(wvf.reshape(9, 128, H * 65).transpose(1, 0, 2))
    d["woT"] = _wtiles(np.ascontiguousarray(inp["L0_attn_Wo"].T), 8)
    d["wbT"] = _wtiles(np.ascontiguousarray(inp["L0_attn_Wb"].T), 8)
    d["wqkT"] = _wtiles(np.ascontiguousarray(Wqkv[0:2 * D, :].T), 8)
    d["woutT"] = _wtiles(np.ascontiguousarray(inp["L0_ffn_W_out"].T), 8)
    d["w1T"] = _wtiles(np.ascontiguousarray(inp["L0_ffn_W1"].T), 8)
    d["w2T"] = _wtiles(np.ascontiguousarray(inp["L0_ffn_W2"].T), 8)
    d["w3T"] = _wtiles(np.ascontiguousarray(inp["L0_ffn_W3"].T), 22)
    d["bqt"] = np.ascontiguousarray(inp["L0_attn_bq"].reshape(8, 128).T)
    d["bkt"] = np.ascontiguousarray(inp["L0_attn_bk"].reshape(8, 128).T)
    d["brows"] = np.concatenate([inp["L0_attn_bo"], inp["L0_attn_bb"]]).reshape(1, 2 * D)
    d["wrb"] = np.ascontiguousarray(np.broadcast_to(inp["L0_ffn_w_router"], (128, D)))
    d["g1bi"] = np.ascontiguousarray(np.broadcast_to(inp["L0_ffn_g1"], (128, D)))
    d["g2bi"] = np.ascontiguousarray(np.broadcast_to(inp["L0_ffn_g2"], (128, D)))
    d["rwrep"] = np.ascontiguousarray(
        inp["L0_attn_radial_weights"].reshape(H)[np.arange(128) % H]).reshape(128, 1)
    d["curv"] = np.asarray(inp["L0_attn_curvature"], dtype=np.float32).reshape(1, 1)
    p = np.arange(128)
    s = np.arange(128)
    d["lt128"] = (s[None, :] < p[:, None]).astype(np.float32)
    d["iota512b"] = np.ascontiguousarray(
        np.broadcast_to(np.arange(512, dtype=np.float32), (128, 512)))
    d["iotarb"] = (p[:, None] + 128 * np.arange(4)[None, :]).astype(np.float32)
    mb = np.zeros((128, 4, 512), dtype=np.float32)
    for st in range(4):
        sglob = 128 * st + p
        mb[:, st, :] = np.where(sglob[:, None] <= np.arange(512)[None, :], 0.0, -1e30)
    d["maskbd"] = mb
    d["onesrow"] = np.ones((1, 128), dtype=np.float32)
    d["onescol"] = np.ones((128, 1), dtype=np.float32)
    d["eyef"] = np.eye(128, dtype=np.float32)
    for k in d:
        d[k] = np.ascontiguousarray(d[k], dtype=np.float32)
    return d


def kernel(**inputs):
    nc = build(2)
    in_maps = [_prep_inputs(inputs, c % 4) for c in range(8)]
    res = run_bass_kernel_spmd(nc, in_maps, core_ids=list(range(8)))
    outp = np.empty((B, T, D), dtype=np.float32)
    for b_ in range(4):
        outp[b_] = res.results[b_]["out"]
    return outp


# version 07
# speedup vs baseline: 1.6426x; 1.0516x over previous
import contextlib
import numpy as np
import ml_dtypes
import concourse.bass as bass
import concourse.mybir as mybir
from concourse.bass_utils import run_bass_kernel_spmd
from concourse.alu_op_type import AluOpType as OP

F32 = mybir.dt.float32
BF16 = mybir.dt.bfloat16
AF = mybir.ActivationFunctionType
AX = mybir.AxisListType.X

B, T, D = 4, 4096, 1024
H, CAP, DFF = 16, 512, 2730
EPS_H, EPS_LN, EPS_RMS = 1e-8, 1e-5, 1e-6
NT, NL, NS = 32, 8, 32
GRPS = [(3 * i, min(3 * i + 3, NS)) for i in range(11)]
ENGINES = ("sync", "gpsimd", "tensor", "vector", "scalar")


class KB:
    def __init__(self, nc, es):
        self.nc, self.es = nc, es
        self.ops = {e: [] for e in ENGINES}
        self.cnt = {}
        self.sems = {}

    def sem(self, name):
        if name not in self.sems:
            self.sems[name] = self.es.enter_context(self.nc.semaphore(name))
            self.cnt[name] = 0
        return self.sems[name]

    def op(self, eng, fn, inc=None):
        if inc is None:
            self.ops[eng].append(fn)
            return None
        s = self.sem(inc)
        self.cnt[inc] += 1
        self.ops[eng].append(lambda e, f=fn, s=s: f(e).then_inc(s, 1))
        return self.cnt[inc]

    def dma(self, eng, out_ap, in_ap, inc, **kw):
        s = self.sem(inc)
        self.cnt[inc] += 16
        self.ops[eng].append(
            lambda e, o=out_ap, i=in_ap, s=s, kw=kw: e.dma_start(out=o, in_=i, **kw).then_inc(s, 16))
        return self.cnt[inc]

    def wait(self, eng, sem, val=None):
        s = self.sem(sem)
        v = self.cnt.get(sem, 0) if val is None else val
        if v <= 0:
            return 0
        self.ops[eng].append(lambda e, s=s, v=v: e.wait_ge(s, v))
        return v

    def barrier(self):
        marks = []
        for eng in ENGINES:
            sm = "bar_" + eng
            marks.append((sm, self.op(eng, lambda e: e.nop(), inc=sm)))
        for eng in ENGINES:
            for sm, v in marks:
                self.wait(eng, sm, v)


class Arena:
    def __init__(self, ap, total):
        self.ap, self.total, self.cur = ap, total, 0

    def mark(self):
        return self.cur

    def reset(self, m):
        self.cur = m

    def t2(self, n):
        assert self.cur + n <= self.total, f"arena overflow {self.cur}+{n}>{self.total}"
        off = self.cur
        self.cur += n
        return self.ap[:, off:off + n]

    def t3(self, a, b):
        return self.t2(a * b).rearrange("p (a b) -> p a b", a=a)


def build(stage, dump=None):
    es = contextlib.ExitStack()
    nc = bass.Bass("TRN2", target_bir_lowering=False)
    kb = KB(nc, es)

    def _dump(pairs):
        for p in pairs:
            o, i = p[0], p[1]
            eng = p[2] if len(p) > 2 else "sync"
            kb.dma(eng, o, i, "stq")
        kb.wait("sync", "stq")
        kb.wait("gpsimd", "stq")
        _finalize(kb, nc)
        return nc

    def din(name, shape):
        return nc.dram_tensor(name, list(shape), F32, kind="ExternalInput").ap()

    xrow = din("xrow", (T, D))
    xtT = din("xtT", (128, 8, T))
    wqT = din("wqT", (128, 8, D))
    wkT = din("wkT", (128, 8, D))
    wvT = din("wvT", (128, 9, H * 65))
    woT = din("woT", (128, 8, D))
    wbT = din("wbT", (128, 8, D))
    wqkT = din("wqkT", (128, 8, 2 * D))
    wvfT = din("wvfT", (128, 9, H * 65))
    woutT = din("woutT", (128, 8, D))
    w1T = din("w1T", (128, 8, DFF))
    w2T = din("w2T", (128, 8, DFF))
    w3T = din("w3T", (128, 22, D))
    bqt = din("bqt", (128, 8))
    bkt = din("bkt", (128, 8))
    brows = din("brows", (1, 2 * D))
    wrb = din("wrb", (128, D))
    g1bi = din("g1bi", (128, D))
    g2bi = din("g2bi", (128, D))
    rwrep = din("rwrep", (128, 1))
    curv = din("curv", (1, 1))
    lt128 = din("lt128", (128, 128))
    iota512b = din("iota512b", (128, 512))
    iotarb = din("iotarb", (128, 4))
    maskbd = din("maskbd", (128, 4, 512))
    onesrow = din("onesrow", (1, 128))
    onescol = din("onescol", (128, 1))
    eyef = din("eyef", (128, 128))

    out = nc.dram_tensor("out", [T, D], F32, kind="ExternalOutput").ap()

    x1d = nc.dram_tensor("x1d", [T, D], F32).ap()
    vpdb = nc.dram_tensor("vpdb", [T, H * 65], BF16).ap()
    otdb = nc.dram_tensor("otdb", [8, 128, T], BF16).ap()
    stkd = nc.dram_tensor("stkd", [128, 8, 512], F32).ap()
    rowd = nc.dram_tensor("rowd", [128, 32], F32).ap()
    rowd2 = nc.dram_tensor("rowd2", [128, 32], F32).ap()

    def sb(name, shape, dt):
        return es.enter_context(nc.sbuf_tensor(name, list(shape), dt))[:]

    onesb = sb("onesb", (1, 128), F32)
    onesbf = sb("onesbf", (1, 128), BF16)
    onesc = sb("onesc", (128, 1), F32)
    eyeb = sb("eyeb", (128, 128), F32)
    lt = sb("lt", (128, 128), F32)
    io512 = sb("io512", (128, 512), F32)
    iorb = sb("iorb", (128, 4), F32)
    bqs = sb("bqs", (128, 8), F32)
    bks = sb("bks", (128, 8), F32)
    radl = sb("radl", (128, 1), F32)
    nalpha = sb("nalpha", (128, 1), F32)
    small = sb("small", (1, 16), F32)
    stat = sb("stat", (128, 16), F32)
    epsln = sb("epsln", (128, 1), F32)
    dinv = sb("dinv", (1, 512), F32)
    dbc = sb("dbc", (64, 512), F32)
    vstk = sb("vstk", (128, 512), F32)
    sc32 = sb("sc32", (128, 32), F32)
    rk32 = sb("rk32", (128, 32), F32)
    rowst = sb("rowst", (1, 512), F32)

    A16 = Arena(sb("A16", (128, 57344), BF16), 57344)
    A32 = Arena(sb("A32", (128, 20992), F32), 20992)

    PSA = es.enter_context(nc.psum_tensor("PSA", [128, 1536], F32))[:]
    PSB = es.enter_context(nc.psum_tensor("PSB", [128, 1536], F32))[:]
    PAV = es.enter_context(nc.psum_tensor("PAV", [128, 512], F32))[:]
    PMI = es.enter_context(nc.psum_tensor("PMI", [128, 512], F32))[:]
    PROJ_PS = [PSA[:, 0:512], PSB[:, 0:512], PSA[:, 512:1024], PSB[:, 512:1024]]
    NRM_PS = [PMI[0:1, 0:512], PAV[0:1, 0:512], PSA[0:1, 1024:1536], PSB[0:1, 1024:1536]]

    for o, i in [(onesb, onesrow), (onesc, onescol), (eyeb, eyef), (lt, lt128),
                 (io512, iota512b), (iorb, iotarb),
                 (bqs, bqt), (bks, bkt), (radl, rwrep), (small[0:1, 0:1], curv)]:
        kb.dma("sync", o, i, "dq")
    kb.dma("gpsimd", onesbf, onesrow, "gq")

    XT = A16.t3(8, T)
    markA16 = A16.mark()
    markA32 = A32.mark()
    kb.dma("gpsimd", XT[:, :, :], xtT, "gq")
    kb.op("gpsimd", lambda e: e.memset(epsln, EPS_LN), inc="gp")

    kb.wait("vector", "dq")
    kb.wait("vector", "gp")
    kb.op("vector", lambda e: e.tensor_scalar(out=small[0:1, 1:2], in0=small[0:1, 0:1],
                                              scalar1=-1.0, scalar2=None, op0=OP.mult), inc="vch")
    kb.wait("vector", "vch")
    n = kb.op("vector", lambda e: e.tensor_scalar(out=small[0:1, 2:3], in0=small[0:1, 0:1],
                                                  scalar1=small[0:1, 1:2], scalar2=EPS_H,
                                                  op0=OP.max, op1=OP.max), inc="ve")
    kb.wait("scalar", "ve", n)
    kb.op("scalar", lambda e: e.activation(out=small[0:1, 4:5], in_=small[0:1, 2:3],
                                           func=AF.Ln), inc="sch")
    kb.wait("scalar", "sch")
    m = kb.op("scalar", lambda e: e.activation(out=small[0:1, 5:6], in_=small[0:1, 4:5],
                                               func=AF.Exp, scale=0.5), inc="se")
    kb.wait("vector", "se", m)
    kb.op("vector", lambda e: e.tensor_scalar(out=small[0:1, 6:7], in0=small[0:1, 5:6],
                                              scalar1=EPS_H, scalar2=8.0, op0=OP.add,
                                              op1=OP.mult), inc="vch")
    kb.wait("vector", "vch")
    kb.op("vector", lambda e: e.reciprocal(out=small[0:1, 7:8], in_=small[0:1, 6:7]), inc="vch")
    kb.wait("vector", "vch")
    n2 = kb.op("vector", lambda e: e.tensor_scalar(out=small[0:1, 8:9], in0=small[0:1, 7:8],
                                                   scalar1=-1.0, scalar2=None, op0=OP.mult), inc="ve")
    kb.wait("tensor", "ve", n2)
    k1 = kb.op("tensor", lambda e: e.matmul(out=PMI[:, 0:1], lhsT=onesb, rhs=small[0:1, 8:9],
                                            start=True, stop=True), inc="pe")
    kb.wait("vector", "pe", k1)
    kb.op("vector", lambda e: e.tensor_copy(out=nalpha, in_=PMI[:, 0:1]), inc="ve")
    ms = kb.op("scalar", lambda e: e.activation(out=radl, in_=radl, func=AF.Exp, scale=-1.0), inc="se")
    kb.wait("vector", "se", ms)
    kb.op("vector", lambda e: e.tensor_scalar(out=radl, in0=radl, scalar1=1.0,
                                              scalar2=None, op0=OP.add), inc="vch")
    kb.wait("vector", "vch")
    kb.op("vector", lambda e: e.reciprocal(out=radl, in_=radl), inc="ve")
    kb.barrier()
    if dump == "smalls":
        return _dump([(out[0:1, 0:16], small), (out[1:2, 0:1], nalpha[0:1, :]),
                      (out[2:3, 0:1], radl[0:1, :])])

    def f_chain(dst, src, sa, sbuf, mul_radial=False):
        kb.wait("scalar", "ve")
        kb.op("scalar", lambda e: e.activation(out=sa, in_=src, func=AF.Ln))
        kb.op("scalar", lambda e: e.activation(out=sa, in_=sa, func=AF.Exp, scale=0.5))
        q = kb.op("scalar", lambda e: e.activation(out=sbuf, in_=sa, func=AF.Exp, scale=2.0), inc="se")
        kb.wait("vector", "se", q)
        kb.op("vector", lambda e: e.tensor_scalar(out=sbuf, in0=sbuf, scalar1=1.0,
                                                  scalar2=None, op0=OP.add))
        kb.op("vector", lambda e: e.reciprocal(out=sbuf, in_=sbuf), inc="vch")
        kb.wait("vector", "vch")
        kb.op("vector", lambda e: e.tensor_scalar(out=sbuf, in0=sbuf, scalar1=-2.0,
                                                  scalar2=1.0, op0=OP.mult, op1=OP.add))
        kb.op("vector", lambda e: e.tensor_scalar(out=sa, in0=sa, scalar1=EPS_H,
                                                  scalar2=None, op0=OP.add))
        kb.op("vector", lambda e: e.reciprocal(out=sa, in_=sa), inc="vch")
        kb.wait("vector", "vch")
        if mul_radial:
            kb.op("vector", lambda e: e.tensor_scalar(out=sa, in0=sa, scalar1=radl,
                                                      scalar2=None, op0=OP.mult))
        return kb.op("vector", lambda e: e.tensor_tensor(out=dst, in0=sa, in1=sbuf,
                                                         op=OP.mult), inc="ve")

    mk16 = A16.mark(); mk32 = A32.mark()
    WVs = A16.t3(9, 260)
    VEVB = A16.t2(260)
    VEV = A32.t2(260)
    SQ5 = A32.t2(260)
    for ch in range(4):
        c0 = 260 * ch
        kb.wait("gpsimd", "pe")
        for kt in range(9):
            kb.dma("gpsimd", WVs[0:128, kt, :], wvT[:, kt, c0:c0 + 260], "gq")
        kb.wait("tensor", "gq")
        for tb in range(NT):
            pscur = PSA if tb % 2 == 0 else PSB
            for kt in range(8):
                kb.op("tensor", lambda e, kt=kt, tb=tb, p=pscur: e.matmul(
                    out=p[0:128, 0:260], lhsT=XT[0:128, kt, 128 * tb:128 * tb + 128],
                    rhs=WVs[0:128, kt, :], start=(kt == 0), stop=False))
            kb.op("tensor", lambda e, p=pscur: e.matmul(
                out=p[0:128, 0:260], lhsT=onesbf, rhs=WVs[0:1, 8, :],
                start=False, stop=True), inc="pe")
            kb.wait("vector", "pe")
            kb.op("vector", lambda e, p=pscur: e.tensor_copy(out=VEV, in_=p[0:128, 0:260]), inc="ve")
            kb.wait("tensor", "ve")
            kb.op("vector", lambda e: e.tensor_tensor(out=SQ5, in0=VEV, in1=VEV, op=OP.mult))
            for hl in range(4):
                h = 4 * ch + hl
                kb.op("vector", lambda e, hl=hl, tb=tb, h=h: e.tensor_reduce(
                    out=vstk[0:128, 16 * tb + h:16 * tb + h + 1],
                    in_=SQ5[:, 65 * hl:65 * hl + 64], axis=AX, op=OP.add))
            kb.op("vector", lambda e: e.tensor_copy(out=VEVB, in_=VEV), inc="ve")
            kb.wait("sync", "ve")
            kb.dma("sync", vpdb[128 * tb:128 * tb + 128, c0:c0 + 260], VEVB, "sq")
            kb.wait("vector", "sq")
    A16.reset(mk16)
    FCA = A32.t2(512)
    FCB = A32.t2(512)
    kb.wait("scalar", "ve")
    f_chain(vstk, vstk, FCA, FCB)
    A32.reset(mk32)
    kb.barrier()
    if dump == "vstk":
        return _dump([(out[0:128, 0:512], vstk)])
    if dump == "vpdb":
        return _dump([(out[:, :], vpdb[:, 0:1024], "gpsimd")])

    mk32 = A32.mark(); mk16 = A16.mark()
    KQP = A32.t2(T)
    KQPc = KQP.rearrange("p (c n) -> p c n", n=512)
    SQB = A32.t2(T)
    SQBc = SQB.rearrange("p (c n) -> p c n", n=512)
    nstk = A32.t3(2, 512)
    NST = A32.t3(2, 512)
    WQs = A16.t3(8, 64)
    WKs = A16.t3(8, 64)

    def proj_head(h, dst, WQb, WKb):
        kb.wait("gpsimd", "pe")
        for kt in range(8):
            kb.dma("gpsimd", WQb[0:128, kt, :], wqT[:, kt, 64 * h:64 * h + 64], "gq")
            kb.dma("gpsimd", WKb[0:128, kt, :], wkT[:, kt, 64 * h:64 * h + 64], "gq")
        kb.wait("tensor", "gq")
        for c in range(8):
            for qk in range(2):
                W = WKb if qk == 0 else WQb
                pbank = PROJ_PS[(2 * c + qk) % 4]
                for kt in range(8):
                    kb.op("tensor", lambda e, W=W, kt=kt, c=c, p=pbank: e.matmul(
                        out=p[0:64, 0:512], lhsT=W[0:128, kt, :],
                        rhs=XT[0:128, kt, 512 * c:512 * c + 512],
                        start=(kt == 0), stop=(kt == 7)), inc="pe" if kt == 7 else None)
                kb.wait("vector", "pe")
                bias = (bks if qk == 0 else bqs)[64 * (h % 2):64 * (h % 2) + 64, h // 2:h // 2 + 1]
                r0 = 0 if qk == 0 else 64
                kb.op("vector", lambda e, p=pbank, bias=bias, r0=r0, c=c, dst=dst: e.tensor_scalar(
                    out=dst[r0:r0 + 64, 512 * c:512 * c + 512], in0=p[0:64, 0:512],
                    scalar1=bias, scalar2=None, op0=OP.add), inc="ve")
        kb.wait("tensor", "ve")

    for h in range(H):
        proj_head(h, KQP, WQs, WKs)
        kb.op("vector", lambda e: e.tensor_tensor(out=SQB, in0=KQP, in1=KQP, op=OP.mult), inc="ve")
        kb.wait("tensor", "ve")
        for c in range(8):
            pk = NRM_PS[c % 4]
            pq = NRM_PS[(c % 4) ^ 1]
            kb.op("tensor", lambda e, c=c, pk=pk: e.matmul(out=pk, lhsT=onesc[0:64, :],
                                                           rhs=SQBc[0:64, c, :],
                                                           start=True, stop=True), inc="pe")
            kb.op("tensor", lambda e, c=c, pq=pq: e.matmul(out=pq, lhsT=onesc[64:128, :],
                                                           rhs=SQBc[64:128, c, :],
                                                           start=True, stop=True), inc="pe")
            kb.wait("vector", "pe")
            p = h + 16 * c
            kb.op("vector", lambda e, pk=pk: e.tensor_copy(out=NST[0:1, 0, :], in_=pk))
            kb.op("vector", lambda e, pq=pq: e.tensor_copy(out=NST[0:1, 1, :], in_=pq), inc="ve")
            kb.wait("sync", "ve")
            kb.dma("sync", nstk[p:p + 1, 1, 0:512], NST[0:1, 0, :], "sq")
            kb.dma("sync", nstk[p:p + 1, 0, 0:512], NST[0:1, 1, :], "sq")
            kb.wait("tensor", "ve")
            kb.wait("vector", "sq")

    if dump == "nstk":
        return _dump([(out[0:128, 0:512], nstk[:, 0, :]),
                      (out[128:256, 0:512], nstk[:, 1, :])])

    FK = A32.t2(512)
    KN = A32.t2(512)
    FCA = A32.t2(512)
    FCB = A32.t2(512)
    csk = A32.t2(512)
    csq = A32.t2(512)
    rxk = A32.t3(3, 512)
    rxq = A32.t3(3, 512)
    kb.wait("scalar", "ve")
    kb.wait("scalar", "sq")
    for side in range(2):
        f_chain(FK, nstk[:, 1 - side, :], FCA, FCB, mul_radial=(side == 1))
        rx = rxk if side == 0 else rxq
        cs = csk if side == 0 else csq
        kb.op("vector", lambda e: e.tensor_tensor(out=KN, in0=FK, in1=FK, op=OP.mult))
        kb.op("vector", lambda e, side=side: e.tensor_tensor(out=KN, in0=KN,
                                                             in1=nstk[:, 1 - side, :], op=OP.mult))
        kb.op("vector", lambda e, rx=rx: e.tensor_scalar(out=rx[:, 2, :], in0=KN, scalar1=-1.0,
                                                         scalar2=1.0, op0=OP.mult, op1=OP.add))
        kb.op("vector", lambda e, rx=rx: e.tensor_scalar(out=rx[:, 2, :], in0=rx[:, 2, :],
                                                         scalar1=1e-20, scalar2=None, op0=OP.max))
        kb.op("vector", lambda e, rx=rx: e.reciprocal(out=rx[:, 1, :], in_=rx[:, 2, :]), inc="vch")
        kb.wait("vector", "vch")
        kb.op("vector", lambda e, rx=rx: e.memset(rx[:, 2, :], 1.0))
        kb.op("vector", lambda e, rx=rx, cs=cs: e.tensor_tensor(out=cs, in0=FK,
                                                                in1=rx[:, 1, :], op=OP.mult))
        if side == 0:
            kb.op("vector", lambda e, cs=cs: e.tensor_scalar(out=cs, in0=cs, scalar1=-4.0,
                                                             scalar2=None, op0=OP.mult))
        kb.op("vector", lambda e, rx=rx: e.tensor_tensor(out=rx[:, 0, :], in0=KN,
                                                         in1=rx[:, 1, :], op=OP.mult))
        kb.op("vector", lambda e, rx=rx: e.tensor_scalar(out=rx[:, 0, :], in0=rx[:, 0, :],
                                                         scalar1=2.0, scalar2=None, op0=OP.mult), inc="ve")
    kb.wait("sync", "ve")
    kb.dma("sync", stkd[:, 0, :], csk, "sq")
    kb.dma("sync", stkd[:, 1, :], csq, "sq")
    for r in range(3):
        kb.dma("sync", stkd[:, 2 + r, :], rxk[:, r, :], "sq")
        kb.dma("sync", stkd[:, 5 + r, :], rxq[:, r, :], "sq")
    kb.barrier()
    A32.reset(mk32)
    A16.reset(mk16)
    if dump == "stkd":
        return _dump([(out[128 * j:128 * j + 128, 0:512], stkd[:, j, :]) for j in range(8)])

    KQb = [A16.t2(T) for _ in range(2)]
    QQb = [A16.t2(T) for _ in range(2)]
    VPh = A16.t3(NT, 65)
    ATb = [A16.t2(1536) for _ in range(2)]
    OTS = A16.t2(512)
    WQs2 = A16.t3(8, 64)
    WKs2 = A16.t3(8, 64)
    KQP2 = A32.t2(T)
    LxB = [A32.t2(1536) for _ in range(2)]

    def stk_row(slot, h, c):
        return stkd[:, slot, :].rearrange("(j h) c -> h j c", h=16)[h:h + 1, c, 0:512]

    attn_end = {}
    av_last = {0: 0, 1: 0}
    ln_last = {0: 0, 1: 0}

    def prep_head(h):
        hb = h % 2
        KQ_, QQ_ = KQb[hb], QQb[hb]
        if h - 2 in attn_end:
            kb.wait("vector", "pesc", attn_end[h - 2][0])
            kb.wait("gpsimd", "pesc", attn_end[h - 2][0])
        kb.wait("tensor", "seln")
        proj_head(h, KQP2, WQs2, WKs2)
        kb.wait("tensor", "ve")
        kb.wait("tensor", "veot")
        kb.wait("sync", "pe")
        kb.wait("sync", "ve")
        rbuf = [rowst, dinv]
        pslot = [PMI[0:64, 0:512], PAV[0:64, 0:512]]
        pe_mk, ve_mk = {}, {}
        for side in range(2):
            src_ = KQP2[0:64, :] if side == 0 else KQP2[64:128, :]
            dst_ = KQ_ if side == 0 else QQ_
            for c in range(8):
                k = 8 * side + c
                kbuf = k % 2
                if k >= 2:
                    kb.wait("sync", "pe", pe_mk[k - 2])
                vk = kb.dma("sync", rbuf[kbuf], stk_row(side, h, c), "sq")
                kb.wait("tensor", "sq", vk)
                if k >= 2:
                    kb.wait("tensor", "ve", ve_mk[k - 2])
                pe_mk[k] = kb.op("tensor", lambda e, kbuf=kbuf: e.matmul(
                    out=pslot[kbuf], lhsT=onesb[0:1, 0:64],
                    rhs=rbuf[kbuf], start=True, stop=True), inc="pe")
                kb.wait("vector", "pe", pe_mk[k])
                ve_mk[k] = kb.op("vector", lambda e, c=c, kbuf=kbuf, dst_=dst_, src_=src_: e.tensor_tensor(
                    out=dst_[0:64, 512 * c:512 * c + 512],
                    in0=src_[:, 512 * c:512 * c + 512], in1=pslot[kbuf],
                    op=OP.mult), inc="ve")
        kv = None
        for r in range(3):
            kb.dma("gpsimd",
                   KQ_[64 + r:65 + r, :].rearrange("p (j c) -> p j c", c=512),
                   stkd[:, 2 + r, :].rearrange("(j h) c -> h j c", h=16)[h:h + 1, :, :], "gq")
            kv = kb.dma("gpsimd",
                        QQ_[64 + r:65 + r, :].rearrange("p (j c) -> p j c", c=512),
                        stkd[:, 5 + r, :].rearrange("(j h) c -> h j c", h=16)[h:h + 1, :, :], "gq")
        return kv

    def load_v(h):
        kb.wait("gpsimd", "peav")
        gv = kb.dma("gpsimd", VPh[:, :, :],
                    vpdb[:, 65 * h:65 * h + 65].rearrange("(tb p) c -> p tb c", p=128), "gq")
        return gv

    def attn_head(h, prep_mark, v_mark):
        hb = h % 2
        KQ_, QQ_ = KQb[hb], QQb[hb]
        kb.wait("tensor", "gq", prep_mark)
        kb.wait("vector", "gq", v_mark)
        if h - 1 in attn_end:
            kb.wait("vector", "peav", attn_end[h - 1][1])
        for tb in range(NT):
            kb.op("vector", lambda e, tb=tb, h=h: e.tensor_scalar(
                out=VPh[:, tb, 0:64], in0=VPh[:, tb, 0:64],
                scalar1=vstk[0:128, 16 * tb + h:16 * tb + h + 1], scalar2=None,
                op0=OP.mult), inc="vfv")
        kb.wait("tensor", "vfv")
        def issue_scores(lc, gi, s0, s1):
            psg = PSA if gi % 2 == 0 else PSB
            kb.wait("tensor", "seln", ln_last[gi % 2])
            for i in range(s0, s1):
                j = i - s0
                kb.op("tensor", lambda e, i=i, j=j, p=psg, KQ_=KQ_, QQ_=QQ_, lc=lc: e.matmul(
                    out=p[0:128, 512 * j:512 * j + 512],
                    lhsT=KQ_[0:67, 128 * i:128 * i + 128],
                    rhs=QQ_[0:67, 512 * lc:512 * lc + 512], start=True, stop=True),
                    inc="pesc" if i == s1 - 1 else None)
            return kb.cnt["pesc"]

        for lc in range(NL):
            sc_cnt = {0: issue_scores(lc, 0, *GRPS[0])}
            for gi, (s0, s1) in enumerate(GRPS):
                psg = PSA if gi % 2 == 0 else PSB
                nst = s1 - s0
                if gi + 1 < len(GRPS):
                    sc_cnt[gi + 1] = issue_scores(lc, gi + 1, *GRPS[gi + 1])
                kb.wait("scalar", "pesc", sc_cnt[gi])
                kb.op("scalar", lambda e, p=psg, nst=nst, gi=gi: e.activation(
                    out=LxB[gi % 2][:, 0:512 * nst], in_=p[0:128, 0:512 * nst],
                    func=AF.Ln), inc="seln")
                ln_last[gi % 2] = kb.cnt["seln"]
                kb.wait("scalar", "peav", av_last[gi % 2])
                ex = kb.op("scalar", lambda e, nst=nst, gi=gi: e.activation(
                    out=ATb[gi % 2][:, 0:512 * nst], in_=LxB[gi % 2][:, 0:512 * nst],
                    func=AF.Exp, scale=nalpha), inc="seexp")
                if gi == 0:
                    kb.wait("tensor", "veot")
                kb.wait("tensor", "seexp", ex)
                for i in range(s0, s1):
                    j = i - s0
                    kb.op("tensor", lambda e, i=i, j=j, gi=gi: e.matmul(
                        out=PAV[0:65, 0:512], lhsT=VPh[:, i, 0:65],
                        rhs=ATb[gi % 2][:, 512 * j:512 * j + 512],
                        start=(i == 0), stop=(i == NS - 1)),
                        inc="peav" if i == s1 - 1 else None)
                av_last[gi % 2] = kb.cnt["peav"]
            kb.wait("vector", "peav")
            kb.op("vector", lambda e: e.reciprocal(out=dinv, in_=PAV[64:65, 0:512]), inc="ve")
            kb.wait("tensor", "ve")
            kb.op("tensor", lambda e: e.matmul(out=PMI[0:64, 0:512], lhsT=onesb[0:1, 0:64],
                                               rhs=dinv, start=True, stop=True), inc="pe")
            kb.wait("vector", "pe")
            kb.op("vector", lambda e: e.tensor_copy(out=dbc, in_=PMI[0:64, 0:512]), inc="ve")
            kb.op("vector", lambda e: e.tensor_tensor(out=OTS[0:64, :], in0=PAV[0:64, 0:512],
                                                      in1=dbc, op=OP.mult), inc="veot")
            kb.wait("sync", "veot")
            kb.dma("sync", otdb[h // 2, 64 * (h % 2):64 * (h % 2) + 64, 512 * lc:512 * lc + 512],
                   OTS[0:64, :], "sq")
            kb.wait("vector", "sq")
        attn_end[h] = (kb.cnt["pesc"], kb.cnt["peav"], kb.cnt["seexp"])

    pm = prep_head(0)
    vm = load_v(0)
    for h in range(H):
        if h + 1 < H:
            nxt = prep_head(h + 1)
        else:
            nxt = None
        attn_head(h, pm, vm)
        if h + 1 < H:
            vm = load_v(h + 1)
        pm = nxt
    kb.barrier()
    A16.reset(markA16)
    A32.reset(markA32)
    if dump == "otdb":
        return _dump([(out[128 * j:128 * j + 128, :], otdb[j, :, 0:1024], "gpsimd")
                      for j in range(8)])

    WOc = A16.t3(8, D)
    WBc = A16.t3(8, D)
    OTL = [A16.t3(8, 128) for _ in range(2)]
    XST = [A32.t2(D) for _ in range(2)]
    YB = A32.t2(D)
    X1B = A32.t2(D)
    SQy = A32.t2(D)
    browS = A32.ap[0:1, A32.cur:A32.cur + 2 * D]; A32.cur += 2 * D
    wrbsS = A32.t2(D)
    kb.dma("gpsimd", WOc[:, :, :], woT, "gq")
    kb.dma("gpsimd", WBc[:, :, :], wbT, "gq")
    kb.dma("sync", browS, brows, "sq")
    kb.dma("sync", wrbsS, wrb, "sq")
    kb.wait("tensor", "gq")
    kb.wait("tensor", "sq")

    for lb in range(NT):
        bb_ = lb % 2
        kb.wait("sync", "pe")
        kb.wait("sync", "ve")
        for j in range(8):
            kb.dma("sync", OTL[bb_][:, j, :], otdb[j, :, 128 * lb:128 * lb + 128], "sq")
        kb.dma("sync", XST[bb_], xrow[128 * lb:128 * lb + 128, :], "sq")
        kb.wait("tensor", "sq")
        for jc in range(2):
            for j in range(8):
                kb.op("tensor", lambda e, j=j, jc=jc, bb_=bb_: e.matmul(
                    out=PSA[0:128, 512 * jc:512 * jc + 512], lhsT=OTL[bb_][:, j, :],
                    rhs=WOc[:, j, 512 * jc:512 * jc + 512], start=(j == 0), stop=False))
            for kt in range(8):
                kb.op("tensor", lambda e, kt=kt, jc=jc, lb=lb: e.matmul(
                    out=PSA[0:128, 512 * jc:512 * jc + 512],
                    lhsT=XT[0:128, kt, 128 * lb:128 * lb + 128],
                    rhs=WBc[0:128, kt, 512 * jc:512 * jc + 512], start=False, stop=False))
            kb.op("tensor", lambda e, jc=jc: e.matmul(
                out=PSA[0:128, 512 * jc:512 * jc + 512], lhsT=onesb,
                rhs=browS[0:1, 512 * jc:512 * jc + 512], start=False, stop=False))
            kb.op("tensor", lambda e, jc=jc: e.matmul(
                out=PSA[0:128, 512 * jc:512 * jc + 512], lhsT=onesb,
                rhs=browS[0:1, D + 512 * jc:D + 512 * jc + 512], start=False, stop=True),
                inc="pe")
        kb.wait("vector", "pe")
        kb.op("vector", lambda e, bb_=bb_: e.scalar_tensor_tensor(
            out=YB, in0=XST[bb_], scalar=1.0, in1=PSA[0:128, 0:1024],
            op0=OP.mult, op1=OP.add), inc="ve")
        kb.wait("tensor", "ve")
        kb.op("vector", lambda e: e.tensor_reduce(out=stat[:, 0:1], in_=YB, axis=AX, op=OP.add))
        kb.op("vector", lambda e: e.tensor_tensor(out=SQy, in0=YB, in1=YB, op=OP.mult))
        kb.op("vector", lambda e: e.tensor_reduce(out=stat[:, 1:2], in_=SQy, axis=AX, op=OP.add))
        kb.op("vector", lambda e: e.tensor_scalar(out=stat[:, 2:3], in0=stat[:, 0:1],
                                                  scalar1=1.0 / D, scalar2=None, op0=OP.mult))
        kb.op("vector", lambda e: e.tensor_scalar(out=stat[:, 4:5], in0=stat[:, 1:2],
                                                  scalar1=1.0 / D, scalar2=None,
                                                  op0=OP.mult), inc="vch")
        kb.wait("vector", "vch")
        kb.op("vector", lambda e: e.tensor_tensor(out=stat[:, 3:4], in0=stat[:, 2:3],
                                                  in1=stat[:, 2:3], op=OP.mult), inc="vch")
        kb.wait("vector", "vch")
        v1 = kb.op("vector", lambda e: e.tensor_tensor(out=stat[:, 5:6], in0=stat[:, 4:5],
                                                       in1=stat[:, 3:4], op=OP.subtract), inc="ve")
        kb.wait("scalar", "ve", v1)
        kb.op("scalar", lambda e: e.activation(out=stat[:, 6:7], in_=stat[:, 5:6],
                                               func=AF.Ln, bias=epsln), inc="sch")
        kb.wait("scalar", "sch")
        s1_ = kb.op("scalar", lambda e: e.activation(out=stat[:, 7:8], in_=stat[:, 6:7],
                                                     func=AF.Exp, scale=-0.5), inc="se")
        kb.wait("vector", "se", s1_)
        kb.op("vector", lambda e: e.tensor_scalar(out=stat[:, 8:9], in0=stat[:, 2:3],
                                                  scalar1=-1.0, scalar2=None,
                                                  op0=OP.mult), inc="vch")
        kb.wait("vector", "vch")
        kb.op("vector", lambda e: e.tensor_scalar(out=X1B, in0=YB, scalar1=stat[:, 8:9],
                                                  scalar2=stat[:, 7:8], op0=OP.add, op1=OP.mult))
        kb.op("vector", lambda e: e.tensor_tensor(out=SQy, in0=X1B, in1=wrbsS, op=OP.mult))
        kb.op("vector", lambda e, lb=lb: e.tensor_reduce(out=sc32[:, lb:lb + 1], in_=SQy,
                                                         axis=AX, op=OP.add), inc="ve")
        kb.wait("sync", "ve")
        kb.dma("sync", x1d[128 * lb:128 * lb + 128, :], X1B, "sq")
        if stage == 1:
            kb.dma("sync", out[128 * lb:128 * lb + 128, :], X1B, "sq")
        kb.wait("vector", "sq")
    if stage == 1:
        _finalize(kb, nc)
        return nc
    kb.barrier()
    A16.reset(0)
    A32.reset(markA32)

    sbc_mark = A32.mark()
    SBC = A32.t2(T)
    ED = A32.t2(128)
    EM = A32.t2(128)

    def bcast_row(src32, dst, rd):
        kb.wait("sync", "ve")
        kb.dma("sync", rd[:, :], src32, "sq")
        kb.wait("sync", "sq")
        for c in range(8):
            kb.dma("sync", rowst.rearrange("r (a p) -> r a p", p=128),
                   rd.rearrange("p c -> () c p")[0:1, 4 * c:4 * c + 4, :], "bcr",
                   allow_slow_non_contiguous=True)
            kb.dma("sync", small[0:1, 15:16], curv, "bfl")
            kb.wait("tensor", "bfl")
            kb.op("tensor", lambda e: e.matmul(out=PMI[0:128, 0:512], lhsT=onesb,
                                               rhs=rowst, start=True, stop=True), inc="pe")
            kb.wait("vector", "pe")
            kb.op("vector", lambda e, c=c, dst=dst: e.tensor_copy(
                out=dst[:, 512 * c:512 * c + 512], in_=PMI[0:128, 0:512]), inc="ve")
            kb.wait("tensor", "ve")
            kb.wait("sync", "ve")

    bcast_row(sc32, SBC, rowd)
    if dump == "sbc":
        return _dump([(out[0:128, 0:32], sc32)] +
                     [(out[128 * (1 + k):128 * (2 + k), :], SBC[:, 1024 * k:1024 * k + 1024])
                      for k in range(4)])
    RTMP = A32.t2(T)
    for c in range(NT):
        kb.op("vector", lambda e, c=c: e.tensor_scalar(
            out=ED, in0=SBC[:, 128 * c:128 * c + 128], scalar1=sc32[:, c:c + 1],
            scalar2=None, op0=OP.is_equal))
        kb.op("vector", lambda e: e.tensor_tensor(out=EM, in0=ED, in1=lt, op=OP.mult))
        kb.op("vector", lambda e: e.tensor_reduce(out=stat[:, 2:3], in_=EM, axis=AX, op=OP.add))
        kb.op("vector", lambda e, c=c: e.tensor_scalar(
            out=RTMP, in0=SBC, scalar1=sc32[:, c:c + 1], scalar2=0.0,
            op0=OP.is_gt, op1=OP.add, accum_out=stat[:, 0:1]),
            inc="vch" if c == 0 else None)
        if c > 0:
            kb.op("vector", lambda e, c=c: e.tensor_scalar(
                out=RTMP[:, 0:128 * c], in0=SBC[:, 0:128 * c], scalar1=sc32[:, c:c + 1],
                scalar2=0.0, op0=OP.is_equal, op1=OP.add, accum_out=stat[:, 1:2]), inc="vch")
        kb.wait("vector", "vch")
        if c > 0:
            kb.op("vector", lambda e: e.tensor_tensor(out=stat[:, 0:1], in0=stat[:, 0:1],
                                                      in1=stat[:, 1:2], op=OP.add), inc="vch")
            kb.wait("vector", "vch")
        kb.op("vector", lambda e: e.tensor_tensor(out=stat[:, 0:1], in0=stat[:, 0:1],
                                                  in1=stat[:, 2:3], op=OP.add), inc="vch")
        kb.wait("vector", "vch")
        kb.op("vector", lambda e, c=c: e.tensor_copy(out=rk32[:, c:c + 1], in_=stat[:, 0:1]), inc="ve")
    A32.reset(sbc_mark)
    RBC = A32.t2(T)
    bcast_row(rk32, RBC, rowd2)
    if dump == "ranks":
        return _dump([(out[0:128, 0:32], rk32)] +
                     [(out[128 * (1 + k):128 * (2 + k), :], RBC[:, 1024 * k:1024 * k + 1024])
                      for k in range(4)])

    PTt = [A16.t2(CAP) for _ in range(2)]
    X1bf = [A16.t2(D) for _ in range(2)]
    X1fs = [A32.t2(D) for _ in range(2)]
    XSEL = A32.t3(4, D)
    GPS = [PSA[:, 0:512], PSA[:, 512:1024], PSA[:, 1024:1536], PSB[:, 0:512],
           PSB[:, 512:1024], PSB[:, 1024:1536], PAV[:, 0:512], PMI[:, 0:512]]
    gthm = {}
    for c in range(NT):
        cb = c % 2
        if c >= 2:
            kb.wait("vector", "pegth", gthm[c - 2])
        kb.wait("sync", "ve")
        kb.dma("sync", X1fs[cb], x1d[128 * c:128 * c + 128, :], "sq")
        kb.wait("vector", "sq")
        kb.op("vector", lambda e, cb=cb: e.tensor_copy(out=X1bf[cb], in_=X1fs[cb]))
        kb.op("vector", lambda e, c=c, cb=cb: e.tensor_scalar(
            out=PTt[cb], in0=io512, scalar1=rk32[:, c:c + 1], scalar2=None,
            op0=OP.is_equal), inc="ve")
        kb.wait("tensor", "ve")
        for rb in range(4):
            for ec in range(2):
                last = (rb == 3 and ec == 1)
                kb.op("tensor", lambda e, c=c, cb=cb, rb=rb, ec=ec: e.matmul(
                    out=GPS[2 * rb + ec][0:128, :], lhsT=PTt[cb][:, 128 * rb:128 * rb + 128],
                    rhs=X1bf[cb][:, 512 * ec:512 * ec + 512],
                    start=(c == 0), stop=(c == NT - 1)), inc="pegth" if last else None)
        gthm[c] = kb.cnt["pegth"]
    kb.wait("vector", "pegth", gthm[NT - 1])
    for rb in range(4):
        for ec in range(2):
            kb.op("vector", lambda e, rb=rb, ec=ec: e.tensor_copy(
                out=XSEL[:, rb, 512 * ec:512 * ec + 512], in_=GPS[2 * rb + ec][0:128, :]), inc="ve")
    kb.wait("tensor", "ve")

    H1F = A32.t3(4, D)
    XATT = A32.t3(4, D)
    SQyB = A32.t2(D)
    MSKS = A32.t2(512)
    H1T = A16.t3(9, CAP)
    QKT = A16.t3(16, CAP)
    VIB = A16.t3(4, H * 65)
    IAT = A16.t3(4, 512)
    OTi = [A16.t2(CAP) for _ in range(8)]
    WS1 = A16.t3(8, 512)
    WS2 = A16.t3(9, 512)
    A1B = A16.t2(512)
    ABT = A16.t3(22, CAP)
    DELB = A16.t3(4, D)
    MKB = A16.t3(4, 512)
    g1bS = A16.t2(D)
    g2bS = A16.t2(D)
    kb.dma("gpsimd", MKB[:, :, :], maskbd, "gq")
    kb.dma("gpsimd", g1bS, g1bi, "gq")
    kb.dma("gpsimd", g2bS, g2bi, "gq")

    def rmsnorm(dst3, src3, gs):
        for rt in range(4):
            kb.op("vector", lambda e, rt=rt: e.tensor_tensor(out=SQyB, in0=src3[:, rt, :],
                                                             in1=src3[:, rt, :], op=OP.mult))
            kb.op("vector", lambda e: e.tensor_reduce(out=stat[:, 0:1], in_=SQyB,
                                                      axis=AX, op=OP.add), inc="vch")
            kb.wait("vector", "vch")
            r = kb.op("vector", lambda e: e.tensor_scalar(out=stat[:, 1:2], in0=stat[:, 0:1],
                                                          scalar1=1.0 / D, scalar2=EPS_RMS,
                                                          op0=OP.mult, op1=OP.add), inc="ve")
            kb.wait("scalar", "ve", r)
            kb.op("scalar", lambda e: e.activation(out=stat[:, 2:3], in_=stat[:, 1:2],
                                                   func=AF.Ln), inc="sch")
            kb.wait("scalar", "sch")
            s2 = kb.op("scalar", lambda e: e.activation(out=stat[:, 3:4], in_=stat[:, 2:3],
                                                        func=AF.Exp, scale=-0.5), inc="se")
            kb.wait("vector", "se", s2)
            kb.op("vector", lambda e, rt=rt: e.tensor_scalar(out=SQyB, in0=src3[:, rt, :],
                                                             scalar1=stat[:, 3:4], scalar2=None,
                                                             op0=OP.mult))
            kb.op("vector", lambda e, rt=rt, dst3=dst3, gs=gs: e.tensor_tensor(
                out=dst3[:, rt, :], in0=SQyB, in1=gs, op=OP.mult), inc="ve")

    def transpose_to(dst3, src3, nkt):
        kb.wait("tensor", "ve")
        for rt in range(4):
            for eh in range(nkt):
                kb.op("tensor", lambda e, rt=rt, eh=eh: e.transpose(
                    out=PMI[0:128, 0:128], in_=src3[0:128, rt, 128 * eh:128 * eh + 128],
                    identity=eyeb), inc="pe")
                kb.wait("vector", "pe")
                kb.op("vector", lambda e, rt=rt, eh=eh: e.tensor_copy(
                    out=dst3[:, eh, 128 * rt:128 * rt + 128], in_=PMI[0:128, 0:128]), inc="ve")
                kb.wait("tensor", "ve")

    if dump == "x1d":
        return _dump([(out[:, :], x1d[:, :])])
    if dump == "xsel":
        return _dump([(out[128 * rt:128 * rt + 128, :], XSEL[:, rt, :]) for rt in range(4)])
    kb.wait("vector", "gq")
    rmsnorm(H1F, XSEL, g1bS)
    if dump == "h1":
        return _dump([(out[128 * rt:128 * rt + 128, :], H1F[:, rt, :]) for rt in range(4)])
    kb.op("gpsimd", lambda e: e.memset(H1T[0:1, 8, :], 1.0), inc="gp")
    kb.wait("tensor", "gp")
    transpose_to(H1T, H1F, 8)
    for dch in range(4):
        kb.wait("gpsimd", "pe")
        for kt in range(8):
            kb.dma("gpsimd", WS1[0:128, kt, :], wqkT[:, kt, 512 * dch:512 * dch + 512], "gq")
        kb.wait("tensor", "gq")
        for db in range(4):
            for kt in range(8):
                kb.op("tensor", lambda e, db=db, kt=kt: e.matmul(
                    out=PAV[0:128, 0:512], lhsT=WS1[0:128, kt, 128 * db:128 * db + 128],
                    rhs=H1T[:, kt, :], start=(kt == 0), stop=(kt == 7)),
                    inc="pe" if kt == 7 else None)
            kb.wait("vector", "pe")
            kb.op("vector", lambda e, dch=dch, db=db: e.tensor_copy(
                out=QKT[:, 4 * dch + db, :], in_=PAV[0:128, 0:512]), inc="ve")
            kb.wait("tensor", "ve")
    for ch in range(4):
        c0 = 260 * ch
        kb.wait("gpsimd", "pe")
        for kt in range(9):
            kb.dma("gpsimd", WS2[0:128, kt, 0:260], wvfT[:, kt, c0:c0 + 260], "gq")
        kb.wait("tensor", "gq")
        for rt in range(4):
            for kt in range(8):
                kb.op("tensor", lambda e, rt=rt, kt=kt: e.matmul(
                    out=PAV[0:128, 0:260], lhsT=H1T[:, kt, 128 * rt:128 * rt + 128],
                    rhs=WS2[0:128, kt, 0:260], start=(kt == 0), stop=False))
            kb.op("tensor", lambda e, rt=rt: e.matmul(
                out=PAV[0:128, 0:260], lhsT=H1T[0:1, 8, 128 * rt:128 * rt + 128],
                rhs=WS2[0:1, 8, 0:260], start=False, stop=True), inc="pe")
            kb.wait("vector", "pe")
            kb.op("vector", lambda e, rt=rt, c0=c0: e.tensor_copy(
                out=VIB[:, rt, c0:c0 + 260], in_=PAV[0:128, 0:260]), inc="ve")
            kb.wait("tensor", "ve")
    for h in range(H):
        qblk, qrow = h // 2, 64 * (h % 2)
        kblk, krow = 8 + h // 2, 64 * (h % 2)
        if h > 0:
            kb.wait("scalar", "peav")
        for st in range(4):
            psl = PSA[0:128, 512 * st:512 * st + 512] if st < 3 else PSB[0:128, 0:512]
            kb.op("tensor", lambda e, st=st, psl=psl, qblk=qblk, qrow=qrow, kblk=kblk, krow=krow: e.matmul(
                out=psl, lhsT=QKT[krow:krow + 64, kblk, 128 * st:128 * st + 128],
                rhs=QKT[qrow:qrow + 64, qblk, 0:512], start=True, stop=True), inc="pesc")
        kb.wait("vector", "pesc")
        for st in range(4):
            psl = PSA[0:128, 512 * st:512 * st + 512] if st < 3 else PSB[0:128, 0:512]
            if st > 0:
                kb.wait("vector", "seexp")
            vv = kb.op("vector", lambda e, st=st, psl=psl: e.scalar_tensor_tensor(
                out=MSKS, in0=psl, scalar=0.125, in1=MKB[:, st, :],
                op0=OP.mult, op1=OP.add), inc="ve")
            kb.wait("scalar", "ve", vv)
            kb.op("scalar", lambda e, st=st: e.activation(out=IAT[:, st, :], in_=MSKS,
                                                          func=AF.Exp), inc="seexp")
        kb.wait("tensor", "seexp")
        for st in range(4):
            kb.op("tensor", lambda e, st=st, h=h: e.matmul(
                out=PAV[0:65, 0:512], lhsT=VIB[:, st, 65 * h:65 * h + 65],
                rhs=IAT[:, st, :], start=(st == 0), stop=(st == 3)),
                inc="peav" if st == 3 else None)
        kb.wait("vector", "peav")
        kb.op("vector", lambda e: e.reciprocal(out=dinv, in_=PAV[64:65, 0:512]), inc="ve")
        kb.wait("tensor", "ve")
        kb.op("tensor", lambda e: e.matmul(out=PMI[0:64, 0:512], lhsT=onesb[0:1, 0:64],
                                           rhs=dinv, start=True, stop=True), inc="pe")
        kb.wait("vector", "pe")
        kb.op("vector", lambda e: e.tensor_copy(out=dbc, in_=PMI[0:64, 0:512]), inc="ve")
        kb.op("vector", lambda e, h=h: e.tensor_tensor(
            out=OTi[h // 2][64 * (h % 2):64 * (h % 2) + 64, :], in0=PAV[0:64, 0:512],
            in1=dbc, op=OP.mult), inc="veot")
        kb.wait("tensor", "veot")
    for jc in range(2):
        kb.wait("gpsimd", "pe")
        for kt in range(8):
            kb.dma("gpsimd", WS1[0:128, kt, :], woutT[:, kt, 512 * jc:512 * jc + 512], "gq")
        kb.wait("tensor", "gq")
        for rt in range(4):
            for j in range(8):
                kb.op("tensor", lambda e, rt=rt, j=j: e.matmul(
                    out=PAV[0:128, 0:512], lhsT=OTi[j][:, 128 * rt:128 * rt + 128],
                    rhs=WS1[0:128, j, :], start=(j == 0), stop=(j == 7)),
                    inc="pe" if j == 7 else None)
            kb.wait("vector", "pe")
            kb.op("vector", lambda e, rt=rt, jc=jc: e.tensor_copy(
                out=XATT[:, rt, 512 * jc:512 * jc + 512], in_=PAV[0:128, 0:512]), inc="ve")
            kb.wait("tensor", "ve")
    if dump == "xattn":
        return _dump([(out[128 * rt:128 * rt + 128, :], XATT[:, rt, :]) for rt in range(4)])
    for rt in range(4):
        kb.op("vector", lambda e, rt=rt: e.tensor_tensor(out=XATT[:, rt, :], in0=XATT[:, rt, :],
                                                         in1=XSEL[:, rt, :], op=OP.add), inc="ve")
    rmsnorm(H1F, XATT, g2bS)
    if dump == "h2":
        return _dump([(out[128 * rt:128 * rt + 128, :], H1F[:, rt, :]) for rt in range(4)])
    transpose_to(H1T, H1F, 8)
    ABQ = H1F
    FCH = [(0, 1024), (1024, 2048), (2048, 2730)]
    for f0, f1 in FCH:
        for sub in range((f1 - f0 + 511) // 512):
            s0f = f0 + 512 * sub
            nn = min(512, f1 - s0f)
            kb.wait("gpsimd", "pe")
            for kt in range(8):
                kb.dma("gpsimd", WS1[0:128, kt, 0:nn], w1T[:, kt, s0f:s0f + nn], "gq")
                kb.dma("gpsimd", WS2[0:128, kt, 0:nn], w2T[:, kt, s0f:s0f + nn], "gq")
            kb.wait("tensor", "gq")
            for rt in range(4):
                for kt in range(8):
                    kb.op("tensor", lambda e, rt=rt, kt=kt, nn=nn: e.matmul(
                        out=PAV[0:128, 0:nn], lhsT=H1T[:, kt, 128 * rt:128 * rt + 128],
                        rhs=WS1[0:128, kt, 0:nn], start=(kt == 0), stop=(kt == 7)))
                for kt in range(8):
                    kb.op("tensor", lambda e, rt=rt, kt=kt, nn=nn: e.matmul(
                        out=PMI[0:128, 0:nn], lhsT=H1T[:, kt, 128 * rt:128 * rt + 128],
                        rhs=WS2[0:128, kt, 0:nn], start=(kt == 0), stop=(kt == 7)),
                        inc="pe" if kt == 7 else None)
                kb.wait("scalar", "pe")
                kb.op("scalar", lambda e, nn=nn: e.activation(out=A1B[:, 0:nn],
                                                              in_=PAV[0:128, 0:nn],
                                                              func=AF.Silu), inc="se")
                kb.wait("vector", "se")
                kb.op("vector", lambda e, rt=rt, nn=nn, s0f=s0f, f0=f0: e.tensor_tensor(
                    out=ABQ[:, rt, s0f - f0:s0f - f0 + nn], in0=A1B[:, 0:nn],
                    in1=PMI[0:128, 0:nn], op=OP.mult), inc="ve")
                kb.wait("tensor", "ve")
        for rt in range(4):
            for eh in range((f1 - f0 + 127) // 128):
                gkt = (f0 + 128 * eh) // 128
                ew = min(128, f1 - f0 - 128 * eh)
                kb.op("tensor", lambda e, rt=rt, eh=eh, ew=ew: e.transpose(
                    out=PMI[0:ew, 0:128], in_=ABQ[0:128, rt, 128 * eh:128 * eh + ew],
                    identity=eyeb), inc="pe")
                kb.wait("vector", "pe")
                kb.op("vector", lambda e, rt=rt, gkt=gkt, ew=ew: e.tensor_copy(
                    out=ABT[0:ew, gkt, 128 * rt:128 * rt + 128], in_=PMI[0:ew, 0:128]), inc="ve")
                kb.wait("tensor", "ve")
    w3_mark = A16.mark()
    W3s = [A16.t2(D) for _ in range(2)]
    xfm = {}
    for kt in range(22):
        kw = min(128, DFF - 128 * kt)
        if kt >= 2:
            kb.wait("gpsimd", "pexf", xfm[kt - 2])
        kb.dma("gpsimd", W3s[kt % 2], w3T[:, kt, :], "gq")
        kb.wait("tensor", "gq")
        for rt in range(4):
            for jc in range(2):
                last = (rt == 3 and jc == 1)
                kb.op("tensor", lambda e, rt=rt, jc=jc, kt=kt, kw=kw: e.matmul(
                    out=GPS[2 * rt + jc][0:128, :], lhsT=ABT[0:kw, kt, 128 * rt:128 * rt + 128],
                    rhs=W3s[kt % 2][0:kw, 512 * jc:512 * jc + 512],
                    start=(kt == 0), stop=(kt == 21)), inc="pexf" if last else None)
        xfm[kt] = kb.cnt["pexf"]
    kb.wait("vector", "pexf", xfm[21])
    for rt in range(4):
        kb.op("vector", lambda e, rt=rt: e.tensor_tensor(
            out=XATT[:, rt, :], in0=XATT[:, rt, :], in1=XSEL[:, rt, :], op=OP.subtract))
        kb.op("vector", lambda e, rt=rt: e.scalar_tensor_tensor(
            out=XATT[:, rt, 0:512], in0=XATT[:, rt, 0:512], scalar=1.0,
            in1=GPS[2 * rt + 0][0:128, :], op0=OP.mult, op1=OP.add))
        kb.op("vector", lambda e, rt=rt: e.scalar_tensor_tensor(
            out=XATT[:, rt, 512:1024], in0=XATT[:, rt, 512:1024], scalar=1.0,
            in1=GPS[2 * rt + 1][0:128, :], op0=OP.mult, op1=OP.add))
        kb.op("vector", lambda e, rt=rt: e.tensor_copy(out=DELB[:, rt, :], in_=XATT[:, rt, :]), inc="ve")
    kb.wait("tensor", "ve")
    if dump == "delb":
        return _dump([(out[128 * rt:128 * rt + 128, :], XATT[:, rt, :]) for rt in range(4)])
    if dump == "qkt":
        return _dump([(out[128 * i:128 * i + 128, 0:512], QKT[:, i, :], "gpsimd")
                      for i in range(16)])

    A16.reset(w3_mark)
    PRB = A16.t3(4, 128)
    Y2B = A32.t2(D)
    for li in range(NT):
        lb = li
        cb = li % 2
        kb.wait("sync", "ve")
        kb.dma("sync", X1fs[cb], x1d[128 * lb:128 * lb + 128, :], "sq")
        kb.wait("vector", "pe")
        for rb in range(4):
            kb.op("vector", lambda e, lb=lb, rb=rb: e.tensor_scalar(
                out=PRB[:, rb, :], in0=RBC[:, 128 * lb:128 * lb + 128],
                scalar1=iorb[:, rb:rb + 1], scalar2=None, op0=OP.is_equal), inc="ve")
        kb.wait("tensor", "ve")
        for jc in range(2):
            for rb in range(4):
                kb.op("tensor", lambda e, rb=rb, jc=jc: e.matmul(
                    out=PAV[0:128, 0:512], lhsT=PRB[:, rb, :],
                    rhs=DELB[0:128, rb, 512 * jc:512 * jc + 512],
                    start=(rb == 0), stop=(rb == 3)), inc="pe" if rb == 3 else None)
            kb.wait("vector", "pe")
            kb.wait("vector", "sq")
            kb.op("vector", lambda e, jc=jc, cb=cb: e.scalar_tensor_tensor(
                out=Y2B[:, 512 * jc:512 * jc + 512], in0=X1fs[cb][:, 512 * jc:512 * jc + 512],
                scalar=2.0, in1=PAV[0:128, 0:512], op0=OP.mult, op1=OP.add), inc="ve")
            kb.wait("tensor", "ve")
        kb.op("vector", lambda e: e.tensor_reduce(out=stat[:, 0:1], in_=Y2B, axis=AX, op=OP.add))
        kb.op("vector", lambda e: e.tensor_tensor(out=SQyB, in0=Y2B, in1=Y2B, op=OP.mult))
        kb.op("vector", lambda e: e.tensor_reduce(out=stat[:, 1:2], in_=SQyB, axis=AX, op=OP.add))
        kb.op("vector", lambda e: e.tensor_scalar(out=stat[:, 2:3], in0=stat[:, 0:1],
                                                  scalar1=1.0 / D, scalar2=None, op0=OP.mult))
        kb.op("vector", lambda e: e.tensor_scalar(out=stat[:, 4:5], in0=stat[:, 1:2],
                                                  scalar1=1.0 / D, scalar2=None,
                                                  op0=OP.mult), inc="vch")
        kb.wait("vector", "vch")
        kb.op("vector", lambda e: e.tensor_tensor(out=stat[:, 3:4], in0=stat[:, 2:3],
                                                  in1=stat[:, 2:3], op=OP.mult), inc="vch")
        kb.wait("vector", "vch")
        v1 = kb.op("vector", lambda e: e.tensor_tensor(out=stat[:, 5:6], in0=stat[:, 4:5],
                                                       in1=stat[:, 3:4], op=OP.subtract), inc="ve")
        kb.wait("scalar", "ve", v1)
        kb.op("scalar", lambda e: e.activation(out=stat[:, 6:7], in_=stat[:, 5:6],
                                               func=AF.Ln, bias=epsln), inc="sch")
        kb.wait("scalar", "sch")
        s1_ = kb.op("scalar", lambda e: e.activation(out=stat[:, 7:8], in_=stat[:, 6:7],
                                                     func=AF.Exp, scale=-0.5), inc="se")
        kb.wait("vector", "se", s1_)
        kb.op("vector", lambda e: e.tensor_scalar(out=stat[:, 8:9], in0=stat[:, 2:3],
                                                  scalar1=-1.0, scalar2=None,
                                                  op0=OP.mult), inc="vch")
        kb.wait("vector", "vch")
        kb.op("vector", lambda e: e.tensor_scalar(out=Y2B, in0=Y2B, scalar1=stat[:, 8:9],
                                                  scalar2=stat[:, 7:8], op0=OP.add, op1=OP.mult), inc="ve")
        kb.wait("sync", "ve")
        kb.dma("sync", out[128 * li:128 * li + 128, :], Y2B, "stq")
        kb.wait("vector", "stq")

    _finalize(kb, nc)
    return nc


def _finalize(kb, nc):
    kb.wait("sync", "stq") if "stq" in kb.cnt else None
    kb.wait("sync", "sq")
    with nc.Block() as block:
        @block.sync
        def _(e):
            for f in kb.ops["sync"]:
                f(e)

        @block.gpsimd
        def _(e):
            for f in kb.ops["gpsimd"]:
                f(e)

        @block.tensor
        def _(e):
            for f in kb.ops["tensor"]:
                f(e)

        @block.vector
        def _(e):
            for f in kb.ops["vector"]:
                f(e)

        @block.scalar
        def _(e):
            for f in kb.ops["scalar"]:
                f(e)


def _wtiles(aT, ktiles):
    K, N = aT.shape
    assert K <= 128 * ktiles
    buf = np.zeros((128 * ktiles, N), dtype=np.float32)
    buf[:K] = aT
    return np.ascontiguousarray(buf.reshape(ktiles, 128, N).transpose(1, 0, 2))


def _prep_inputs(inp, b):
    x = np.ascontiguousarray(inp["L0_attn_query"][b]).astype(np.float32, copy=False)
    d = {}
    d["xrow"] = x
    d["xtT"] = _wtiles(np.ascontiguousarray(x.T), 8)
    d["wqT"] = _wtiles(np.ascontiguousarray(inp["L0_attn_Wq"].T), 8)
    d["wkT"] = _wtiles(np.ascontiguousarray(inp["L0_attn_Wk"].T), 8)
    wv = np.zeros((1152, H * 65), dtype=np.float32)
    wvf = np.zeros((1152, H * 65), dtype=np.float32)
    WvT = inp["L0_attn_Wv"].T
    Wqkv = inp["L0_ffn_W_qkv"]
    for h in range(H):
        wv[0:1024, 65 * h:65 * h + 64] = WvT[:, 64 * h:64 * h + 64]
        wv[1024, 65 * h:65 * h + 64] = inp["L0_attn_bv"][64 * h:64 * h + 64]
        wv[1024, 65 * h + 64] = 1.0
        wvf[0:1024, 65 * h:65 * h + 64] = Wqkv[2 * D + 64 * h:2 * D + 64 * h + 64, :].T
        wvf[1024, 65 * h + 64] = 1.0
    d["wvT"] = np.ascontiguousarray(wv.reshape(9, 128, H * 65).transpose(1, 0, 2))
    d["wvfT"] = np.ascontiguousarray(wvf.reshape(9, 128, H * 65).transpose(1, 0, 2))
    d["woT"] = _wtiles(np.ascontiguousarray(inp["L0_attn_Wo"].T), 8)
    d["wbT"] = _wtiles(np.ascontiguousarray(inp["L0_attn_Wb"].T), 8)
    d["wqkT"] = _wtiles(np.ascontiguousarray(Wqkv[0:2 * D, :].T), 8)
    d["woutT"] = _wtiles(np.ascontiguousarray(inp["L0_ffn_W_out"].T), 8)
    d["w1T"] = _wtiles(np.ascontiguousarray(inp["L0_ffn_W1"].T), 8)
    d["w2T"] = _wtiles(np.ascontiguousarray(inp["L0_ffn_W2"].T), 8)
    d["w3T"] = _wtiles(np.ascontiguousarray(inp["L0_ffn_W3"].T), 22)
    d["bqt"] = np.ascontiguousarray(inp["L0_attn_bq"].reshape(8, 128).T)
    d["bkt"] = np.ascontiguousarray(inp["L0_attn_bk"].reshape(8, 128).T)
    d["brows"] = np.concatenate([inp["L0_attn_bo"], inp["L0_attn_bb"]]).reshape(1, 2 * D)
    d["wrb"] = np.ascontiguousarray(np.broadcast_to(inp["L0_ffn_w_router"], (128, D)))
    d["g1bi"] = np.ascontiguousarray(np.broadcast_to(inp["L0_ffn_g1"], (128, D)))
    d["g2bi"] = np.ascontiguousarray(np.broadcast_to(inp["L0_ffn_g2"], (128, D)))
    d["rwrep"] = np.ascontiguousarray(
        inp["L0_attn_radial_weights"].reshape(H)[np.arange(128) % H]).reshape(128, 1)
    d["curv"] = np.asarray(inp["L0_attn_curvature"], dtype=np.float32).reshape(1, 1)
    p = np.arange(128)
    s = np.arange(128)
    d["lt128"] = (s[None, :] < p[:, None]).astype(np.float32)
    d["iota512b"] = np.ascontiguousarray(
        np.broadcast_to(np.arange(512, dtype=np.float32), (128, 512)))
    d["iotarb"] = (p[:, None] + 128 * np.arange(4)[None, :]).astype(np.float32)
    mb = np.zeros((128, 4, 512), dtype=np.float32)
    for st in range(4):
        sglob = 128 * st + p
        mb[:, st, :] = np.where(sglob[:, None] <= np.arange(512)[None, :], 0.0, -1e30)
    d["maskbd"] = mb
    d["onesrow"] = np.ones((1, 128), dtype=np.float32)
    d["onescol"] = np.ones((128, 1), dtype=np.float32)
    d["eyef"] = np.eye(128, dtype=np.float32)
    for k in d:
        d[k] = np.ascontiguousarray(d[k], dtype=np.float32)
    return d


def kernel(**inputs):
    nc = build(2)
    in_maps = [_prep_inputs(inputs, c % 4) for c in range(8)]
    res = run_bass_kernel_spmd(nc, in_maps, core_ids=list(range(8)))
    outp = np.empty((B, T, D), dtype=np.float32)
    for b_ in range(4):
        outp[b_] = res.results[b_]["out"]
    return outp


# version 08 (final)
# speedup vs baseline: 1.6537x; 1.0068x over previous
import contextlib
import numpy as np
import ml_dtypes
import concourse.bass as bass
import concourse.mybir as mybir
from concourse.bass_utils import run_bass_kernel_spmd
from concourse.alu_op_type import AluOpType as OP

F32 = mybir.dt.float32
BF16 = mybir.dt.bfloat16
AF = mybir.ActivationFunctionType
AX = mybir.AxisListType.X

B, T, D = 4, 4096, 1024
H, CAP, DFF = 16, 512, 2730
EPS_H, EPS_LN, EPS_RMS = 1e-8, 1e-5, 1e-6
NT, NL, NS = 32, 8, 32
GRPS = [(3 * i, min(3 * i + 3, NS)) for i in range(11)]
ENGINES = ("sync", "gpsimd", "tensor", "vector", "scalar")


class KB:
    def __init__(self, nc, es):
        self.nc, self.es = nc, es
        self.ops = {e: [] for e in ENGINES}
        self.cnt = {}
        self.sems = {}

    def sem(self, name):
        if name not in self.sems:
            self.sems[name] = self.es.enter_context(self.nc.semaphore(name))
            self.cnt[name] = 0
        return self.sems[name]

    def op(self, eng, fn, inc=None):
        if inc is None:
            self.ops[eng].append(fn)
            return None
        s = self.sem(inc)
        self.cnt[inc] += 1
        self.ops[eng].append(lambda e, f=fn, s=s: f(e).then_inc(s, 1))
        return self.cnt[inc]

    def dma(self, eng, out_ap, in_ap, inc, **kw):
        s = self.sem(inc)
        self.cnt[inc] += 16
        self.ops[eng].append(
            lambda e, o=out_ap, i=in_ap, s=s, kw=kw: e.dma_start(out=o, in_=i, **kw).then_inc(s, 16))
        return self.cnt[inc]

    def wait(self, eng, sem, val=None):
        s = self.sem(sem)
        v = self.cnt.get(sem, 0) if val is None else val
        if v <= 0:
            return 0
        self.ops[eng].append(lambda e, s=s, v=v: e.wait_ge(s, v))
        return v

    def barrier(self):
        marks = []
        for eng in ENGINES:
            sm = "bar_" + eng
            marks.append((sm, self.op(eng, lambda e: e.nop(), inc=sm)))
        for eng in ENGINES:
            for sm, v in marks:
                self.wait(eng, sm, v)


class Arena:
    def __init__(self, ap, total):
        self.ap, self.total, self.cur = ap, total, 0

    def mark(self):
        return self.cur

    def reset(self, m):
        self.cur = m

    def t2(self, n):
        assert self.cur + n <= self.total, f"arena overflow {self.cur}+{n}>{self.total}"
        off = self.cur
        self.cur += n
        return self.ap[:, off:off + n]

    def t3(self, a, b):
        return self.t2(a * b).rearrange("p (a b) -> p a b", a=a)


def build(stage, dump=None):
    es = contextlib.ExitStack()
    nc = bass.Bass("TRN2", target_bir_lowering=False)
    kb = KB(nc, es)

    def _dump(pairs):
        for p in pairs:
            o, i = p[0], p[1]
            eng = p[2] if len(p) > 2 else "sync"
            kb.dma(eng, o, i, "stq")
        kb.wait("sync", "stq")
        kb.wait("gpsimd", "stq")
        _finalize(kb, nc)
        return nc

    def din(name, shape):
        return nc.dram_tensor(name, list(shape), F32, kind="ExternalInput").ap()

    xrow = din("xrow", (T, D))
    xtT = din("xtT", (128, 8, T))
    wqT = din("wqT", (128, 8, D))
    wkT = din("wkT", (128, 8, D))
    wvT = din("wvT", (128, 9, H * 65))
    woT = din("woT", (128, 8, D))
    wbT = din("wbT", (128, 8, D))
    wqkT = din("wqkT", (128, 8, 2 * D))
    wvfT = din("wvfT", (128, 9, H * 65))
    woutT = din("woutT", (128, 8, D))
    w1T = din("w1T", (128, 8, DFF))
    w2T = din("w2T", (128, 8, DFF))
    w3T = din("w3T", (128, 22, D))
    bqt = din("bqt", (128, 8))
    bkt = din("bkt", (128, 8))
    brows = din("brows", (1, 2 * D))
    wrb = din("wrb", (128, D))
    g1bi = din("g1bi", (128, D))
    g2bi = din("g2bi", (128, D))
    rwrep = din("rwrep", (128, 1))
    curv = din("curv", (1, 1))
    lt128 = din("lt128", (128, 128))
    iota512b = din("iota512b", (128, 512))
    iotarb = din("iotarb", (128, 4))
    maskbd = din("maskbd", (128, 4, 512))
    onesrow = din("onesrow", (1, 128))
    onescol = din("onescol", (128, 1))
    eyef = din("eyef", (128, 128))

    out = nc.dram_tensor("out", [T, D], F32, kind="ExternalOutput").ap()

    x1d = nc.dram_tensor("x1d", [T, D], F32).ap()
    vpdb = nc.dram_tensor("vpdb", [T, H * 65], BF16).ap()
    otdb = nc.dram_tensor("otdb", [8, 128, T], BF16).ap()
    stkd = nc.dram_tensor("stkd", [128, 8, 512], F32).ap()
    rowd = nc.dram_tensor("rowd", [128, 32], F32).ap()
    rowd2 = nc.dram_tensor("rowd2", [128, 32], F32).ap()

    def sb(name, shape, dt):
        return es.enter_context(nc.sbuf_tensor(name, list(shape), dt))[:]

    onesb = sb("onesb", (1, 128), F32)
    onesbf = sb("onesbf", (1, 128), BF16)
    onesc = sb("onesc", (128, 1), F32)
    eyeb = sb("eyeb", (128, 128), F32)
    lt = sb("lt", (128, 128), F32)
    io512 = sb("io512", (128, 512), F32)
    iorb = sb("iorb", (128, 4), F32)
    bqs = sb("bqs", (128, 8), F32)
    bks = sb("bks", (128, 8), F32)
    radl = sb("radl", (128, 1), F32)
    nalpha = sb("nalpha", (128, 1), F32)
    small = sb("small", (1, 16), F32)
    stat = sb("stat", (128, 16), F32)
    epsln = sb("epsln", (128, 1), F32)
    dinv = sb("dinv", (1, 512), F32)
    dbc = sb("dbc", (64, 512), F32)
    vstk = sb("vstk", (128, 512), F32)
    sc32 = sb("sc32", (128, 32), F32)
    rk32 = sb("rk32", (128, 32), F32)
    rowst = sb("rowst", (1, 512), F32)

    A16 = Arena(sb("A16", (128, 57344), BF16), 57344)
    A32 = Arena(sb("A32", (128, 20992), F32), 20992)

    PSA = es.enter_context(nc.psum_tensor("PSA", [128, 1536], F32))[:]
    PSB = es.enter_context(nc.psum_tensor("PSB", [128, 1536], F32))[:]
    PAV = es.enter_context(nc.psum_tensor("PAV", [128, 512], F32))[:]
    PMI = es.enter_context(nc.psum_tensor("PMI", [128, 512], F32))[:]
    PROJ_PS = [PSA[:, 0:512], PSB[:, 0:512], PSA[:, 512:1024], PSB[:, 512:1024]]
    NRM_PS = [PMI[0:1, 0:512], PAV[0:1, 0:512], PSA[0:1, 1024:1536], PSB[0:1, 1024:1536]]

    for o, i in [(onesb, onesrow), (onesc, onescol), (eyeb, eyef), (lt, lt128),
                 (io512, iota512b), (iorb, iotarb),
                 (bqs, bqt), (bks, bkt), (radl, rwrep), (small[0:1, 0:1], curv)]:
        kb.dma("sync", o, i, "dq")
    kb.dma("gpsimd", onesbf, onesrow, "gq")

    XT = A16.t3(8, T)
    markA16 = A16.mark()
    markA32 = A32.mark()
    kb.dma("gpsimd", XT[:, :, :], xtT, "gq")
    kb.op("gpsimd", lambda e: e.memset(epsln, EPS_LN), inc="gp")

    kb.wait("vector", "dq")
    kb.wait("vector", "gp")
    kb.op("vector", lambda e: e.tensor_scalar(out=small[0:1, 1:2], in0=small[0:1, 0:1],
                                              scalar1=-1.0, scalar2=None, op0=OP.mult), inc="vch")
    kb.wait("vector", "vch")
    n = kb.op("vector", lambda e: e.tensor_scalar(out=small[0:1, 2:3], in0=small[0:1, 0:1],
                                                  scalar1=small[0:1, 1:2], scalar2=EPS_H,
                                                  op0=OP.max, op1=OP.max), inc="ve")
    kb.wait("scalar", "ve", n)
    kb.op("scalar", lambda e: e.activation(out=small[0:1, 4:5], in_=small[0:1, 2:3],
                                           func=AF.Ln), inc="sch")
    kb.wait("scalar", "sch")
    m = kb.op("scalar", lambda e: e.activation(out=small[0:1, 5:6], in_=small[0:1, 4:5],
                                               func=AF.Exp, scale=0.5), inc="se")
    kb.wait("vector", "se", m)
    kb.op("vector", lambda e: e.tensor_scalar(out=small[0:1, 6:7], in0=small[0:1, 5:6],
                                              scalar1=EPS_H, scalar2=8.0, op0=OP.add,
                                              op1=OP.mult), inc="vch")
    kb.wait("vector", "vch")
    kb.op("vector", lambda e: e.reciprocal(out=small[0:1, 7:8], in_=small[0:1, 6:7]), inc="vch")
    kb.wait("vector", "vch")
    n2 = kb.op("vector", lambda e: e.tensor_scalar(out=small[0:1, 8:9], in0=small[0:1, 7:8],
                                                   scalar1=-1.0, scalar2=None, op0=OP.mult), inc="ve")
    kb.wait("tensor", "ve", n2)
    k1 = kb.op("tensor", lambda e: e.matmul(out=PMI[:, 0:1], lhsT=onesb, rhs=small[0:1, 8:9],
                                            start=True, stop=True), inc="pe")
    kb.wait("vector", "pe", k1)
    kb.op("vector", lambda e: e.tensor_copy(out=nalpha, in_=PMI[:, 0:1]), inc="ve")
    ms = kb.op("scalar", lambda e: e.activation(out=radl, in_=radl, func=AF.Exp, scale=-1.0), inc="se")
    kb.wait("vector", "se", ms)
    kb.op("vector", lambda e: e.tensor_scalar(out=radl, in0=radl, scalar1=1.0,
                                              scalar2=None, op0=OP.add), inc="vch")
    kb.wait("vector", "vch")
    kb.op("vector", lambda e: e.reciprocal(out=radl, in_=radl), inc="ve")
    kb.barrier()
    if dump == "smalls":
        return _dump([(out[0:1, 0:16], small), (out[1:2, 0:1], nalpha[0:1, :]),
                      (out[2:3, 0:1], radl[0:1, :])])

    def f_chain(dst, src, sa, sbuf, mul_radial=False):
        kb.wait("scalar", "ve")
        kb.op("scalar", lambda e: e.activation(out=sa, in_=src, func=AF.Ln))
        kb.op("scalar", lambda e: e.activation(out=sa, in_=sa, func=AF.Exp, scale=0.5))
        q = kb.op("scalar", lambda e: e.activation(out=sbuf, in_=sa, func=AF.Exp, scale=2.0), inc="se")
        kb.wait("vector", "se", q)
        kb.op("vector", lambda e: e.tensor_scalar(out=sbuf, in0=sbuf, scalar1=1.0,
                                                  scalar2=None, op0=OP.add))
        kb.op("vector", lambda e: e.reciprocal(out=sbuf, in_=sbuf), inc="vch")
        kb.wait("vector", "vch")
        kb.op("vector", lambda e: e.tensor_scalar(out=sbuf, in0=sbuf, scalar1=-2.0,
                                                  scalar2=1.0, op0=OP.mult, op1=OP.add))
        kb.op("vector", lambda e: e.tensor_scalar(out=sa, in0=sa, scalar1=EPS_H,
                                                  scalar2=None, op0=OP.add))
        kb.op("vector", lambda e: e.reciprocal(out=sa, in_=sa), inc="vch")
        kb.wait("vector", "vch")
        if mul_radial:
            kb.op("vector", lambda e: e.tensor_scalar(out=sa, in0=sa, scalar1=radl,
                                                      scalar2=None, op0=OP.mult))
        return kb.op("vector", lambda e: e.tensor_tensor(out=dst, in0=sa, in1=sbuf,
                                                         op=OP.mult), inc="ve")

    mk16 = A16.mark(); mk32 = A32.mark()
    WVs = A16.t3(9, 260)
    VEVB = A16.t2(260)
    VEV = A32.t2(260)
    SQ5 = A32.t2(260)
    for ch in range(4):
        c0 = 260 * ch
        kb.wait("gpsimd", "pe")
        for kt in range(9):
            kb.dma("gpsimd", WVs[0:128, kt, :], wvT[:, kt, c0:c0 + 260], "gq")
        kb.wait("tensor", "gq")
        for tb in range(NT):
            pscur = PSA if tb % 2 == 0 else PSB
            for kt in range(8):
                kb.op("tensor", lambda e, kt=kt, tb=tb, p=pscur: e.matmul(
                    out=p[0:128, 0:260], lhsT=XT[0:128, kt, 128 * tb:128 * tb + 128],
                    rhs=WVs[0:128, kt, :], start=(kt == 0), stop=False))
            kb.op("tensor", lambda e, p=pscur: e.matmul(
                out=p[0:128, 0:260], lhsT=onesbf, rhs=WVs[0:1, 8, :],
                start=False, stop=True), inc="pe")
            kb.wait("vector", "pe")
            kb.op("vector", lambda e, p=pscur: e.tensor_copy(out=VEV, in_=p[0:128, 0:260]), inc="ve")
            kb.wait("tensor", "ve")
            kb.op("vector", lambda e: e.tensor_tensor(out=SQ5, in0=VEV, in1=VEV, op=OP.mult))
            for hl in range(4):
                h = 4 * ch + hl
                kb.op("vector", lambda e, hl=hl, tb=tb, h=h: e.tensor_reduce(
                    out=vstk[0:128, 16 * tb + h:16 * tb + h + 1],
                    in_=SQ5[:, 65 * hl:65 * hl + 64], axis=AX, op=OP.add))
            kb.op("vector", lambda e: e.tensor_copy(out=VEVB, in_=VEV), inc="ve")
            kb.wait("sync", "ve")
            kb.dma("sync", vpdb[128 * tb:128 * tb + 128, c0:c0 + 260], VEVB, "sq")
            kb.wait("vector", "sq")
    A16.reset(mk16)
    FCA = A32.t2(512)
    FCB = A32.t2(512)
    kb.wait("scalar", "ve")
    f_chain(vstk, vstk, FCA, FCB)
    A32.reset(mk32)
    kb.barrier()
    if dump == "vstk":
        return _dump([(out[0:128, 0:512], vstk)])
    if dump == "vpdb":
        return _dump([(out[:, :], vpdb[:, 0:1024], "gpsimd")])

    mk32 = A32.mark(); mk16 = A16.mark()
    KQP = A32.t2(T)
    KQPc = KQP.rearrange("p (c n) -> p c n", n=512)
    SQB = A32.t2(T)
    SQBc = SQB.rearrange("p (c n) -> p c n", n=512)
    nstk = A32.t3(2, 512)
    NST = A32.t3(2, 512)
    WQs = A16.t3(8, 64)
    WKs = A16.t3(8, 64)

    def proj_head(h, dst, WQb, WKb):
        kb.wait("gpsimd", "pe")
        for kt in range(8):
            kb.dma("gpsimd", WQb[0:128, kt, :], wqT[:, kt, 64 * h:64 * h + 64], "gq")
            kb.dma("gpsimd", WKb[0:128, kt, :], wkT[:, kt, 64 * h:64 * h + 64], "gq")
        kb.wait("tensor", "gq")
        for c in range(8):
            for qk in range(2):
                W = WKb if qk == 0 else WQb
                pbank = PROJ_PS[(2 * c + qk) % 4]
                for kt in range(8):
                    kb.op("tensor", lambda e, W=W, kt=kt, c=c, p=pbank: e.matmul(
                        out=p[0:64, 0:512], lhsT=W[0:128, kt, :],
                        rhs=XT[0:128, kt, 512 * c:512 * c + 512],
                        start=(kt == 0), stop=(kt == 7)), inc="pe" if kt == 7 else None)
                kb.wait("vector", "pe")
                bias = (bks if qk == 0 else bqs)[64 * (h % 2):64 * (h % 2) + 64, h // 2:h // 2 + 1]
                r0 = 0 if qk == 0 else 64
                kb.op("vector", lambda e, p=pbank, bias=bias, r0=r0, c=c, dst=dst: e.tensor_scalar(
                    out=dst[r0:r0 + 64, 512 * c:512 * c + 512], in0=p[0:64, 0:512],
                    scalar1=bias, scalar2=None, op0=OP.add), inc="ve")
        kb.wait("tensor", "ve")

    for h in range(H):
        proj_head(h, KQP, WQs, WKs)
        kb.op("vector", lambda e: e.tensor_tensor(out=SQB, in0=KQP, in1=KQP, op=OP.mult), inc="ve")
        kb.wait("tensor", "ve")
        for c in range(8):
            pk = NRM_PS[c % 4]
            pq = NRM_PS[(c % 4) ^ 1]
            kb.op("tensor", lambda e, c=c, pk=pk: e.matmul(out=pk, lhsT=onesc[0:64, :],
                                                           rhs=SQBc[0:64, c, :],
                                                           start=True, stop=True), inc="pe")
            kb.op("tensor", lambda e, c=c, pq=pq: e.matmul(out=pq, lhsT=onesc[64:128, :],
                                                           rhs=SQBc[64:128, c, :],
                                                           start=True, stop=True), inc="pe")
            kb.wait("vector", "pe")
            p = h + 16 * c
            kb.op("vector", lambda e, pk=pk: e.tensor_copy(out=NST[0:1, 0, :], in_=pk))
            kb.op("vector", lambda e, pq=pq: e.tensor_copy(out=NST[0:1, 1, :], in_=pq), inc="ve")
            kb.wait("sync", "ve")
            kb.dma("sync", nstk[p:p + 1, 1, 0:512], NST[0:1, 0, :], "sq")
            kb.dma("sync", nstk[p:p + 1, 0, 0:512], NST[0:1, 1, :], "sq")
            kb.wait("tensor", "ve")
            kb.wait("vector", "sq")

    if dump == "nstk":
        return _dump([(out[0:128, 0:512], nstk[:, 0, :]),
                      (out[128:256, 0:512], nstk[:, 1, :])])

    FK = A32.t2(512)
    KN = A32.t2(512)
    FCA = A32.t2(512)
    FCB = A32.t2(512)
    csk = A32.t2(512)
    csq = A32.t2(512)
    rxk = A32.t3(3, 512)
    rxq = A32.t3(3, 512)
    kb.wait("scalar", "ve")
    kb.wait("scalar", "sq")
    for side in range(2):
        f_chain(FK, nstk[:, 1 - side, :], FCA, FCB, mul_radial=(side == 1))
        rx = rxk if side == 0 else rxq
        cs = csk if side == 0 else csq
        kb.op("vector", lambda e: e.tensor_tensor(out=KN, in0=FK, in1=FK, op=OP.mult))
        kb.op("vector", lambda e, side=side: e.tensor_tensor(out=KN, in0=KN,
                                                             in1=nstk[:, 1 - side, :], op=OP.mult))
        kb.op("vector", lambda e, rx=rx: e.tensor_scalar(out=rx[:, 2, :], in0=KN, scalar1=-1.0,
                                                         scalar2=1.0, op0=OP.mult, op1=OP.add))
        kb.op("vector", lambda e, rx=rx: e.tensor_scalar(out=rx[:, 2, :], in0=rx[:, 2, :],
                                                         scalar1=1e-20, scalar2=None, op0=OP.max))
        kb.op("vector", lambda e, rx=rx: e.reciprocal(out=rx[:, 1, :], in_=rx[:, 2, :]), inc="vch")
        kb.wait("vector", "vch")
        kb.op("vector", lambda e, rx=rx: e.memset(rx[:, 2, :], 1.0))
        kb.op("vector", lambda e, rx=rx, cs=cs: e.tensor_tensor(out=cs, in0=FK,
                                                                in1=rx[:, 1, :], op=OP.mult))
        if side == 0:
            kb.op("vector", lambda e, cs=cs: e.tensor_scalar(out=cs, in0=cs, scalar1=-4.0,
                                                             scalar2=None, op0=OP.mult))
        kb.op("vector", lambda e, rx=rx: e.tensor_tensor(out=rx[:, 0, :], in0=KN,
                                                         in1=rx[:, 1, :], op=OP.mult))
        kb.op("vector", lambda e, rx=rx: e.tensor_scalar(out=rx[:, 0, :], in0=rx[:, 0, :],
                                                         scalar1=2.0, scalar2=None, op0=OP.mult), inc="ve")
    kb.wait("sync", "ve")
    kb.dma("sync", stkd[:, 0, :], csk, "sq")
    kb.dma("sync", stkd[:, 1, :], csq, "sq")
    for r in range(3):
        kb.dma("sync", stkd[:, 2 + r, :], rxk[:, r, :], "sq")
        kb.dma("sync", stkd[:, 5 + r, :], rxq[:, r, :], "sq")
    kb.barrier()
    A32.reset(mk32)
    A16.reset(mk16)
    if dump == "stkd":
        return _dump([(out[128 * j:128 * j + 128, 0:512], stkd[:, j, :]) for j in range(8)])

    KQb = [A16.t2(T) for _ in range(2)]
    QQb = [A16.t2(T) for _ in range(2)]
    VPh = A16.t3(NT, 65)
    ATb = [A16.t2(1536) for _ in range(2)]
    OTS = A16.t2(512)
    WQs2 = A16.t3(8, 64)
    WKs2 = A16.t3(8, 64)
    KQP2 = A32.t2(T)
    LxB = [A32.t2(1536) for _ in range(2)]

    def stk_row(slot, h, c):
        return stkd[:, slot, :].rearrange("(j h) c -> h j c", h=16)[h:h + 1, c, 0:512]

    attn_end = {}
    av_last = {0: 0, 1: 0}
    ln_last = {0: 0, 1: 0}

    def prep_head(h):
        hb = h % 2
        KQ_, QQ_ = KQb[hb], QQb[hb]
        if h - 2 in attn_end:
            kb.wait("vector", "pesc", attn_end[h - 2][0])
            kb.wait("gpsimd", "pesc", attn_end[h - 2][0])
        kb.wait("tensor", "seln")
        pe_b4, ve_b4 = kb.cnt.get("pe", 0), kb.cnt.get("ve", 0)
        proj_head(h, KQP2, WQs2, WKs2)
        kb.wait("tensor", "ve")
        kb.wait("tensor", "veot")
        kb.wait("sync", "pe", pe_b4)
        kb.wait("sync", "ve", ve_b4)
        rbuf = [rowst, dinv]
        pslot = [PMI[0:64, 0:512], PAV[0:64, 0:512]]
        pe_mk, ve_mk = {}, {}
        for side in range(2):
            src_ = KQP2[0:64, :] if side == 0 else KQP2[64:128, :]
            dst_ = KQ_ if side == 0 else QQ_
            for c in range(8):
                k = 8 * side + c
                kbuf = k % 2
                if k >= 2:
                    kb.wait("sync", "pe", pe_mk[k - 2])
                vk = kb.dma("sync", rbuf[kbuf], stk_row(side, h, c), "sq")
                kb.wait("tensor", "sq", vk)
                if k >= 2:
                    kb.wait("tensor", "ve", ve_mk[k - 2])
                pe_mk[k] = kb.op("tensor", lambda e, kbuf=kbuf: e.matmul(
                    out=pslot[kbuf], lhsT=onesb[0:1, 0:64],
                    rhs=rbuf[kbuf], start=True, stop=True), inc="pe")
                kb.wait("vector", "pe", pe_mk[k])
                ve_mk[k] = kb.op("vector", lambda e, c=c, kbuf=kbuf, dst_=dst_, src_=src_: e.tensor_tensor(
                    out=dst_[0:64, 512 * c:512 * c + 512],
                    in0=src_[:, 512 * c:512 * c + 512], in1=pslot[kbuf],
                    op=OP.mult), inc="ve")
        kv = None
        for r in range(3):
            kb.dma("gpsimd",
                   KQ_[64 + r:65 + r, :].rearrange("p (j c) -> p j c", c=512),
                   stkd[:, 2 + r, :].rearrange("(j h) c -> h j c", h=16)[h:h + 1, :, :], "gq")
            kv = kb.dma("gpsimd",
                        QQ_[64 + r:65 + r, :].rearrange("p (j c) -> p j c", c=512),
                        stkd[:, 5 + r, :].rearrange("(j h) c -> h j c", h=16)[h:h + 1, :, :], "gq")
        return kv

    def load_v(h):
        kb.wait("gpsimd", "peav")
        gv = kb.dma("gpsimd", VPh[:, :, :],
                    vpdb[:, 65 * h:65 * h + 65].rearrange("(tb p) c -> p tb c", p=128), "gq")
        return gv

    def attn_head(h, prep_mark, v_mark):
        hb = h % 2
        KQ_, QQ_ = KQb[hb], QQb[hb]
        kb.wait("tensor", "gq", prep_mark)
        kb.wait("vector", "gq", v_mark)
        if h - 1 in attn_end:
            kb.wait("vector", "peav", attn_end[h - 1][1])
        for tb in range(NT):
            kb.op("vector", lambda e, tb=tb, h=h: e.tensor_scalar(
                out=VPh[:, tb, 0:64], in0=VPh[:, tb, 0:64],
                scalar1=vstk[0:128, 16 * tb + h:16 * tb + h + 1], scalar2=None,
                op0=OP.mult), inc="vfv")
        kb.wait("tensor", "vfv")
        def issue_scores(lc, gi, s0, s1):
            psg = PSA if gi % 2 == 0 else PSB
            kb.wait("tensor", "seln", ln_last[gi % 2])
            for i in range(s0, s1):
                j = i - s0
                kb.op("tensor", lambda e, i=i, j=j, p=psg, KQ_=KQ_, QQ_=QQ_, lc=lc: e.matmul(
                    out=p[0:128, 512 * j:512 * j + 512],
                    lhsT=KQ_[0:67, 128 * i:128 * i + 128],
                    rhs=QQ_[0:67, 512 * lc:512 * lc + 512], start=True, stop=True),
                    inc="pesc" if i == s1 - 1 else None)
            return kb.cnt["pesc"]

        for lc in range(NL):
            sc_cnt = {0: issue_scores(lc, 0, *GRPS[0])}
            for gi, (s0, s1) in enumerate(GRPS):
                psg = PSA if gi % 2 == 0 else PSB
                nst = s1 - s0
                if gi + 1 < len(GRPS):
                    sc_cnt[gi + 1] = issue_scores(lc, gi + 1, *GRPS[gi + 1])
                kb.wait("scalar", "pesc", sc_cnt[gi])
                kb.op("scalar", lambda e, p=psg, nst=nst, gi=gi: e.activation(
                    out=LxB[gi % 2][:, 0:512 * nst], in_=p[0:128, 0:512 * nst],
                    func=AF.Ln), inc="seln")
                ln_last[gi % 2] = kb.cnt["seln"]
                kb.wait("scalar", "peav", av_last[gi % 2])
                ex = kb.op("scalar", lambda e, nst=nst, gi=gi: e.activation(
                    out=ATb[gi % 2][:, 0:512 * nst], in_=LxB[gi % 2][:, 0:512 * nst],
                    func=AF.Exp, scale=nalpha), inc="seexp")
                if gi == 0:
                    kb.wait("tensor", "veot")
                kb.wait("tensor", "seexp", ex)
                for i in range(s0, s1):
                    j = i - s0
                    kb.op("tensor", lambda e, i=i, j=j, gi=gi: e.matmul(
                        out=PAV[0:65, 0:512], lhsT=VPh[:, i, 0:65],
                        rhs=ATb[gi % 2][:, 512 * j:512 * j + 512],
                        start=(i == 0), stop=(i == NS - 1)),
                        inc="peav" if i == s1 - 1 else None)
                av_last[gi % 2] = kb.cnt["peav"]
            kb.wait("vector", "peav")
            kb.op("vector", lambda e: e.reciprocal(out=dinv, in_=PAV[64:65, 0:512]), inc="ve")
            kb.wait("tensor", "ve")
            kb.op("tensor", lambda e: e.matmul(out=PMI[0:64, 0:512], lhsT=onesb[0:1, 0:64],
                                               rhs=dinv, start=True, stop=True), inc="pe")
            kb.wait("vector", "pe")
            kb.op("vector", lambda e: e.tensor_copy(out=dbc, in_=PMI[0:64, 0:512]), inc="ve")
            kb.op("vector", lambda e: e.tensor_tensor(out=OTS[0:64, :], in0=PAV[0:64, 0:512],
                                                      in1=dbc, op=OP.mult), inc="veot")
            kb.wait("sync", "veot")
            kb.dma("sync", otdb[h // 2, 64 * (h % 2):64 * (h % 2) + 64, 512 * lc:512 * lc + 512],
                   OTS[0:64, :], "sq")
            kb.wait("vector", "sq")
        attn_end[h] = (kb.cnt["pesc"], kb.cnt["peav"], kb.cnt["seexp"])

    pm = prep_head(0)
    vm = load_v(0)
    for h in range(H):
        if h + 1 < H:
            nxt = prep_head(h + 1)
        else:
            nxt = None
        attn_head(h, pm, vm)
        if h + 1 < H:
            vm = load_v(h + 1)
        pm = nxt
    kb.barrier()
    A16.reset(markA16)
    A32.reset(markA32)
    if dump == "otdb":
        return _dump([(out[128 * j:128 * j + 128, :], otdb[j, :, 0:1024], "gpsimd")
                      for j in range(8)])

    WOc = A16.t3(8, D)
    WBc = A16.t3(8, D)
    OTL = [A16.t3(8, 128) for _ in range(2)]
    XST = [A32.t2(D) for _ in range(2)]
    YB = A32.t2(D)
    X1B = A32.t2(D)
    SQy = A32.t2(D)
    browS = A32.ap[0:1, A32.cur:A32.cur + 2 * D]; A32.cur += 2 * D
    wrbsS = A32.t2(D)
    kb.dma("gpsimd", WOc[:, :, :], woT, "gq")
    kb.dma("gpsimd", WBc[:, :, :], wbT, "gq")
    kb.dma("sync", browS, brows, "sq")
    kb.dma("sync", wrbsS, wrb, "sq")
    kb.wait("tensor", "gq")
    kb.wait("tensor", "sq")

    for lb in range(NT):
        bb_ = lb % 2
        kb.wait("sync", "pe")
        kb.wait("sync", "ve")
        for j in range(8):
            kb.dma("sync", OTL[bb_][:, j, :], otdb[j, :, 128 * lb:128 * lb + 128], "sq")
        kb.dma("sync", XST[bb_], xrow[128 * lb:128 * lb + 128, :], "sq")
        kb.wait("tensor", "sq")
        for jc in range(2):
            for j in range(8):
                kb.op("tensor", lambda e, j=j, jc=jc, bb_=bb_: e.matmul(
                    out=PSA[0:128, 512 * jc:512 * jc + 512], lhsT=OTL[bb_][:, j, :],
                    rhs=WOc[:, j, 512 * jc:512 * jc + 512], start=(j == 0), stop=False))
            for kt in range(8):
                kb.op("tensor", lambda e, kt=kt, jc=jc, lb=lb: e.matmul(
                    out=PSA[0:128, 512 * jc:512 * jc + 512],
                    lhsT=XT[0:128, kt, 128 * lb:128 * lb + 128],
                    rhs=WBc[0:128, kt, 512 * jc:512 * jc + 512], start=False, stop=False))
            kb.op("tensor", lambda e, jc=jc: e.matmul(
                out=PSA[0:128, 512 * jc:512 * jc + 512], lhsT=onesb,
                rhs=browS[0:1, 512 * jc:512 * jc + 512], start=False, stop=False))
            kb.op("tensor", lambda e, jc=jc: e.matmul(
                out=PSA[0:128, 512 * jc:512 * jc + 512], lhsT=onesb,
                rhs=browS[0:1, D + 512 * jc:D + 512 * jc + 512], start=False, stop=True),
                inc="pe")
        kb.wait("vector", "pe")
        kb.op("vector", lambda e, bb_=bb_: e.scalar_tensor_tensor(
            out=YB, in0=XST[bb_], scalar=1.0, in1=PSA[0:128, 0:1024],
            op0=OP.mult, op1=OP.add), inc="ve")
        kb.wait("tensor", "ve")
        kb.op("vector", lambda e: e.tensor_reduce(out=stat[:, 0:1], in_=YB, axis=AX, op=OP.add))
        kb.op("vector", lambda e: e.tensor_tensor(out=SQy, in0=YB, in1=YB, op=OP.mult))
        kb.op("vector", lambda e: e.tensor_reduce(out=stat[:, 1:2], in_=SQy, axis=AX, op=OP.add))
        kb.op("vector", lambda e: e.tensor_scalar(out=stat[:, 2:3], in0=stat[:, 0:1],
                                                  scalar1=1.0 / D, scalar2=None, op0=OP.mult))
        kb.op("vector", lambda e: e.tensor_scalar(out=stat[:, 4:5], in0=stat[:, 1:2],
                                                  scalar1=1.0 / D, scalar2=None,
                                                  op0=OP.mult), inc="vch")
        kb.wait("vector", "vch")
        kb.op("vector", lambda e: e.tensor_tensor(out=stat[:, 3:4], in0=stat[:, 2:3],
                                                  in1=stat[:, 2:3], op=OP.mult), inc="vch")
        kb.wait("vector", "vch")
        v1 = kb.op("vector", lambda e: e.tensor_tensor(out=stat[:, 5:6], in0=stat[:, 4:5],
                                                       in1=stat[:, 3:4], op=OP.subtract), inc="ve")
        kb.wait("scalar", "ve", v1)
        kb.op("scalar", lambda e: e.activation(out=stat[:, 6:7], in_=stat[:, 5:6],
                                               func=AF.Ln, bias=epsln), inc="sch")
        kb.wait("scalar", "sch")
        s1_ = kb.op("scalar", lambda e: e.activation(out=stat[:, 7:8], in_=stat[:, 6:7],
                                                     func=AF.Exp, scale=-0.5), inc="se")
        kb.wait("vector", "se", s1_)
        kb.op("vector", lambda e: e.tensor_scalar(out=stat[:, 8:9], in0=stat[:, 2:3],
                                                  scalar1=-1.0, scalar2=None,
                                                  op0=OP.mult), inc="vch")
        kb.wait("vector", "vch")
        kb.op("vector", lambda e: e.tensor_scalar(out=X1B, in0=YB, scalar1=stat[:, 8:9],
                                                  scalar2=stat[:, 7:8], op0=OP.add, op1=OP.mult))
        kb.op("vector", lambda e: e.tensor_tensor(out=SQy, in0=X1B, in1=wrbsS, op=OP.mult))
        kb.op("vector", lambda e, lb=lb: e.tensor_reduce(out=sc32[:, lb:lb + 1], in_=SQy,
                                                         axis=AX, op=OP.add), inc="ve")
        kb.wait("sync", "ve")
        kb.dma("sync", x1d[128 * lb:128 * lb + 128, :], X1B, "sq")
        if stage == 1:
            kb.dma("sync", out[128 * lb:128 * lb + 128, :], X1B, "sq")
        kb.wait("vector", "sq")
    if stage == 1:
        _finalize(kb, nc)
        return nc
    kb.barrier()
    A16.reset(0)
    A32.reset(markA32)

    sbc_mark = A32.mark()
    SBC = A32.t2(T)
    ED = A32.t2(128)
    EM = A32.t2(128)

    def bcast_row(src32, dst, rd):
        kb.wait("sync", "ve")
        kb.dma("sync", rd[:, :], src32, "sq")
        kb.wait("sync", "sq")
        for c in range(8):
            kb.dma("sync", rowst.rearrange("r (a p) -> r a p", p=128),
                   rd.rearrange("p c -> () c p")[0:1, 4 * c:4 * c + 4, :], "bcr",
                   allow_slow_non_contiguous=True)
            kb.dma("sync", small[0:1, 15:16], curv, "bfl")
            kb.wait("tensor", "bfl")
            kb.op("tensor", lambda e: e.matmul(out=PMI[0:128, 0:512], lhsT=onesb,
                                               rhs=rowst, start=True, stop=True), inc="pe")
            kb.wait("vector", "pe")
            kb.op("vector", lambda e, c=c, dst=dst: e.tensor_copy(
                out=dst[:, 512 * c:512 * c + 512], in_=PMI[0:128, 0:512]), inc="ve")
            kb.wait("tensor", "ve")
            kb.wait("sync", "ve")

    bcast_row(sc32, SBC, rowd)
    if dump == "sbc":
        return _dump([(out[0:128, 0:32], sc32)] +
                     [(out[128 * (1 + k):128 * (2 + k), :], SBC[:, 1024 * k:1024 * k + 1024])
                      for k in range(4)])
    RTMP = A32.t2(T)
    for c in range(NT):
        kb.op("vector", lambda e, c=c: e.tensor_scalar(
            out=ED, in0=SBC[:, 128 * c:128 * c + 128], scalar1=sc32[:, c:c + 1],
            scalar2=None, op0=OP.is_equal))
        kb.op("vector", lambda e: e.tensor_tensor(out=EM, in0=ED, in1=lt, op=OP.mult))
        kb.op("vector", lambda e: e.tensor_reduce(out=stat[:, 2:3], in_=EM, axis=AX, op=OP.add))
        kb.op("vector", lambda e, c=c: e.tensor_scalar(
            out=RTMP, in0=SBC, scalar1=sc32[:, c:c + 1], scalar2=0.0,
            op0=OP.is_gt, op1=OP.add, accum_out=stat[:, 0:1]),
            inc="vch" if c == 0 else None)
        if c > 0:
            kb.op("vector", lambda e, c=c: e.tensor_scalar(
                out=RTMP[:, 0:128 * c], in0=SBC[:, 0:128 * c], scalar1=sc32[:, c:c + 1],
                scalar2=0.0, op0=OP.is_equal, op1=OP.add, accum_out=stat[:, 1:2]), inc="vch")
        kb.wait("vector", "vch")
        if c > 0:
            kb.op("vector", lambda e: e.tensor_tensor(out=stat[:, 0:1], in0=stat[:, 0:1],
                                                      in1=stat[:, 1:2], op=OP.add), inc="vch")
            kb.wait("vector", "vch")
        kb.op("vector", lambda e: e.tensor_tensor(out=stat[:, 0:1], in0=stat[:, 0:1],
                                                  in1=stat[:, 2:3], op=OP.add), inc="vch")
        kb.wait("vector", "vch")
        kb.op("vector", lambda e, c=c: e.tensor_copy(out=rk32[:, c:c + 1], in_=stat[:, 0:1]), inc="ve")
    A32.reset(sbc_mark)
    RBC = A32.t2(T)
    bcast_row(rk32, RBC, rowd2)
    if dump == "ranks":
        return _dump([(out[0:128, 0:32], rk32)] +
                     [(out[128 * (1 + k):128 * (2 + k), :], RBC[:, 1024 * k:1024 * k + 1024])
                      for k in range(4)])

    PTt = [A16.t2(CAP) for _ in range(2)]
    X1bf = [A16.t2(D) for _ in range(2)]
    X1fs = [A32.t2(D) for _ in range(2)]
    XSEL = A32.t3(4, D)
    GPS = [PSA[:, 0:512], PSA[:, 512:1024], PSA[:, 1024:1536], PSB[:, 0:512],
           PSB[:, 512:1024], PSB[:, 1024:1536], PAV[:, 0:512], PMI[:, 0:512]]
    gthm = {}
    for c in range(NT):
        cb = c % 2
        if c >= 2:
            kb.wait("vector", "pegth", gthm[c - 2])
        kb.wait("sync", "ve")
        kb.dma("sync", X1fs[cb], x1d[128 * c:128 * c + 128, :], "sq")
        kb.wait("vector", "sq")
        kb.op("vector", lambda e, cb=cb: e.tensor_copy(out=X1bf[cb], in_=X1fs[cb]))
        kb.op("vector", lambda e, c=c, cb=cb: e.tensor_scalar(
            out=PTt[cb], in0=io512, scalar1=rk32[:, c:c + 1], scalar2=None,
            op0=OP.is_equal), inc="ve")
        kb.wait("tensor", "ve")
        for rb in range(4):
            for ec in range(2):
                last = (rb == 3 and ec == 1)
                kb.op("tensor", lambda e, c=c, cb=cb, rb=rb, ec=ec: e.matmul(
                    out=GPS[2 * rb + ec][0:128, :], lhsT=PTt[cb][:, 128 * rb:128 * rb + 128],
                    rhs=X1bf[cb][:, 512 * ec:512 * ec + 512],
                    start=(c == 0), stop=(c == NT - 1)), inc="pegth" if last else None)
        gthm[c] = kb.cnt["pegth"]
    kb.wait("vector", "pegth", gthm[NT - 1])
    for rb in range(4):
        for ec in range(2):
            kb.op("vector", lambda e, rb=rb, ec=ec: e.tensor_copy(
                out=XSEL[:, rb, 512 * ec:512 * ec + 512], in_=GPS[2 * rb + ec][0:128, :]), inc="ve")
    kb.wait("tensor", "ve")

    H1F = A32.t3(4, D)
    XATT = A32.t3(4, D)
    SQyB = A32.t2(D)
    MSKS = A32.t2(512)
    H1T = A16.t3(9, CAP)
    QKT = A16.t3(16, CAP)
    VIB = A16.t3(4, H * 65)
    IAT = A16.t3(4, 512)
    OTi = [A16.t2(CAP) for _ in range(8)]
    WS1 = A16.t3(8, 512)
    WS2 = A16.t3(9, 512)
    A1B = A16.t2(512)
    ABT = A16.t3(22, CAP)
    DELB = A16.t3(4, D)
    MKB = A16.t3(4, 512)
    g1bS = A16.t2(D)
    g2bS = A16.t2(D)
    kb.dma("gpsimd", MKB[:, :, :], maskbd, "gq")
    kb.dma("gpsimd", g1bS, g1bi, "gq")
    kb.dma("gpsimd", g2bS, g2bi, "gq")

    def rmsnorm(dst3, src3, gs):
        for rt in range(4):
            kb.op("vector", lambda e, rt=rt: e.tensor_tensor(out=SQyB, in0=src3[:, rt, :],
                                                             in1=src3[:, rt, :], op=OP.mult))
            kb.op("vector", lambda e: e.tensor_reduce(out=stat[:, 0:1], in_=SQyB,
                                                      axis=AX, op=OP.add), inc="vch")
            kb.wait("vector", "vch")
            r = kb.op("vector", lambda e: e.tensor_scalar(out=stat[:, 1:2], in0=stat[:, 0:1],
                                                          scalar1=1.0 / D, scalar2=EPS_RMS,
                                                          op0=OP.mult, op1=OP.add), inc="ve")
            kb.wait("scalar", "ve", r)
            kb.op("scalar", lambda e: e.activation(out=stat[:, 2:3], in_=stat[:, 1:2],
                                                   func=AF.Ln), inc="sch")
            kb.wait("scalar", "sch")
            s2 = kb.op("scalar", lambda e: e.activation(out=stat[:, 3:4], in_=stat[:, 2:3],
                                                        func=AF.Exp, scale=-0.5), inc="se")
            kb.wait("vector", "se", s2)
            kb.op("vector", lambda e, rt=rt: e.tensor_scalar(out=SQyB, in0=src3[:, rt, :],
                                                             scalar1=stat[:, 3:4], scalar2=None,
                                                             op0=OP.mult))
            kb.op("vector", lambda e, rt=rt, dst3=dst3, gs=gs: e.tensor_tensor(
                out=dst3[:, rt, :], in0=SQyB, in1=gs, op=OP.mult), inc="ve")

    def transpose_to(dst3, src3, nkt):
        kb.wait("tensor", "ve")
        for rt in range(4):
            for eh in range(nkt):
                kb.op("tensor", lambda e, rt=rt, eh=eh: e.transpose(
                    out=PMI[0:128, 0:128], in_=src3[0:128, rt, 128 * eh:128 * eh + 128],
                    identity=eyeb), inc="pe")
                kb.wait("vector", "pe")
                kb.op("vector", lambda e, rt=rt, eh=eh: e.tensor_copy(
                    out=dst3[:, eh, 128 * rt:128 * rt + 128], in_=PMI[0:128, 0:128]), inc="ve")
                kb.wait("tensor", "ve")

    if dump == "x1d":
        return _dump([(out[:, :], x1d[:, :])])
    if dump == "xsel":
        return _dump([(out[128 * rt:128 * rt + 128, :], XSEL[:, rt, :]) for rt in range(4)])
    kb.wait("vector", "gq")
    rmsnorm(H1F, XSEL, g1bS)
    if dump == "h1":
        return _dump([(out[128 * rt:128 * rt + 128, :], H1F[:, rt, :]) for rt in range(4)])
    kb.op("gpsimd", lambda e: e.memset(H1T[0:1, 8, :], 1.0), inc="gp")
    kb.wait("tensor", "gp")
    transpose_to(H1T, H1F, 8)
    for dch in range(4):
        kb.wait("gpsimd", "pe")
        for kt in range(8):
            kb.dma("gpsimd", WS1[0:128, kt, :], wqkT[:, kt, 512 * dch:512 * dch + 512], "gq")
        kb.wait("tensor", "gq")
        for db in range(4):
            for kt in range(8):
                kb.op("tensor", lambda e, db=db, kt=kt: e.matmul(
                    out=PAV[0:128, 0:512], lhsT=WS1[0:128, kt, 128 * db:128 * db + 128],
                    rhs=H1T[:, kt, :], start=(kt == 0), stop=(kt == 7)),
                    inc="pe" if kt == 7 else None)
            kb.wait("vector", "pe")
            kb.op("vector", lambda e, dch=dch, db=db: e.tensor_copy(
                out=QKT[:, 4 * dch + db, :], in_=PAV[0:128, 0:512]), inc="ve")
            kb.wait("tensor", "ve")
    for ch in range(4):
        c0 = 260 * ch
        kb.wait("gpsimd", "pe")
        for kt in range(9):
            kb.dma("gpsimd", WS2[0:128, kt, 0:260], wvfT[:, kt, c0:c0 + 260], "gq")
        kb.wait("tensor", "gq")
        for rt in range(4):
            for kt in range(8):
                kb.op("tensor", lambda e, rt=rt, kt=kt: e.matmul(
                    out=PAV[0:128, 0:260], lhsT=H1T[:, kt, 128 * rt:128 * rt + 128],
                    rhs=WS2[0:128, kt, 0:260], start=(kt == 0), stop=False))
            kb.op("tensor", lambda e, rt=rt: e.matmul(
                out=PAV[0:128, 0:260], lhsT=H1T[0:1, 8, 128 * rt:128 * rt + 128],
                rhs=WS2[0:1, 8, 0:260], start=False, stop=True), inc="pe")
            kb.wait("vector", "pe")
            kb.op("vector", lambda e, rt=rt, c0=c0: e.tensor_copy(
                out=VIB[:, rt, c0:c0 + 260], in_=PAV[0:128, 0:260]), inc="ve")
            kb.wait("tensor", "ve")
    for h in range(H):
        qblk, qrow = h // 2, 64 * (h % 2)
        kblk, krow = 8 + h // 2, 64 * (h % 2)
        if h > 0:
            kb.wait("scalar", "peav")
        for st in range(4):
            psl = PSA[0:128, 512 * st:512 * st + 512] if st < 3 else PSB[0:128, 0:512]
            kb.op("tensor", lambda e, st=st, psl=psl, qblk=qblk, qrow=qrow, kblk=kblk, krow=krow: e.matmul(
                out=psl, lhsT=QKT[krow:krow + 64, kblk, 128 * st:128 * st + 128],
                rhs=QKT[qrow:qrow + 64, qblk, 0:512], start=True, stop=True), inc="pesc")
        kb.wait("vector", "pesc")
        for st in range(4):
            psl = PSA[0:128, 512 * st:512 * st + 512] if st < 3 else PSB[0:128, 0:512]
            if st > 0:
                kb.wait("vector", "seexp")
            vv = kb.op("vector", lambda e, st=st, psl=psl: e.scalar_tensor_tensor(
                out=MSKS, in0=psl, scalar=0.125, in1=MKB[:, st, :],
                op0=OP.mult, op1=OP.add), inc="ve")
            kb.wait("scalar", "ve", vv)
            kb.op("scalar", lambda e, st=st: e.activation(out=IAT[:, st, :], in_=MSKS,
                                                          func=AF.Exp), inc="seexp")
        kb.wait("tensor", "seexp")
        for st in range(4):
            kb.op("tensor", lambda e, st=st, h=h: e.matmul(
                out=PAV[0:65, 0:512], lhsT=VIB[:, st, 65 * h:65 * h + 65],
                rhs=IAT[:, st, :], start=(st == 0), stop=(st == 3)),
                inc="peav" if st == 3 else None)
        kb.wait("vector", "peav")
        kb.op("vector", lambda e: e.reciprocal(out=dinv, in_=PAV[64:65, 0:512]), inc="ve")
        kb.wait("tensor", "ve")
        kb.op("tensor", lambda e: e.matmul(out=PMI[0:64, 0:512], lhsT=onesb[0:1, 0:64],
                                           rhs=dinv, start=True, stop=True), inc="pe")
        kb.wait("vector", "pe")
        kb.op("vector", lambda e: e.tensor_copy(out=dbc, in_=PMI[0:64, 0:512]), inc="ve")
        kb.op("vector", lambda e, h=h: e.tensor_tensor(
            out=OTi[h // 2][64 * (h % 2):64 * (h % 2) + 64, :], in0=PAV[0:64, 0:512],
            in1=dbc, op=OP.mult), inc="veot")
        kb.wait("tensor", "veot")
    for jc in range(2):
        kb.wait("gpsimd", "pe")
        for kt in range(8):
            kb.dma("gpsimd", WS1[0:128, kt, :], woutT[:, kt, 512 * jc:512 * jc + 512], "gq")
        kb.wait("tensor", "gq")
        for rt in range(4):
            for j in range(8):
                kb.op("tensor", lambda e, rt=rt, j=j: e.matmul(
                    out=PAV[0:128, 0:512], lhsT=OTi[j][:, 128 * rt:128 * rt + 128],
                    rhs=WS1[0:128, j, :], start=(j == 0), stop=(j == 7)),
                    inc="pe" if j == 7 else None)
            kb.wait("vector", "pe")
            kb.op("vector", lambda e, rt=rt, jc=jc: e.tensor_copy(
                out=XATT[:, rt, 512 * jc:512 * jc + 512], in_=PAV[0:128, 0:512]), inc="ve")
            kb.wait("tensor", "ve")
    if dump == "xattn":
        return _dump([(out[128 * rt:128 * rt + 128, :], XATT[:, rt, :]) for rt in range(4)])
    for rt in range(4):
        kb.op("vector", lambda e, rt=rt: e.tensor_tensor(out=XATT[:, rt, :], in0=XATT[:, rt, :],
                                                         in1=XSEL[:, rt, :], op=OP.add), inc="ve")
    rmsnorm(H1F, XATT, g2bS)
    if dump == "h2":
        return _dump([(out[128 * rt:128 * rt + 128, :], H1F[:, rt, :]) for rt in range(4)])
    transpose_to(H1T, H1F, 8)
    ABQ = H1F
    FCH = [(0, 1024), (1024, 2048), (2048, 2730)]
    for f0, f1 in FCH:
        for sub in range((f1 - f0 + 511) // 512):
            s0f = f0 + 512 * sub
            nn = min(512, f1 - s0f)
            kb.wait("gpsimd", "pe")
            for kt in range(8):
                kb.dma("gpsimd", WS1[0:128, kt, 0:nn], w1T[:, kt, s0f:s0f + nn], "gq")
                kb.dma("gpsimd", WS2[0:128, kt, 0:nn], w2T[:, kt, s0f:s0f + nn], "gq")
            kb.wait("tensor", "gq")
            for rt in range(4):
                for kt in range(8):
                    kb.op("tensor", lambda e, rt=rt, kt=kt, nn=nn: e.matmul(
                        out=PAV[0:128, 0:nn], lhsT=H1T[:, kt, 128 * rt:128 * rt + 128],
                        rhs=WS1[0:128, kt, 0:nn], start=(kt == 0), stop=(kt == 7)))
                for kt in range(8):
                    kb.op("tensor", lambda e, rt=rt, kt=kt, nn=nn: e.matmul(
                        out=PMI[0:128, 0:nn], lhsT=H1T[:, kt, 128 * rt:128 * rt + 128],
                        rhs=WS2[0:128, kt, 0:nn], start=(kt == 0), stop=(kt == 7)),
                        inc="pe" if kt == 7 else None)
                kb.wait("scalar", "pe")
                kb.op("scalar", lambda e, nn=nn: e.activation(out=A1B[:, 0:nn],
                                                              in_=PAV[0:128, 0:nn],
                                                              func=AF.Silu), inc="se")
                kb.wait("vector", "se")
                kb.op("vector", lambda e, rt=rt, nn=nn, s0f=s0f, f0=f0: e.tensor_tensor(
                    out=ABQ[:, rt, s0f - f0:s0f - f0 + nn], in0=A1B[:, 0:nn],
                    in1=PMI[0:128, 0:nn], op=OP.mult), inc="ve")
                kb.wait("tensor", "ve")
        for rt in range(4):
            for eh in range((f1 - f0 + 127) // 128):
                gkt = (f0 + 128 * eh) // 128
                ew = min(128, f1 - f0 - 128 * eh)
                kb.op("tensor", lambda e, rt=rt, eh=eh, ew=ew: e.transpose(
                    out=PMI[0:ew, 0:128], in_=ABQ[0:128, rt, 128 * eh:128 * eh + ew],
                    identity=eyeb), inc="pe")
                kb.wait("vector", "pe")
                kb.op("vector", lambda e, rt=rt, gkt=gkt, ew=ew: e.tensor_copy(
                    out=ABT[0:ew, gkt, 128 * rt:128 * rt + 128], in_=PMI[0:ew, 0:128]), inc="ve")
                kb.wait("tensor", "ve")
    w3_mark = A16.mark()
    W3s = [A16.t2(D) for _ in range(2)]
    xfm = {}
    for kt in range(22):
        kw = min(128, DFF - 128 * kt)
        if kt >= 2:
            kb.wait("gpsimd", "pexf", xfm[kt - 2])
        kb.dma("gpsimd", W3s[kt % 2], w3T[:, kt, :], "gq")
        kb.wait("tensor", "gq")
        for rt in range(4):
            for jc in range(2):
                last = (rt == 3 and jc == 1)
                kb.op("tensor", lambda e, rt=rt, jc=jc, kt=kt, kw=kw: e.matmul(
                    out=GPS[2 * rt + jc][0:128, :], lhsT=ABT[0:kw, kt, 128 * rt:128 * rt + 128],
                    rhs=W3s[kt % 2][0:kw, 512 * jc:512 * jc + 512],
                    start=(kt == 0), stop=(kt == 21)), inc="pexf" if last else None)
        xfm[kt] = kb.cnt["pexf"]
    kb.wait("vector", "pexf", xfm[21])
    for rt in range(4):
        kb.op("vector", lambda e, rt=rt: e.tensor_tensor(
            out=XATT[:, rt, :], in0=XATT[:, rt, :], in1=XSEL[:, rt, :], op=OP.subtract))
        kb.op("vector", lambda e, rt=rt: e.scalar_tensor_tensor(
            out=XATT[:, rt, 0:512], in0=XATT[:, rt, 0:512], scalar=1.0,
            in1=GPS[2 * rt + 0][0:128, :], op0=OP.mult, op1=OP.add))
        kb.op("vector", lambda e, rt=rt: e.scalar_tensor_tensor(
            out=XATT[:, rt, 512:1024], in0=XATT[:, rt, 512:1024], scalar=1.0,
            in1=GPS[2 * rt + 1][0:128, :], op0=OP.mult, op1=OP.add))
        kb.op("vector", lambda e, rt=rt: e.tensor_copy(out=DELB[:, rt, :], in_=XATT[:, rt, :]), inc="ve")
    kb.wait("tensor", "ve")
    if dump == "delb":
        return _dump([(out[128 * rt:128 * rt + 128, :], XATT[:, rt, :]) for rt in range(4)])
    if dump == "qkt":
        return _dump([(out[128 * i:128 * i + 128, 0:512], QKT[:, i, :], "gpsimd")
                      for i in range(16)])

    A16.reset(w3_mark)
    PRB = A16.t3(4, 128)
    Y2B = A32.t2(D)
    for li in range(NT):
        lb = li
        cb = li % 2
        kb.wait("sync", "ve")
        kb.dma("sync", X1fs[cb], x1d[128 * lb:128 * lb + 128, :], "sq")
        kb.wait("vector", "pe")
        for rb in range(4):
            kb.op("vector", lambda e, lb=lb, rb=rb: e.tensor_scalar(
                out=PRB[:, rb, :], in0=RBC[:, 128 * lb:128 * lb + 128],
                scalar1=iorb[:, rb:rb + 1], scalar2=None, op0=OP.is_equal), inc="ve")
        kb.wait("tensor", "ve")
        for jc in range(2):
            for rb in range(4):
                kb.op("tensor", lambda e, rb=rb, jc=jc: e.matmul(
                    out=PAV[0:128, 0:512], lhsT=PRB[:, rb, :],
                    rhs=DELB[0:128, rb, 512 * jc:512 * jc + 512],
                    start=(rb == 0), stop=(rb == 3)), inc="pe" if rb == 3 else None)
            kb.wait("vector", "pe")
            kb.wait("vector", "sq")
            kb.op("vector", lambda e, jc=jc, cb=cb: e.scalar_tensor_tensor(
                out=Y2B[:, 512 * jc:512 * jc + 512], in0=X1fs[cb][:, 512 * jc:512 * jc + 512],
                scalar=2.0, in1=PAV[0:128, 0:512], op0=OP.mult, op1=OP.add), inc="ve")
            kb.wait("tensor", "ve")
        kb.op("vector", lambda e: e.tensor_reduce(out=stat[:, 0:1], in_=Y2B, axis=AX, op=OP.add))
        kb.op("vector", lambda e: e.tensor_tensor(out=SQyB, in0=Y2B, in1=Y2B, op=OP.mult))
        kb.op("vector", lambda e: e.tensor_reduce(out=stat[:, 1:2], in_=SQyB, axis=AX, op=OP.add))
        kb.op("vector", lambda e: e.tensor_scalar(out=stat[:, 2:3], in0=stat[:, 0:1],
                                                  scalar1=1.0 / D, scalar2=None, op0=OP.mult))
        kb.op("vector", lambda e: e.tensor_scalar(out=stat[:, 4:5], in0=stat[:, 1:2],
                                                  scalar1=1.0 / D, scalar2=None,
                                                  op0=OP.mult), inc="vch")
        kb.wait("vector", "vch")
        kb.op("vector", lambda e: e.tensor_tensor(out=stat[:, 3:4], in0=stat[:, 2:3],
                                                  in1=stat[:, 2:3], op=OP.mult), inc="vch")
        kb.wait("vector", "vch")
        v1 = kb.op("vector", lambda e: e.tensor_tensor(out=stat[:, 5:6], in0=stat[:, 4:5],
                                                       in1=stat[:, 3:4], op=OP.subtract), inc="ve")
        kb.wait("scalar", "ve", v1)
        kb.op("scalar", lambda e: e.activation(out=stat[:, 6:7], in_=stat[:, 5:6],
                                               func=AF.Ln, bias=epsln), inc="sch")
        kb.wait("scalar", "sch")
        s1_ = kb.op("scalar", lambda e: e.activation(out=stat[:, 7:8], in_=stat[:, 6:7],
                                                     func=AF.Exp, scale=-0.5), inc="se")
        kb.wait("vector", "se", s1_)
        kb.op("vector", lambda e: e.tensor_scalar(out=stat[:, 8:9], in0=stat[:, 2:3],
                                                  scalar1=-1.0, scalar2=None,
                                                  op0=OP.mult), inc="vch")
        kb.wait("vector", "vch")
        kb.op("vector", lambda e: e.tensor_scalar(out=Y2B, in0=Y2B, scalar1=stat[:, 8:9],
                                                  scalar2=stat[:, 7:8], op0=OP.add, op1=OP.mult), inc="ve")
        kb.wait("sync", "ve")
        kb.dma("sync", out[128 * li:128 * li + 128, :], Y2B, "stq")
        kb.wait("vector", "stq")

    _finalize(kb, nc)
    return nc


def _finalize(kb, nc):
    kb.wait("sync", "stq") if "stq" in kb.cnt else None
    kb.wait("sync", "sq")
    with nc.Block() as block:
        @block.sync
        def _(e):
            for f in kb.ops["sync"]:
                f(e)

        @block.gpsimd
        def _(e):
            for f in kb.ops["gpsimd"]:
                f(e)

        @block.tensor
        def _(e):
            for f in kb.ops["tensor"]:
                f(e)

        @block.vector
        def _(e):
            for f in kb.ops["vector"]:
                f(e)

        @block.scalar
        def _(e):
            for f in kb.ops["scalar"]:
                f(e)


def _wtiles(aT, ktiles):
    K, N = aT.shape
    assert K <= 128 * ktiles
    buf = np.zeros((128 * ktiles, N), dtype=np.float32)
    buf[:K] = aT
    return np.ascontiguousarray(buf.reshape(ktiles, 128, N).transpose(1, 0, 2))


def _prep_inputs(inp, b):
    x = np.ascontiguousarray(inp["L0_attn_query"][b]).astype(np.float32, copy=False)
    d = {}
    d["xrow"] = x
    d["xtT"] = _wtiles(np.ascontiguousarray(x.T), 8)
    d["wqT"] = _wtiles(np.ascontiguousarray(inp["L0_attn_Wq"].T), 8)
    d["wkT"] = _wtiles(np.ascontiguousarray(inp["L0_attn_Wk"].T), 8)
    wv = np.zeros((1152, H * 65), dtype=np.float32)
    wvf = np.zeros((1152, H * 65), dtype=np.float32)
    WvT = inp["L0_attn_Wv"].T
    Wqkv = inp["L0_ffn_W_qkv"]
    for h in range(H):
        wv[0:1024, 65 * h:65 * h + 64] = WvT[:, 64 * h:64 * h + 64]
        wv[1024, 65 * h:65 * h + 64] = inp["L0_attn_bv"][64 * h:64 * h + 64]
        wv[1024, 65 * h + 64] = 1.0
        wvf[0:1024, 65 * h:65 * h + 64] = Wqkv[2 * D + 64 * h:2 * D + 64 * h + 64, :].T
        wvf[1024, 65 * h + 64] = 1.0
    d["wvT"] = np.ascontiguousarray(wv.reshape(9, 128, H * 65).transpose(1, 0, 2))
    d["wvfT"] = np.ascontiguousarray(wvf.reshape(9, 128, H * 65).transpose(1, 0, 2))
    d["woT"] = _wtiles(np.ascontiguousarray(inp["L0_attn_Wo"].T), 8)
    d["wbT"] = _wtiles(np.ascontiguousarray(inp["L0_attn_Wb"].T), 8)
    d["wqkT"] = _wtiles(np.ascontiguousarray(Wqkv[0:2 * D, :].T), 8)
    d["woutT"] = _wtiles(np.ascontiguousarray(inp["L0_ffn_W_out"].T), 8)
    d["w1T"] = _wtiles(np.ascontiguousarray(inp["L0_ffn_W1"].T), 8)
    d["w2T"] = _wtiles(np.ascontiguousarray(inp["L0_ffn_W2"].T), 8)
    d["w3T"] = _wtiles(np.ascontiguousarray(inp["L0_ffn_W3"].T), 22)
    d["bqt"] = np.ascontiguousarray(inp["L0_attn_bq"].reshape(8, 128).T)
    d["bkt"] = np.ascontiguousarray(inp["L0_attn_bk"].reshape(8, 128).T)
    d["brows"] = np.concatenate([inp["L0_attn_bo"], inp["L0_attn_bb"]]).reshape(1, 2 * D)
    d["wrb"] = np.ascontiguousarray(np.broadcast_to(inp["L0_ffn_w_router"], (128, D)))
    d["g1bi"] = np.ascontiguousarray(np.broadcast_to(inp["L0_ffn_g1"], (128, D)))
    d["g2bi"] = np.ascontiguousarray(np.broadcast_to(inp["L0_ffn_g2"], (128, D)))
    d["rwrep"] = np.ascontiguousarray(
        inp["L0_attn_radial_weights"].reshape(H)[np.arange(128) % H]).reshape(128, 1)
    d["curv"] = np.asarray(inp["L0_attn_curvature"], dtype=np.float32).reshape(1, 1)
    p = np.arange(128)
    s = np.arange(128)
    d["lt128"] = (s[None, :] < p[:, None]).astype(np.float32)
    d["iota512b"] = np.ascontiguousarray(
        np.broadcast_to(np.arange(512, dtype=np.float32), (128, 512)))
    d["iotarb"] = (p[:, None] + 128 * np.arange(4)[None, :]).astype(np.float32)
    mb = np.zeros((128, 4, 512), dtype=np.float32)
    for st in range(4):
        sglob = 128 * st + p
        mb[:, st, :] = np.where(sglob[:, None] <= np.arange(512)[None, :], 0.0, -1e30)
    d["maskbd"] = mb
    d["onesrow"] = np.ones((1, 128), dtype=np.float32)
    d["onescol"] = np.ones((128, 1), dtype=np.float32)
    d["eyef"] = np.eye(128, dtype=np.float32)
    for k in d:
        d[k] = np.ascontiguousarray(d[k], dtype=np.float32)
    return d


def kernel(**inputs):
    nc = build(2)
    in_maps = [_prep_inputs(inputs, c % 4) for c in range(8)]
    res = run_bass_kernel_spmd(nc, in_maps, core_ids=list(range(8)))
    outp = np.empty((B, T, D), dtype=np.float32)
    for b_ in range(4):
        outp[b_] = res.results[b_]["out"]
    return outp
